# version 06
# baseline (speedup 1.0000x reference)
import os
import numpy as np
import concourse.bass as bass
import concourse.mybir as mybir
from concourse.bass_utils import run_bass_kernel_spmd

F32 = mybir.dt.float32
F16 = mybir.dt.float16
BF16 = mybir.dt.bfloat16
I32 = mybir.dt.int32
AF = mybir.ActivationFunctionType
ALU = mybir.AluOpType

L = 8192
DM = 1024
DIN = 2048
DH = 1024
NC8 = 8
N16 = 16
HH = 1536
HK = 12
MACRO = 1024
NMAC = L // MACRO
SUB = 128
NSUB = MACRO // SUB
EPS = 1e-5
RG = [[0, 1], [2, 3], [4, 5], [6, 7]]


class Sched:
    ENG = ("sync", "scalar", "vector", "tensor", "gpsimd")

    def __init__(self, nc):
        self.nc = nc
        self.ops = {e: [] for e in self.ENG}
        self.cnt = {}
        for e in self.ENG:
            self.cnt["e_" + e] = 0
            self.cnt["d_" + e] = 0

    SERIAL = False
    MARK = False

    def op(self, eng, emit, after=(), dma=False, coll=False, stream=None):
        if dma or coll:
            sem = "d_" + (stream or eng)
        else:
            sem = "e_" + eng
        inc = 16 if dma else 1
        if sem not in self.cnt:
            self.cnt[sem] = 0
        self.cnt[sem] += inc
        after = tuple(a for a in after if a is not None)
        if self.SERIAL or self.MARK:
            after = after + ((self.last_ev,) if getattr(self, "last_ev", None) else ())
        self.ops[eng].append((emit, after, sem, inc))
        ev = (sem, self.cnt[sem])
        self.last_ev = ev
        return ev

    def wait_only(self, eng, after):
        self.ops[eng].append((None, tuple(a for a in after if a is not None), None, 0))

    def emit(self, block, sems):
        for eng_name in self.ENG:
            ops = self.ops[eng_name]
            if not ops:
                continue

            def mk(eng_name, ops):
                own_e = "e_" + eng_name
                tight = eng_name in ("vector", "scalar", "gpsimd")

                def body(eng):
                    hi = {}
                    own_cnt = 0
                    for emit, after, sem, inc in ops:
                        for wsem, wval in after:
                            if wsem == own_e:
                                if not tight or own_cnt - wval > 4:
                                    continue
                            if hi.get(wsem, 0) >= wval:
                                continue
                            eng.wait_ge(sems[wsem], wval)
                            hi[wsem] = wval
                        if emit is not None:
                            ins = emit(eng)
                            if ins is not None and sem is not None:
                                ins.then_inc(sems[sem], inc)
                        if sem == own_e:
                            own_cnt += inc
                return body

            getattr(block, eng_name)(mk(eng_name, ops))


def host_prep(inputs, core):
    pair, role = core // 2, core % 2
    d0 = role * DH
    h0 = role * HH
    ip = {}
    ip["x_seq"] = np.ascontiguousarray(np.asarray(inputs["L0_attn_x"])[pair], dtype=np.float32)
    W = np.asarray(inputs["L0_attn_in_proj_w"])
    ip["w_in"] = np.ascontiguousarray(
        np.concatenate([W[:, d0:d0 + DH], W[:, DIN + d0:DIN + d0 + DH]], axis=1))
    cw = np.asarray(inputs["L0_attn_conv_w"])[d0:d0 + DH]
    idx = np.arange(128)
    diag = np.zeros((4, NC8, 128, 128), np.float32)
    for j in range(4):
        for c in range(NC8):
            diag[j, c, idx, idx] = cw[c * 128:(c + 1) * 128, j]
    ip["conv_diag"] = diag
    ip["conv_b"] = np.ascontiguousarray(
        np.asarray(inputs["L0_attn_conv_b"])[d0:d0 + DH].reshape(NC8, 128).T)
    ip["xprojw"] = np.ascontiguousarray(np.asarray(inputs["L0_attn_x_proj_w"])[d0:d0 + DH])
    ip["dtw"] = np.ascontiguousarray(np.asarray(inputs["L0_attn_dt_proj_w"])[:, d0:d0 + DH])
    ip["dtb"] = np.ascontiguousarray(
        np.asarray(inputs["L0_attn_dt_proj_b"])[d0:d0 + DH].reshape(NC8, 128).T)
    ip["alog"] = np.ascontiguousarray(
        np.asarray(inputs["L0_attn_A_log"])[d0:d0 + DH].reshape(NC8, 128, N16)
        .transpose(1, 0, 2).reshape(128, NC8 * N16))
    Dv = np.asarray(inputs["L0_attn_D"])[d0:d0 + DH]
    ddiag = np.zeros((NC8, 128, 128), np.float32)
    for c in range(NC8):
        ddiag[c, idx, idx] = Dv[c * 128:(c + 1) * 128]
    ip["d_diag"] = ddiag
    ip["d_col"] = np.ascontiguousarray(Dv.reshape(NC8, 128).T)
    ip["outw"] = np.ascontiguousarray(np.asarray(inputs["L0_attn_out_proj_w"])[d0:d0 + DH])
    ip["ln_g"] = np.ascontiguousarray(np.asarray(inputs["L0_attn_ln_g"]).reshape(NC8, 128).T)
    ip["ln_b"] = np.ascontiguousarray(np.asarray(inputs["L0_attn_ln_b"]).reshape(NC8, 128).T)
    ip["wgT"] = np.ascontiguousarray(np.asarray(inputs["L0_ffn_w_gate"])[h0:h0 + HH].T)
    ip["wuT"] = np.ascontiguousarray(np.asarray(inputs["L0_ffn_w_up"])[h0:h0 + HH].T)
    ip["wdT"] = np.ascontiguousarray(np.asarray(inputs["L0_ffn_w_down"])[:, h0:h0 + HH].T)
    ip["eye"] = np.eye(128, dtype=np.float32)
    return ip


def build_nc(debug=False, serial=False, serial_phases=()):
    nc = bass.Bass("TRN2", target_bir_lowering=False)
    A = lambda name, shape, dt: nc.dram_tensor(name, shape, dt, kind="ExternalInput").ap()
    x_seq = A("x_seq", [L, DM], F32)
    w_in = A("w_in", [DM, 2 * DH], F32)
    conv_diag = A("conv_diag", [4, NC8, 128, 128], F32)
    conv_b = A("conv_b", [128, NC8], F32)
    xprojw = A("xprojw", [DH, 96], F32)
    dtw_i = A("dtw", [64, DH], F32)
    dtb_i = A("dtb", [128, NC8], F32)
    alog = A("alog", [128, NC8 * N16], F32)
    d_diag = A("d_diag", [NC8, 128, 128], F32)
    d_col = A("d_col", [128, NC8], F32)
    outw = A("outw", [DH, DM], F32)
    ln_g = A("ln_g", [128, NC8], F32)
    ln_b = A("ln_b", [128, NC8], F32)
    wgT = A("wgT", [DM, HH], I32)
    wuT = A("wuT", [DM, HH], I32)
    wdT = A("wdT", [HH, DM], I32)
    eye_i = A("eye", [128, 128], F32)
    y_out = nc.dram_tensor("y_out", [L, DM], F32, kind="ExternalOutput").ap()

    Dr = lambda name, shape, dt: nc.dram_tensor(name, shape, dt).ap()
    wg_bf = Dr("wg_bf", [DM, 2 * DH], BF16)
    outw_bf = Dr("outw_bf", [DH, DM], BF16)
    ffng_bf = Dr("ffng_bf", [DM, HH], BF16)
    ffnu_bf = Dr("ffnu_bf", [DM, HH], BF16)
    ffnd_bf = Dr("ffnd_bf", [HH, DM], BF16)
    y2_sp = Dr("y2_sp", [2, NC8, 128, MACRO], BF16)
    sz_sp = Dr("sz_sp", [NC8, 128, MACRO], BF16)
    x2_sp = Dr("x2_sp", [MACRO, DM], F16)
    bwrow = Dr("bwrow", [2048], F32)
    cc_dbc_in = Dr("cc_dbc_in", [96, MACRO], F32)
    cc_dbc_out = Dr("cc_dbc_out", [96, MACRO], F32)
    bcrows = Dr("bcrows", [32, MACRO], F16)
    cc_o_in = Dr("cc_o_in", [MACRO, DM], F32)
    cc_o_out = Dr("cc_o_out", [MACRO, DM], F32)
    cc_f_in = Dr("cc_f_in", [MACRO, DM], F32)
    cc_f_out = Dr("cc_f_out", [MACRO, DM], F32)

    dbg = {}
    if debug:
        for nm, shape in [("dbg_xn", [128, DM]), ("dbg_u2", [128, MACRO]),
                          ("dbg_delta", [128, MACRO]), ("dbg_dbc", [96, MACRO]),
                          ("dbg_y2", [128, MACRO]), ("dbg_o", [MACRO, DM]),
                          ("dbg_x2", [128, DM]), ("dbg_upre", [128, MACRO]),
                          ("dbg_xnt", [128, MACRO]), ("dbg_bwt", [128, 16]), ("dbg_stats", [128, 40]), ("dbg_xln1", [128, DM]), ("dbg_nm", [128, 16]), ("dbg_can", [128, 40]), ("dbg_can2", [128, 40])]:
            dbg[nm] = nc.dram_tensor(nm, shape, F32, kind="ExternalOutput").ap()

    S = lambda name, shape, dt: nc.alloc_sbuf_tensor(name, shape, dt).ap()
    negm_all = S("negm_all", [128, NMAC * NSUB], F32)
    r1_all = S("r1_all", [128, NMAC * NSUB], F32)
    s_all = S("s_all", [128, NMAC * NSUB], F32)
    convd_sb = S("convd_sb", [128, 4 * NC8 * 128], BF16)
    xprojw_sb = S("xprojw_sb", [128, NC8 * 96], BF16)
    dtw_sb = S("dtw_sb", [64, DH], BF16)
    a_sb = S("a_sb", [128, NC8 * N16], F32)
    eye_sb = S("eye_sb", [128, 128], BF16)
    bwt_sb = S("bwt_sb", [128, 16], F32)
    convb_sb = S("convb_sb", [128, NC8], F32)
    dtb_sb = S("dtb_sb", [128, NC8], F32)
    gcol_sb = S("gcol_sb", [128, NC8], F32)
    bcol_sb = S("bcol_sb", [128, NC8], BF16)
    wgoc_sb = S("wgoc_sb", [128, 2 * NC8 * 128], BF16)
    ffg_sb = S("ffg_sb", [128, 2 * NC8 * 128], BF16)
    ffu_sb = S("ffu_sb", [128, 2 * NC8 * 128], BF16)
    xnt_sb = S("xnt_sb", [128, NC8 * MACRO], BF16)
    delta_sb = S("delta_sb", [128, NC8 * MACRO], F16)
    u2_sb = S("u2_sb", [128, NC8 * MACRO], BF16)
    upre_sb = S("upre_sb", [128, 2 * (MACRO + 4)], BF16)
    uhalo_sb = S("uhalo_sb", [128, 3 * NC8], BF16)
    wq_sb = S("wq_sb", [128, NC8 * MACRO], F16)
    ysb_sb = S("ysb_sb", [128, NC8 * MACRO], BF16)
    dA_sb = S("dA_sb", [128, 2 * MACRO], F32)
    dBu_sb = S("dBu_sb", [128, MACRO], F16)
    h8_sb = S("h8_sb", [128, 4 * MACRO], BF16)
    g_sb = S("g_sb", [128, 2 * MACRO], BF16)
    bbc_sb = S("bbc_sb", [128, 4 * MACRO], F16)
    cbc_sb = S("cbc_sb", [128, 4 * MACRO], F16)
    hst_sb = S("hst_sb", [128, NC8 * N16], F32)
    dt_sb = S("dt_sb", [64, MACRO], BF16)
    x2_sb = S("x2_sb", [128, 2 * DM], F16)
    stage_bf = S("stage_bf", [128, 2 * DM], BF16)
    stage_f32 = S("stage_f32", [128, 2 * DM], F32)
    fs = [S(f"fscr{i}", [128, DM], F32) for i in range(4)]
    y2r_sb = S("y2r_sb", [128, 4 * NC8 * 128], BF16)
    bw_sb = S("bw_sb", [128, 2 * DM], BF16)
    hact_sb = u2_sb[:, 0:HK * 512]
    x2r_sb = x2_sb
    szb_sb = S("szb_sb", [128, MACRO], BF16)
    y2b_sb = S("y2b_sb", [128, MACRO], BF16)
    spx_sb = S("spx_sb", [128, 2 * MACRO], F32)
    cst_sb = S("cst_sb", [128, 2], F32)
    st_acc = S("st_acc", [128, 4 * NSUB], F32)
    st_tmp = S("st_tmp", [128, 12 * NSUB], F32)

    P = lambda name, shape, dt: nc.alloc_psum_tensor(name, shape, dt).ap()
    psA = P("psA", [128, 1024], F32)
    psB = P("psB", [128, 1024], F32)
    psC = P("psC", [128, 1024], F32)
    psT = P("psT", [128, 1024], BF16)

    sch = Sched(nc)
    sch.SERIAL = serial
    O = sch.op

    def mark(ph):
        sch.MARK = ph in serial_phases or ph == "prep"
    war = {}

    def guard(name):
        return war.get(name)

    def setg(name, ev):
        war[name] = ev

    mark("prep")
    pf = u2_sb.bitcast(F32)[:, 0:3072]
    pb = u2_sb[:, 6144:8192]

    e1 = O("sync", lambda e: e.dma_start(out=fs[0][:, 0:128], in_=eye_i), dma=True)
    ev_eye = O("vector", lambda e: e.tensor_copy(out=eye_sb, in_=fs[0][:, 0:128]), after=[e1])
    dcol_sb = nc.alloc_sbuf_tensor("dcol_sb", [128, NC8], F32).ap()
    for src, dst in ((conv_b, convb_sb), (dtb_i, dtb_sb), (ln_g, gcol_sb), (d_col, dcol_sb)):
        O("sync", lambda e, s=src, d=dst: e.dma_start(out=d, in_=s), dma=True)
    ev_cols = O("sync", lambda e: e.dma_start(out=fs[1][:, 0:NC8], in_=ln_b), dma=True)
    ev_bcol = O("vector", lambda e: e.tensor_copy(out=bcol_sb, in_=fs[1][:, 0:NC8]), after=[ev_cols])
    e1 = O("sync", lambda e: e.dma_start(out=a_sb, in_=alog), dma=True)
    e2 = O("scalar", lambda e: e.activation(out=a_sb, in_=a_sb, func=AF.Exp), after=[e1])
    ev_a = O("vector", lambda e: e.tensor_single_scalar(out=a_sb, in_=a_sb, scalar=-1.0, op=ALU.mult), after=[e2])
    ev_ffc = []
    for src, dst in ((wgT, ffng_bf), (wuT, ffnu_bf), (wdT, ffnd_bf)):
        ev_ffc.append(O("gpsimd", lambda e, s=src, d=dst: e.dma_start(out=d, in_=s), dma=True))
    ev_convd = None
    for jh in range(2):
        e1 = O("sync", lambda e, jh=jh: e.dma_start(
            out=pf[:, 0:2 * NC8 * 128].rearrange("p (j c n) -> p j c n", j=2, c=NC8),
            in_=conv_diag[2 * jh:2 * jh + 2].rearrange("j c p n -> p j c n")),
            after=[ev_convd], dma=True)
        ev_convd = O("vector", lambda e, jh=jh: e.tensor_copy(
            out=convd_sb[:, jh * 2 * NC8 * 128:(jh + 1) * 2 * NC8 * 128],
            in_=pf[:, 0:2 * NC8 * 128]), after=[e1])
    e1 = O("sync", lambda e: e.dma_start(
        out=pf[:, 0:NC8 * 96].rearrange("p (c n) -> p c n", c=NC8),
        in_=xprojw.rearrange("(c p) n -> p c n", p=128)), after=[ev_convd], dma=True)
    ev_xpw = O("vector", lambda e: e.tensor_copy(out=xprojw_sb, in_=pf[:, 0:NC8 * 96]), after=[e1])
    e1 = O("sync", lambda e: e.dma_start(out=pf[0:64, 0:DH], in_=dtw_i), after=[ev_xpw], dma=True)
    ev_dtw = O("vector", lambda e: e.tensor_copy(out=dtw_sb, in_=pf[0:64, 0:DH]), after=[e1])

    ev_bw_mm = None
    ev_wg_st = {}
    ev_eg_prev = None
    for k in range(NC8):
        el = O("sync", lambda e, k=k: e.dma_start(out=pf[:, 0:2 * DH], in_=w_in[k * 128:(k + 1) * 128, :]),
               after=[ev_dtw, ev_eg_prev, ev_bw_mm], dma=True)
        er = O("vector", lambda e: e.tensor_copy(out=pb, in_=pf[:, 0:2 * DH]), after=[el])
        eg = O("vector", lambda e, k=k: e.tensor_scalar(
            out=stage_bf[:, 0:2 * DH], in0=pf[:, 0:2 * DH],
            scalar1=gcol_sb[:, k:k + 1], scalar2=None, op0=ALU.mult),
            after=[el, ev_wg_st.get(k - 1)])
        ev_eg_prev = eg
        ev_wg_st[k] = O("sync", lambda e, k=k: e.dma_start(
            out=wg_bf[k * 128:(k + 1) * 128, :], in_=stage_bf[:, 0:2 * DH]), after=[eg], dma=True)
        for fc in range(4):
            pdst = psA if fc < 2 else psC
            ev_bw_mm = O("tensor", lambda e, k=k, fc=fc, pdst=pdst: e.matmul(
                pdst[0:1, (fc % 2) * 512:(fc % 2 + 1) * 512], bcol_sb[:, k:k + 1], pb[:, fc * 512:(fc + 1) * 512],
                start=(k == 0), stop=(k == NC8 - 1)), after=[er, ev_bcol])
    e_bev = O("scalar", lambda e: e.copy(out=stage_f32[0:1, 0:1024], in_=psA[0:1, 0:1024]), after=[ev_bw_mm])
    e_bev = O("scalar", lambda e: e.copy(out=stage_f32[0:1, 1024:2048], in_=psC[0:1, 0:1024]), after=[ev_bw_mm, e_bev])
    setg("psA", e_bev)
    setg("psC", e_bev)
    e_bst = O("sync", lambda e: e.dma_start(out=bwrow.rearrange("(a b) -> a b", a=2)[0:1], in_=stage_f32[0:1, 0:1024]), after=[e_bev], dma=True)
    e_bst = O("sync", lambda e: e.dma_start(out=bwrow.rearrange("(a b) -> a b", a=2)[1:2], in_=stage_f32[0:1, 1024:2048]), after=[e_bev, e_bst], dma=True)
    ev_bwt = e_bst
    for oc in range(16):
        ev_bwt = O("sync", lambda e, oc=oc: e.dma_start(
            out=bwt_sb[:, oc:oc + 1], in_=bwrow.rearrange("(oc j) -> oc j", oc=16)[oc:oc + 1, :].rearrange("a j -> (a j) a" if False else "a j -> j a")),
            after=[e_bst], dma=True)

    prev_st = ev_wg_st[NC8 - 1]
    for k in range(NC8):
        el = O("sync", lambda e, k=k: e.dma_start(out=pf[:, 0:DM], in_=outw[k * 128:(k + 1) * 128, :]),
               after=[prev_st, ev_bw_mm], dma=True)
        ec = O("vector", lambda e: e.tensor_copy(out=stage_bf[:, 0:DM], in_=pf[:, 0:DM]), after=[el])
        prev_st = O("sync", lambda e, k=k: e.dma_start(
            out=outw_bf[k * 128:(k + 1) * 128, :], in_=stage_bf[:, 0:DM]), after=[ec], dma=True)
    ev_ce = O("vector", lambda e: e.memset(cst_sb[:, 0:1], EPS))
    ev_c1 = O("vector", lambda e: e.memset(cst_sb[:, 1:2], 1.0))
    setg("fs0", ev_eye)
    setg("fs1", ev_bcol)
    setg("stagebf0", prev_st)
    setg("stagebf1", prev_st)
    setg("stagef0", ev_bwt)
    setg("stagef1", ev_bwt)
    ev_z0 = O("vector", lambda e: e.memset(hst_sb, 0.0))
    ev_z1 = O("vector", lambda e: e.memset(uhalo_sb, 0.0))
    prep_done = prev_st
    setg("u2", prev_st)

    ev_sz = {}
    ev_y2wr = {}
    ev_hsave = {}
    ud_events = {mq: {} for mq in range(NMAC)}
    rd2_events = {}
    y2free = {}
    last_store = None
    ev_xnt_free = None
    ev_stacc_free = None
    ev_stacc2_free = None

    def macro_body(m):
        nonlocal last_store, ev_xnt_free, ev_stacc_free, ev_stacc2_free
        T0 = m * MACRO
        gbase = m * NSUB

        mark("p1")
        ev_acc = []
        for i in range(NSUB):
            xs = i % 2
            e_l = O("sync", lambda e, i=i, xs=xs, T0=T0: e.dma_start(
                out=fs[xs], in_=x_seq[T0 + i * SUB:T0 + (i + 1) * SUB, :]),
                after=[guard(f"fs{xs}"), prep_done if m == 0 else None], dma=True, stream="xload")
            e0 = O("scalar", lambda e, i=i, xs=xs: e.activation(
                out=fs[2], in_=fs[xs], func=AF.Copy, accum_out=st_acc[:, 4 * i:4 * i + 1]),
                after=[e_l, ev_stacc_free, guard("fs2")] + ([guard("st_acc")] if i == 0 else []))
            e1 = O("scalar", lambda e, i=i, xs=xs: e.activation(
                out=fs[2], in_=fs[xs], func=AF.Square, accum_out=st_acc[:, 4 * i + 1:4 * i + 2]),
                after=[e0])
            ev_acc.append(e1)
            setg(f"fs{xs}", e1)
            setg("fs2", e1)
        sa = st_acc.rearrange("p (i k) -> p k i", k=4)
        nm_c = negm_all[:, gbase:gbase + NSUB]
        r1_c = r1_all[:, gbase:gbase + NSUB]
        s_c = s_all[:, gbase:gbase + NSUB]
        t0 = st_tmp[:, 0:NSUB]
        t1 = st_tmp[:, NSUB:2 * NSUB]
        t2 = st_tmp[:, 2 * NSUB:3 * NSUB]
        e1 = O("vector", lambda e: e.tensor_single_scalar(out=nm_c, in_=sa[:, 0], scalar=-1.0 / DM, op=ALU.mult), after=[ev_acc[-1]])
        tmq = st_tmp[:, 7 * NSUB:8 * NSUB]
        e2 = O("vector", lambda e: e.tensor_mul(out=tmq, in0=nm_c, in1=nm_c), after=[e1, guard("st7")])
        setg("st7", e3 if False else None)
        e3 = O("vector", lambda e: e.scalar_tensor_tensor(out=t0, in0=sa[:, 1], scalar=1.0 / DM, in1=tmq, op0=ALU.mult, op1=ALU.subtract), after=[e2])
        e4 = O("scalar", lambda e: e.activation(out=t1, in_=t0, func=AF.Ln, bias=cst_sb[:, 0:1]), after=[e3])
        e5 = O("scalar", lambda e: e.activation(out=t1, in_=t1, func=AF.Exp, scale=-0.5), after=[e4])
        e6 = O("vector", lambda e: e.tensor_mul(out=t2, in0=t1, in1=t1), after=[e5])
        e7 = O("vector", lambda e: e.tensor_mul(out=t2, in0=t2, in1=t0), after=[e6])
        e8 = O("scalar", lambda e: e.activation(out=t2, in_=t2, func=AF.Ln, bias=cst_sb[:, 0:1]), after=[e7])
        e9 = O("scalar", lambda e: e.activation(out=t2, in_=t2, func=AF.Exp, scale=-0.5), after=[e8])
        e10 = O("vector", lambda e: e.tensor_copy(out=r1_c, in_=t1), after=[e5])
        ev_stats = O("vector", lambda e: e.tensor_mul(out=s_c, in0=t1, in1=t2), after=[e9, e10])
        setg("st_acc", ev_stats)
        ev_stacc_free = ev_stats
        if debug and m == 0:
            O("gpsimd", lambda e: e.dma_start(out=dbg["dbg_can2"][:, 0:NSUB], in_=negm_all[:, 0:NSUB]),
              after=[ev_stats], dma=True)

        xnt3 = xnt_sb.rearrange("p (k t) -> p k t", k=NC8)
        psT3 = psT.rearrange("p (k t) -> p k t", k=NC8)
        ev_tr = []
        for i in range(NSUB):
            xs = i % 2
            sl = i % 2
            e_xl2 = O("sync", lambda e, i=i, xs=xs, T0=T0: e.dma_start(
                out=fs[xs], in_=x_seq[T0 + i * SUB:T0 + (i + 1) * SUB, :]),
                after=[guard(f"fs{xs}")], dma=True, stream="xload")
            e_xn = O("vector", lambda e, i=i, xs=xs, sl=sl: e.tensor_scalar(
                out=stage_bf[:, sl * DM:(sl + 1) * DM], in0=fs[xs],
                scalar1=nm_c[:, i:i + 1], scalar2=s_c[:, i:i + 1], op0=ALU.add, op1=ALU.mult),
                after=[e_xl2, ev_stats, guard(f"stagebf{sl}")])
            setg(f"fs{xs}", e_xn)
            tp = None
            for k in range(NC8):
                tp = O("tensor", lambda e, k=k, sl=sl: e.transpose(
                    psT3[:, k], stage_bf[:, sl * DM + k * 128:sl * DM + (k + 1) * 128], eye_sb),
                    after=[e_xn, ev_eye, guard("psT")])
            e_ev = O("scalar", lambda e, i=i: e.copy(out=xnt3[:, :, i * SUB:(i + 1) * SUB], in_=psT3),
                     after=[tp] + ([ev_xnt_free, rd2_events.get(m - 1)] if i == 0 else []))
            setg("psT", e_ev)
            setg(f"stagebf{sl}", e_ev)
            ev_tr.append(e_ev)
        ev_xnt = ev_tr[-1]
        if debug and m == 0:
            O("gpsimd", lambda e: e.dma_start(out=dbg["dbg_can2"][:, NSUB:2 * NSUB], in_=negm_all[:, 0:NSUB]),
              after=[ev_xnt], dma=True)
            e_st8 = O("vector", lambda e: e.tensor_copy(out=fs[2][:, 0:3 * NSUB], in_=st_tmp[:, 0:3 * NSUB]), after=[ev_stats, guard("fs2")])
            e_st9 = O("vector", lambda e: e.tensor_copy(out=fs[2][:, 3 * NSUB:3 * NSUB + NSUB], in_=nm_c), after=[ev_stats, e_st8])
            e_st10 = O("vector", lambda e: e.tensor_copy(out=fs[2][:, 4 * NSUB:4 * NSUB + NSUB], in_=s_c), after=[ev_stats, e_st9])
            e_stD = O("gpsimd", lambda e: e.dma_start(out=dbg["dbg_stats"], in_=fs[2][:, 0:5 * NSUB]), after=[e_st10], dma=True)
            setg("fs2", e_stD)
            e_xd = O("vector", lambda e: e.tensor_copy(out=fs[3], in_=stage_bf[:, 0:DM]), after=[ev_tr[0], guard("fs3")])
            e_xdD = O("gpsimd", lambda e: e.dma_start(out=dbg["dbg_xn"], in_=fs[3]), after=[e_xd], dma=True)
            setg("fs3", e_xdD)

        mark("p2")
        wg3 = wgoc_sb.rearrange("p (s k n) -> p s k n", s=2, k=NC8)
        ev_u = {}
        ev_z = {}
        ev_u2 = {}
        for oc in range(16):
            sl = oc % 2
            ps = psA if sl == 0 else psC
            pname = "psA" if sl == 0 else "psC"
            e_w = O("sync", lambda e, oc=oc, sl=sl: e.dma_start(
                out=wg3[:, sl], in_=wg_bf[:, oc * 128:(oc + 1) * 128].rearrange("(k p) n -> p k n", p=128)),
                after=[guard(f"wgoc{sl}"), prep_done if m == 0 else None], dma=True, stream="wg")
            mm = None
            for k in range(NC8):
                for hf in range(2):
                    mm = O("tensor", lambda e, k=k, hf=hf, ps=ps, sl=sl: e.matmul(
                        ps[:, hf * 512:(hf + 1) * 512], wg3[:, sl, k],
                        xnt3[:, k, hf * 512:(hf + 1) * 512],
                        start=(k == 0), stop=(k == NC8 - 1)),
                        after=[e_w, ev_xnt, guard(pname)])
            setg(f"wgoc{sl}", mm)
            if oc < NC8:
                c = oc
                us = c % 2
                upv = upre_sb.rearrange("p (s t) -> p s t", s=2)[:, us]
                e_h = O("vector", lambda e, c=c, upv=upv: e.tensor_copy(
                    out=upv[:, 0:3], in_=uhalo_sb[:, c * 3:c * 3 + 3]),
                    after=[guard(f"upre{us}"), guard(f"uhalo{c}"), ev_z1 if m == 0 else None])
                e_u = O("scalar", lambda e, c=c, ps=ps, upv=upv: e.activation(
                    out=upv[:, 3:3 + MACRO], in_=ps, func=AF.Identity, bias=bwt_sb[:, c:c + 1]),
                    after=[mm, e_h, ev_bwt])
                e_h2 = O("vector", lambda e, c=c, upv=upv: e.tensor_copy(
                    out=uhalo_sb[:, c * 3:c * 3 + 3], in_=upv[:, MACRO:MACRO + 3]), after=[e_u])
                ev_u[c] = (e_u, e_h2)
                setg(f"uhalo{c}", e_h2)
                setg(pname, e_u)
                cmm = None
                for j in range(4):
                    for hf in range(2):
                        cmm = O("tensor", lambda e, j=j, c=c, hf=hf, ps=ps, upv=upv: e.matmul(
                            ps[:, hf * 512:(hf + 1) * 512],
                            convd_sb[:, (j * NC8 + c) * 128:(j * NC8 + c + 1) * 128],
                            upv[:, j + hf * 512:j + hf * 512 + 512],
                            start=(j == 0), stop=(j == 3)),
                            after=[e_u, ev_convd, guard(pname)])
                e_cs = O("scalar", lambda e, c=c, ps=ps: e.activation(
                    out=u2_sb[:, c * MACRO:(c + 1) * MACRO], in_=ps, func=AF.Silu,
                    bias=convb_sb[:, c:c + 1]),
                    after=[cmm, guard("u2"), ud_events.get(m - 1, {}).get(c)])
                ev_u2[c] = e_cs
                setg(pname, e_cs)
                setg(f"upre{us}", cmm)
                if debug and m == 0 and c == 0:
                    O("gpsimd", lambda e: e.dma_start(out=dbg["dbg_can2"][:, 2 * NSUB:3 * NSUB], in_=negm_all[:, 0:NSUB]),
                      after=[ev_u[0][1]], dma=True)
                    e_upd = O("vector", lambda e, upv=upv: e.tensor_copy(out=fs[3], in_=upv[:, 3:3 + MACRO]),
                              after=[e_u, guard("fs3")])
                    e_upD = O("gpsimd", lambda e: e.dma_start(out=dbg["dbg_upre"], in_=fs[3][:, 0:MACRO]), after=[e_upd], dma=True)
                    setg("fs3", e_upD)
            else:
                c = oc - NC8
                zsl = oc % 2
                e_zv = O("scalar", lambda e, c=c, ps=ps, zsl=zsl: e.activation(
                    out=stage_bf[:, zsl * DM:zsl * DM + MACRO], in_=ps, func=AF.Silu,
                    bias=bwt_sb[:, NC8 + c:NC8 + c + 1]),
                    after=[mm, guard(f"stagebf{zsl}"), ev_bwt])
                e_zs = O("scalar", lambda e, c=c, zsl=zsl: e.dma_start(
                    out=sz_sp[c], in_=stage_bf[:, zsl * DM:zsl * DM + MACRO]),
                    after=[e_zv, ev_sz.get((m - 1, c))], dma=True, stream="spill")
                ev_z[c] = e_zs
                setg(f"stagebf{zsl}", e_zs)
                setg(pname, e_zv)

        rd2_events[m] = mm
        if debug and m == 0:
            O("gpsimd", lambda e: e.dma_start(out=dbg["dbg_can"][:, 0:NSUB], in_=negm_all[:, 0:NSUB]),
              after=[ev_u2[NC8 - 1]], dma=True)
        if debug and m == 0:
            e_c = O("vector", lambda e: e.tensor_copy(out=fs[2], in_=u2_sb[:, 0:MACRO]), after=[ev_u2[0], guard("fs2")])
            e_cD = O("gpsimd", lambda e: e.dma_start(out=dbg["dbg_u2"], in_=fs[2][:, 0:MACRO]), after=[e_c], dma=True)
            setg("fs2", e_cD)
        mark("p4")
        mm = None
        for k in range(NC8):
            for hf in range(2):
                mm = O("tensor", lambda e, k=k, hf=hf: e.matmul(
                    psB[0:96, hf * 512:hf * 512 + 512],
                    xprojw_sb.rearrange("p (c n) -> p c n", c=NC8)[:, k],
                    u2_sb[:, k * MACRO + hf * 512:k * MACRO + (hf + 1) * 512],
                    start=(k == 0), stop=(k == NC8 - 1)),
                    after=[ev_u2[k], ev_xpw, guard("psB")])
        e_ev = O("scalar", lambda e: e.copy(out=stage_f32[0:96, 0:MACRO], in_=psB[0:96, 0:MACRO]),
                 after=[mm, guard("stagef0")])
        setg("psB", e_ev)
        e_st = O("gpsimd", lambda e: e.dma_start(out=cc_dbc_in, in_=stage_f32[0:96, 0:MACRO]),
                 after=[e_ev, guard("cc_dbc_in"), ev_ffc[-1] if m == 0 else None], dma=True, stream="gst96")
        setg("stagef0", e_st)
        e_cc = O("gpsimd", lambda e: e.collective_compute(
            "AllReduce", ALU.add, replica_groups=RG,
            ins=[cc_dbc_in.opt()], outs=[cc_dbc_out.opt()]),
            after=[e_st, guard("cc_dbc_out")], coll=True, stream="coll")
        setg("cc_dbc_in", e_cc)
        e_dt = O("gpsimd", lambda e: e.dma_start(out=dt_sb, in_=cc_dbc_out[0:64, :]),
                 after=[e_cc, guard("dt_sb")], dma=True, stream="dtld")
        e_bcr = O("gpsimd", lambda e: e.dma_start(out=bcrows, in_=cc_dbc_out[64:96, :]),
                  after=[e_cc, guard("bcrows")], dma=True, stream="bcr")
        setg("cc_dbc_out", e_bcr)
        if debug and m == 0:
            O("gpsimd", lambda e: e.dma_start(out=dbg["dbg_dbc"], in_=cc_dbc_out), after=[e_cc], dma=True)

        if debug and m == 0:
            O("gpsimd", lambda e: e.dma_start(out=dbg["dbg_can"][:, NSUB:2 * NSUB], in_=negm_all[:, 0:NSUB]),
              after=[e_bcr], dma=True)
        mark("p5")
        ev_dl = {}
        for c in range(NC8):
            sl = c % 2
            ps = psA if sl == 0 else psC
            pname = "psA" if sl == 0 else "psC"
            mm = None
            for hf in range(2):
                mm = O("tensor", lambda e, c=c, hf=hf, ps=ps: e.matmul(
                    ps[:, hf * 512:(hf + 1) * 512], dtw_sb[:, c * 128:(c + 1) * 128],
                    dt_sb[:, hf * 512:(hf + 1) * 512], start=True, stop=True),
                    after=[e_dt, ev_dtw, guard(pname)])
            sp0 = spx_sb[:, 0:MACRO]
            sp1 = spx_sb[:, MACRO:2 * MACRO]
            ea = O("scalar", lambda e, c=c, ps=ps, sp0=sp0: e.activation(
                out=sp0, in_=ps, func=AF.Abs, bias=dtb_sb[:, c:c + 1]), after=[mm, guard("spx")])
            eb = O("scalar", lambda e, sp0=sp0: e.activation(out=sp0, in_=sp0, func=AF.Exp, scale=-1.0), after=[ea])
            ec = O("scalar", lambda e, sp0=sp0: e.activation(out=sp0, in_=sp0, func=AF.Ln, bias=cst_sb[:, 1:2]), after=[eb])
            ed = O("scalar", lambda e, c=c, ps=ps, sp1=sp1: e.activation(
                out=sp1, in_=ps, func=AF.Relu, bias=dtb_sb[:, c:c + 1]), after=[ec])
            ev = O("vector", lambda e, c=c, sp0=sp0, sp1=sp1: e.tensor_add(
                out=delta_sb[:, c * MACRO:(c + 1) * MACRO], in0=sp0, in1=sp1),
                after=[ed, guard("delta")])
            ev_dl[c] = ev
            setg("spx", ev)
            setg(pname, ed)
            if c == NC8 - 1:
                setg("dt_sb", mm)
        if debug and m == 0:
            e_c = O("vector", lambda e: e.tensor_copy(out=fs[2], in_=delta_sb[:, 0:MACRO]), after=[ev_dl[0], guard("fs2")])
            e_cD2 = O("gpsimd", lambda e: e.dma_start(out=dbg["dbg_delta"], in_=fs[2][:, 0:MACRO]), after=[e_c], dma=True)
            setg("fs2", e_cD2)

        mark("scan")
        ev_w = {}
        for c in range(NC8):
            ev_w[c] = O("vector", lambda e, c=c: e.tensor_mul(
                out=wq_sb[:, c * MACRO:(c + 1) * MACRO],
                in0=delta_sb[:, c * MACRO:(c + 1) * MACRO],
                in1=u2_sb[:, c * MACRO:(c + 1) * MACRO]),
                after=[ev_dl[c], ev_u2[c], guard("wq")])
        prev_tts = None
        jidx = 0
        ev_yq = {}
        ev_qevac = {}
        ev_ymul = {}
        for nq in range(4):
            ev_bcq = {}
            for j2 in range(4):
                nn = nq * 4 + j2
                eb_ = O("sync", lambda e, nn=nn, j2=j2: e.dma_start(
                    out=bbc_sb[:, j2 * MACRO:(j2 + 1) * MACRO],
                    in_=bcrows[nn:nn + 1, :].broadcast_to([128, MACRO])),
                    after=[e_bcr, guard(f"bbc{j2}")], dma=True, stream="bc")
                ec_ = O("sync", lambda e, nn=nn, j2=j2: e.dma_start(
                    out=cbc_sb[:, j2 * MACRO:(j2 + 1) * MACRO],
                    in_=bcrows[16 + nn:17 + nn, :].broadcast_to([128, MACRO])),
                    after=[e_bcr, guard(f"cbc{j2}")], dma=True, stream="bc")
                ev_bcq[j2] = (eb_, ec_)
                if nq == 3 and j2 == 3:
                    setg("bcrows", ec_)
            for c in range(NC8):
                ya = None
                for jn in range(4):
                    n = nq * 4 + jn
                    da = jidx % 2
                    jidx += 1
                    e_exp = O("scalar", lambda e, c=c, n=n, da=da: e.activation(
                        out=dA_sb[:, da * MACRO:(da + 1) * MACRO],
                        in_=delta_sb[:, c * MACRO:(c + 1) * MACRO], func=AF.Exp,
                        scale=a_sb[:, c * N16 + n:c * N16 + n + 1]),
                        after=[ev_dl[c], ev_a, guard(f"dA{da}")])
                    e_dbu = O("vector", lambda e, c=c, jn=jn: e.tensor_mul(
                        out=dBu_sb, in0=wq_sb[:, c * MACRO:(c + 1) * MACRO],
                        in1=bbc_sb[:, jn * MACRO:(jn + 1) * MACRO]),
                        after=[ev_w[c], ev_bcq[jn][0], guard("dBu")])
                    e_tts = O("vector", lambda e, c=c, n=n, jn=jn, da=da: e.tensor_tensor_scan(
                        out=h8_sb[:, jn * MACRO:(jn + 1) * MACRO],
                        data0=dA_sb[:, da * MACRO:(da + 1) * MACRO], data1=dBu_sb,
                        initial=hst_sb[:, c * N16 + n:c * N16 + n + 1], op0=ALU.mult, op1=ALU.add),
                        after=[e_exp, e_dbu, guard(f"h8_{jn}"),
                               ev_z0 if m == 0 else None,
                               ev_hsave.get((c, nq))])
                    setg(f"dA{da}", e_tts)
                    setg("dBu", e_tts)
                    prev_tts = e_tts
                    gsl = 0 if jn == 0 else 1
                    e_ym = O("vector", lambda e, jn=jn, gsl=gsl: e.tensor_tensor(
                        out=g_sb[:, gsl * MACRO:(gsl + 1) * MACRO],
                        in0=h8_sb[:, jn * MACRO:(jn + 1) * MACRO],
                        in1=cbc_sb[:, jn * MACRO:(jn + 1) * MACRO], op=ALU.mult),
                        after=[e_tts, ev_bcq[jn][1], guard(f"g{gsl}")])
                    setg(f"h8_{jn}", e_ym)
                    setg(f"g{gsl}", e_ym)
                    ev_ymul[(c, n)] = e_ym
                    if jn >= 1:
                        ga = O("vector", lambda e: e.tensor_add(
                            out=g_sb[:, 0:MACRO], in0=g_sb[:, 0:MACRO],
                            in1=g_sb[:, MACRO:2 * MACRO]),
                            after=[e_ym, guard("g0")])
                        setg("g0", ga)
                        setg("g1", ga)
                    if c == NC8 - 1:
                        setg(f"bbc{jn}", e_dbu)
                        setg(f"cbc{jn}", e_ym)
                eh = O("vector", lambda e, c=c, nq=nq: e.tensor_copy(
                    out=hst_sb[:, c * N16 + nq * 4:c * N16 + nq * 4 + 4],
                    in_=h8_sb.rearrange("p (s t) -> p s t", s=4)[:, :, MACRO - 1]),
                    after=[prev_tts])
                ev_hsave[(c, nq)] = eh
                yv = ysb_sb[:, c * MACRO:(c + 1) * MACRO]
                if nq == 0:
                    eq = O("vector", lambda e, yv=yv: e.tensor_copy(
                        out=yv, in_=g_sb[:, 0:MACRO]), after=[guard("g0"), guard(f"ysb{c}")])
                else:
                    eq = O("vector", lambda e, yv=yv: e.tensor_add(
                        out=yv, in0=yv, in1=g_sb[:, 0:MACRO]),
                        after=[guard("g0"), ev_qevac.get((nq - 1, c))])
                setg("g0", eq)
                ev_qevac[(nq, c)] = eq
        ev_yfin_c = {}
        for c in range(NC8):
            e_ud = O("vector", lambda e, c=c: e.scalar_tensor_tensor(
                out=ysb_sb[:, c * MACRO:(c + 1) * MACRO],
                in0=u2_sb[:, c * MACRO:(c + 1) * MACRO], scalar=dcol_sb[:, c:c + 1],
                in1=ysb_sb[:, c * MACRO:(c + 1) * MACRO],
                op0=ALU.mult, op1=ALU.add),
                after=[ev_u2[c], ev_qevac[(3, c)]])
            ud_events[m][c] = e_ud
            e_szl = O("sync", lambda e, c=c: e.dma_start(
                out=szb_sb, in_=sz_sp[c]),
                after=[ev_z[c], guard("szb")], dma=True, stream="szl")
            e_fin = O("vector", lambda e, c=c: e.tensor_tensor(
                out=y2b_sb, in0=ysb_sb[:, c * MACRO:(c + 1) * MACRO],
                in1=szb_sb, op=ALU.mult),
                after=[e_ud, e_szl, guard("y2b")])
            setg(f"ysb{c}", e_fin)
            setg("szb", e_fin)
            e_y2s = O("scalar", lambda e, c=c, m=m: e.dma_start(out=y2_sp[m % 2][c], in_=y2b_sb),
                      after=[e_fin, y2free.get(m % 2)], dma=True, stream="spill")
            ev_y2wr[c] = e_y2s
            ev_sz[(m, c)] = e_szl
            setg("y2b", e_y2s)
            ev_yfin_c[c] = e_y2s
            if debug and m == 0 and c == 0:
                e_cp = O("vector", lambda e: e.tensor_copy(out=fs[2], in_=y2b_sb),
                         after=[e_fin, guard("fs2")])
                e_cpD = O("gpsimd", lambda e: e.dma_start(out=dbg["dbg_y2"], in_=fs[2][:, 0:MACRO]), after=[e_cp], dma=True)
                setg("fs2", e_cpD)
        ev_scan_done = ev_yfin_c[NC8 - 1]
        setg("wq", ev_scan_done)

        if debug and m == 0:
            O("gpsimd", lambda e: e.dma_start(out=dbg["dbg_can"][:, 2 * NSUB:3 * NSUB], in_=negm_all[:, 0:NSUB]),
              after=[ev_scan_done], dma=True)
        yield
        mark("p7")
        y2r4 = y2r_sb.rearrange("p (t c n) -> p t c n", t=4, c=NC8)
        last_os = None
        for tgrp in range(2):
            ev_tile = {}
            for t in range(4):
                ts8 = tgrp * 4 + t
                ev_tile[t] = O("sync", lambda e, ts8=ts8, t=t: e.dma_start(
                    out=y2r4[:, t],
                    in_=y2_sp[m % 2][:, :, ts8 * SUB:(ts8 + 1) * SUB].rearrange("c p n -> p c n")),
                    after=[ev_scan_done, guard(f"y2r{t}")], dma=True, stream="y2l")
            mml = {}
            for ohf in range(2):
                gA = guard("psA")
                gC = guard("psC")
                for c in range(NC8):
                    sl = c % 2
                    e_owc = O("sync", lambda e, c=c, ohf=ohf, sl=sl: e.dma_start(
                        out=bw_sb[:, sl * DM:sl * DM + 512],
                        in_=outw_bf[c * 128:(c + 1) * 128, ohf * 512:(ohf + 1) * 512]),
                        after=[guard(f"bw{sl}"), prep_done if m == 0 else None],
                        dma=True, stream="ow")
                    for t in range(4):
                        ps = psA if t < 2 else psC
                        mm = O("tensor", lambda e, c=c, t=t, ps=ps, sl=sl: e.matmul(
                            ps[:, (t % 2) * 512:(t % 2) * 512 + 512],
                            y2r4[:, t, c], bw_sb[:, sl * DM:sl * DM + 512],
                            start=(c == 0), stop=(c == NC8 - 1)),
                            after=[ev_tile[t], e_owc] +
                                  ([gA if t < 2 else gC] if c == 0 else []))
                        mml[t] = mm
                    setg(f"bw{sl}", mm)
                for t in range(4):
                    ts8 = tgrp * 4 + t
                    ps = psA if t < 2 else psC
                    e_ev = O("scalar", lambda e, ps=ps, t=t: e.copy(
                        out=stage_f32[:, t * 512:(t + 1) * 512],
                        in_=ps[:, (t % 2) * 512:(t % 2) * 512 + 512]),
                        after=[mml[t]] + ([guard(f"stagef{t // 2}")] if t % 2 == 0 else []))
                    if t % 2 == 1:
                        setg("psA" if t < 2 else "psC", e_ev)
                    e_os = O("gpsimd", lambda e, ts8=ts8, t=t, ohf=ohf: e.dma_start(
                        out=cc_o_in[ts8 * SUB:(ts8 + 1) * SUB, ohf * 512:(ohf + 1) * 512],
                        in_=stage_f32[:, t * 512:(t + 1) * 512]),
                        after=[e_ev, guard("cc_o_in")], dma=True, stream="gst")
                    last_os = e_os
                    setg(f"stagef{t // 2}", e_os)
            for t in range(4):
                setg(f"y2r{t}", mml[t])
        y2free[m % 2] = ev_tile[3]
        e_cco = O("gpsimd", lambda e: e.collective_compute(
            "AllReduce", ALU.add, replica_groups=RG,
            ins=[cc_o_in.opt()], outs=[cc_o_out.opt()]),
            after=[last_os, guard("cc_o_out")], coll=True, stream="coll")
        setg("cc_o_in", e_cco)
        if debug and m == 0:
            O("gpsimd", lambda e: e.dma_start(out=dbg["dbg_o"], in_=cc_o_out), after=[e_cco], dma=True)

        if debug and m == 0:
            O("gpsimd", lambda e: e.dma_start(out=dbg["dbg_can"][:, 3 * NSUB:4 * NSUB], in_=negm_all[:, 0:NSUB]),
              after=[e_cco], dma=True)
        mark("p8")
        ev_x2s = {}
        for i in range(NSUB):
            gi = gbase + i
            xs = i % 2
            e_xl = O("sync", lambda e, i=i, xs=xs, T0=T0: e.dma_start(
                out=fs[xs], in_=x_seq[T0 + i * SUB:T0 + (i + 1) * SUB, :]),
                after=[guard(f"fs{xs}")], dma=True, stream="xload")
            e_ol = O("sync", lambda e, i=i: e.dma_start(
                out=fs[2], in_=cc_o_out[i * SUB:(i + 1) * SUB, :]),
                after=[e_cco, guard("fs2")], dma=True, stream="ol")
            p_a0 = O("scalar", lambda e, i=i, xs=xs: e.activation(
                out=fs[3], in_=fs[xs], func=AF.Copy, accum_out=st_acc[:, 4 * i:4 * i + 1]),
                after=[e_xl, guard("fs3"), guard("st_acc")])
            p_a1 = O("scalar", lambda e, i=i, xs=xs: e.activation(
                out=fs[3], in_=fs[xs], func=AF.Square, accum_out=st_acc[:, 4 * i + 1:4 * i + 2]), after=[p_a0])
            pm = st_tmp[:, 8 * NSUB + i:8 * NSUB + i + 1]
            pv = st_tmp[:, 9 * NSUB + i:9 * NSUB + i + 1]
            pq = st_tmp[:, 10 * NSUB + i:10 * NSUB + i + 1]
            p_m = O("vector", lambda e, i=i, pm=pm: e.tensor_single_scalar(
                out=pm, in_=st_acc[:, 4 * i:4 * i + 1], scalar=-1.0 / DM, op=ALU.mult), after=[p_a1])
            p_v1 = O("vector", lambda e, pm=pm, pq=pq: e.tensor_mul(out=pq, in0=pm, in1=pm), after=[p_m])
            p_v2 = O("vector", lambda e, i=i, pv=pv, pq=pq: e.scalar_tensor_tensor(
                out=pv, in0=st_acc[:, 4 * i + 1:4 * i + 2], scalar=1.0 / DM, in1=pq,
                op0=ALU.mult, op1=ALU.subtract), after=[p_v1])
            p_v3 = O("scalar", lambda e, pv=pv: e.activation(out=pv, in_=pv, func=AF.Ln, bias=cst_sb[:, 0:1]), after=[p_v2])
            p_v4 = O("scalar", lambda e, pv=pv: e.activation(out=pv, in_=pv, func=AF.Exp, scale=-0.5), after=[p_v3])
            p_nr = O("vector", lambda e, pm=pm, pv=pv, pq=pq: e.tensor_mul(out=pq, in0=pm, in1=pv), after=[p_v4])
            e_o1 = O("scalar", lambda e, xs=xs, pv=pv, pq=pq: e.activation(
                out=fs[3], in_=fs[xs], func=AF.Identity, scale=pv, bias=pq), after=[p_nr, guard("fs3")])
            setg("st_acc", p_a1)
            if debug and m == 0 and i == 0:
                e_xlnD = O("gpsimd", lambda e: e.dma_start(out=dbg["dbg_xln1"], in_=fs[3]), after=[e_o1], dma=True)
                e_nmD = O("gpsimd", lambda e: e.dma_start(out=dbg["dbg_nm"][:, 0:NSUB], in_=negm_all[:, 0:NSUB]), after=[e_o1], dma=True)
                e_r1D = O("gpsimd", lambda e: e.dma_start(out=dbg["dbg_nm"][:, NSUB:2 * NSUB], in_=r1_all[:, 0:NSUB]), after=[e_o1], dma=True)
            e_o2 = O("vector", lambda e: e.tensor_add(out=fs[2], in0=fs[2], in1=fs[3]),
                     after=[e_ol, e_o1] + ([e_xlnD] if debug and m == 0 and i == 0 else []))
            setg("fs3", e_o2)
            e_a0 = O("scalar", lambda e, i=i: e.activation(
                out=fs[3], in_=fs[2], func=AF.Copy, accum_out=st_acc[:, 4 * i + 2:4 * i + 3]),
                after=[e_o2, ev_stacc2_free, guard("fs3")])
            e_a1 = O("scalar", lambda e, i=i: e.activation(
                out=fs[3], in_=fs[2], func=AF.Square, accum_out=st_acc[:, 4 * i + 3:4 * i + 4]), after=[e_a0])
            setg("fs3", e_a1)
            tm = st_tmp[:, 3 * NSUB + i:3 * NSUB + i + 1]
            tv = st_tmp[:, 4 * NSUB + i:4 * NSUB + i + 1]
            e_m = O("vector", lambda e, i=i, tm=tm: e.tensor_single_scalar(
                out=tm, in_=st_acc[:, 4 * i + 2:4 * i + 3], scalar=-1.0 / DM, op=ALU.mult), after=[e_a1])
            tq = st_tmp[:, 7 * NSUB + i:7 * NSUB + i + 1]
            e_v1 = O("vector", lambda e, tm=tm, tq=tq: e.tensor_mul(out=tq, in0=tm, in1=tm), after=[e_m])
            e_v2 = O("vector", lambda e, i=i, tv=tv, tq=tq: e.scalar_tensor_tensor(
                out=tv, in0=st_acc[:, 4 * i + 3:4 * i + 4], scalar=1.0 / DM, in1=tq,
                op0=ALU.mult, op1=ALU.subtract), after=[e_v1])
            e_v3 = O("scalar", lambda e, tv=tv: e.activation(out=tv, in_=tv, func=AF.Ln, bias=cst_sb[:, 0:1]), after=[e_v2])
            e_v4 = O("scalar", lambda e, tv=tv: e.activation(out=tv, in_=tv, func=AF.Exp, scale=-0.5), after=[e_v3])
            x2s = i % 2
            nr2 = st_tmp[:, 7 * NSUB + i:7 * NSUB + i + 1]
            e_nr2 = O("vector", lambda e, tm=tm, tv=tv, nr2=nr2: e.tensor_mul(out=nr2, in0=tm, in1=tv),
                      after=[e_v4, guard("st7")])
            e_x1 = O("scalar", lambda e, tv=tv, nr2=nr2: e.activation(
                out=fs[2], in_=fs[2], func=AF.Identity, scale=tv, bias=nr2), after=[e_nr2])
            setg("st7", e_x1)
            e_x2 = O("vector", lambda e, xs=xs, x2s=x2s: e.tensor_add(
                out=x2_sb[:, x2s * DM:(x2s + 1) * DM], in0=fs[2], in1=fs[xs]),
                after=[e_x1, guard(f"x2_{x2s}")])
            setg("fs2", e_x2)
            setg(f"fs{xs}", e_x2)
            e_x2sp = O("scalar", lambda e, i=i, x2s=x2s: e.dma_start(
                out=x2_sp[i * SUB:(i + 1) * SUB, :], in_=x2_sb[:, x2s * DM:(x2s + 1) * DM]),
                after=[e_x2, guard("x2_sp")], dma=True, stream="spill")
            ev_x2s[i] = (e_x2, e_x2sp, x2s)
            if debug and m == 0 and i == 0:
                e_cp = O("vector", lambda e: e.tensor_copy(out=fs[3], in_=x2_sb[:, 0:DM]), after=[e_x2, guard("fs3")])
                e_cpD2 = O("gpsimd", lambda e: e.dma_start(out=dbg["dbg_x2"], in_=fs[3]), after=[e_cp], dma=True)
                setg("fs3", e_cpD2)
            x2v = x2_sb[:, x2s * DM:(x2s + 1) * DM]
            f_a0 = O("scalar", lambda e, i=i, x2v=x2v: e.activation(
                out=fs[3], in_=x2v, func=AF.Copy, accum_out=st_acc[:, 4 * i:4 * i + 1]),
                after=[e_x2, ev_stacc_free, guard("fs3")])
            f_a1 = O("scalar", lambda e, i=i, x2v=x2v: e.activation(
                out=fs[3], in_=x2v, func=AF.Square, accum_out=st_acc[:, 4 * i + 1:4 * i + 2]), after=[f_a0])
            setg("fs3", f_a1)
            tm3 = st_tmp[:, 5 * NSUB + i:5 * NSUB + i + 1]
            tv3 = st_tmp[:, 6 * NSUB + i:6 * NSUB + i + 1]
            f_m = O("vector", lambda e, i=i, tm3=tm3: e.tensor_single_scalar(
                out=tm3, in_=st_acc[:, 4 * i:4 * i + 1], scalar=-1.0 / DM, op=ALU.mult), after=[f_a1])
            tq3 = st_tmp[:, 7 * NSUB + i:7 * NSUB + i + 1]
            f_v1 = O("vector", lambda e, tm3=tm3, tq3=tq3: e.tensor_mul(out=tq3, in0=tm3, in1=tm3), after=[f_m])
            f_v2 = O("vector", lambda e, i=i, tv3=tv3, tq3=tq3: e.scalar_tensor_tensor(
                out=tv3, in0=st_acc[:, 4 * i + 1:4 * i + 2], scalar=1.0 / DM, in1=tq3,
                op0=ALU.mult, op1=ALU.subtract), after=[f_v1])
            f_v3 = O("scalar", lambda e, tv3=tv3: e.activation(out=tv3, in_=tv3, func=AF.Ln, bias=cst_sb[:, 0:1]), after=[f_v2])
            f_v4 = O("scalar", lambda e, tv3=tv3: e.activation(out=tv3, in_=tv3, func=AF.Exp, scale=-0.5), after=[f_v3])
            sl = i % 2
            f_xn = O("vector", lambda e, tm3=tm3, tv3=tv3, sl=sl, x2v=x2v: e.tensor_scalar(
                out=stage_bf[:, sl * DM:(sl + 1) * DM], in0=x2v, scalar1=tm3, scalar2=tv3,
                op0=ALU.add, op1=ALU.mult), after=[f_v4, guard(f"stagebf{sl}")])
            setg(f"x2_{x2s}", f_xn)
            tp = None
            for k in range(NC8):
                tp = O("tensor", lambda e, k=k, sl=sl: e.transpose(
                    psT3[:, k], stage_bf[:, sl * DM + k * 128:sl * DM + (k + 1) * 128], eye_sb),
                    after=[f_xn, guard("psT")])
            f_ev = O("scalar", lambda e, i=i: e.copy(out=xnt3[:, :, i * SUB:(i + 1) * SUB], in_=psT3),
                     after=[tp] + ([rd2_events.get(m + 1)] if i == 0 else []))
            setg("psT", f_ev)
            setg(f"stagebf{sl}", f_ev)
            ev_x2s[i] = (e_x2, e_x2sp, x2s)
            if i == NSUB - 1:
                ev_xnt3_done = f_ev
        setg("x2_sp", ev_x2s[NSUB - 1][1])
        ev_stacc_free = ev_xnt3_done

        ffg3 = ffg_sb.rearrange("p (s k n) -> p s k n", s=2, k=NC8)
        mark("p9")
        ffu3 = ffu_sb.rearrange("p (s k n) -> p s k n", s=2, k=NC8)
        ev_dn = {}
        mm_last_xnt = None
        for half in range(2):
            mark("p9a")
            hof = half * 512
            ev_hact = {}
            for hc in range(HK):
                sl = hc % 2
                e_wg = O("sync", lambda e, hc=hc, sl=sl: e.dma_start(
                    out=ffg3[:, sl], in_=ffng_bf[:, hc * 128:(hc + 1) * 128].rearrange("(k p) n -> p k n", p=128)),
                    after=[guard(f"ffg{sl}"), ev_ffc[0] if m == 0 else None], dma=True, stream="ffgu")
                e_wu = O("sync", lambda e, hc=hc, sl=sl: e.dma_start(
                    out=ffu3[:, sl], in_=ffnu_bf[:, hc * 128:(hc + 1) * 128].rearrange("(k p) n -> p k n", p=128)),
                    after=[guard(f"ffu{sl}"), ev_ffc[1] if m == 0 else None], dma=True, stream="ffgu")
                mmg = mmu = None
                for k in range(NC8):
                    mmg = O("tensor", lambda e, k=k, sl=sl, hof=hof: e.matmul(
                        psA[:, 0:512], ffg3[:, sl, k],
                        xnt3[:, k, hof:hof + 512], start=(k == 0), stop=(k == NC8 - 1)),
                        after=[e_wg, ev_xnt3_done, guard("psA")])
                for k in range(NC8):
                    mmu = O("tensor", lambda e, k=k, sl=sl, hof=hof: e.matmul(
                        psC[:, 0:512], ffu3[:, sl, k],
                        xnt3[:, k, hof:hof + 512], start=(k == 0), stop=(k == NC8 - 1)),
                        after=[e_wu, ev_xnt3_done, guard("psC")])
                setg(f"ffg{sl}", mmg)
                setg(f"ffu{sl}", mmu)
                mm_last_xnt = mmu
                e_sg = O("scalar", lambda e: e.activation(
                    out=stage_bf[:, 0:512], in_=psA[:, 0:512], func=AF.Silu, scale=1.0 / 127.0),
                    after=[mmg, guard("stagebf0")])
                setg("psA", e_sg)
                e_su = O("scalar", lambda e: e.activation(
                    out=stage_bf[:, DM:DM + 512], in_=psC[:, 0:512], func=AF.Copy, scale=1.0 / 127.0),
                    after=[mmu, guard("stagebf1")])
                setg("psC", e_su)
                e_h = O("vector", lambda e, hc=hc: e.tensor_mul(
                    out=hact_sb[:, hc * 512:(hc + 1) * 512],
                    in0=stage_bf[:, 0:512], in1=stage_bf[:, DM:DM + 512]),
                    after=[e_sg, e_su, guard("hact"),
                           ud_events.get(m + 1, {}).get(hc // 2)])
                ev_hact[hc] = e_h
                setg("stagebf0", e_h)
                setg("stagebf1", e_h)
            mark("p9b")
            for dmh in range(2):
                gA = guard("psA")
                gC = guard("psC")
                mmt = {}
                for hk in range(HK):
                    sl = hk % 2
                    e_wdc = O("sync", lambda e, hk=hk, dmh=dmh, sl=sl: e.dma_start(
                        out=bw_sb[:, sl * DM:sl * DM + 512],
                        in_=ffnd_bf[hk * 128:(hk + 1) * 128, dmh * 512:(dmh + 1) * 512]),
                        after=[guard(f"bw{sl}"), ev_ffc[2] if m == 0 else None],
                        dma=True, stream="wd")
                    mm = None
                    for t4 in range(4):
                        ps = psA if t4 < 2 else psC
                        mm = O("tensor", lambda e, hk=hk, t4=t4, ps=ps, sl=sl: e.matmul(
                            ps[:, (t4 % 2) * 512:(t4 % 2) * 512 + 512],
                            hact_sb[:, hk * 512 + t4 * SUB:hk * 512 + (t4 + 1) * SUB],
                            bw_sb[:, sl * DM:sl * DM + 512],
                            start=(hk == 0), stop=(hk == HK - 1)),
                            after=[ev_hact[hk], e_wdc] +
                                  ([gA if t4 < 2 else gC] if hk == 0 else []))
                        mmt[t4] = mm
                    setg(f"bw{sl}", mm)
                for t4 in range(4):
                    ts8 = half * 4 + t4
                    ps = psA if t4 < 2 else psC
                    e_ev = O("scalar", lambda e, ps=ps, t4=t4: e.activation(
                        out=stage_f32[:, t4 * 512:(t4 + 1) * 512],
                        in_=ps[:, (t4 % 2) * 512:(t4 % 2) * 512 + 512],
                        func=AF.Copy, scale=1.0 / 127.0),
                        after=[mmt[t4]] + ([guard(f"stagef{t4 // 2}")] if t4 % 2 == 0 else []))
                    if t4 % 2 == 1:
                        setg("psA" if t4 < 2 else "psC", e_ev)
                    e_fsd = O("gpsimd", lambda e, ts8=ts8, dmh=dmh, t4=t4: e.dma_start(
                        out=cc_f_in[ts8 * SUB:(ts8 + 1) * SUB, dmh * 512:(dmh + 1) * 512],
                        in_=stage_f32[:, t4 * 512:(t4 + 1) * 512]),
                        after=[e_ev, guard("cc_f_in")], dma=True, stream="gst")
                    ev_dn[(half, dmh, t4)] = e_fsd
                    setg(f"stagef{t4 // 2}", e_fsd)
            setg("hact", ev_dn[(half, 1, 3)])
        ev_xnt_free = mm_last_xnt
        setg("u2", ev_dn[(1, 1, 3)])
        e_ccf = O("gpsimd", lambda e: e.collective_compute(
            "AllReduce", ALU.add, replica_groups=RG,
            ins=[cc_f_in.opt()], outs=[cc_f_out.opt()]),
            after=[ev_dn[(1, 1, 3)], guard("cc_f_out")], coll=True, stream="coll")
        setg("cc_f_in", e_ccf)

        yield
        mark("p10")
        for i in range(NSUB):
            e_ol = O("sync", lambda e, i=i: e.dma_start(
                out=fs[2], in_=cc_f_out[i * SUB:(i + 1) * SUB, :]),
                after=[e_ccf, guard("fs2")], dma=True, stream="ol")
            x2rs = i % 2
            e_xr = O("sync", lambda e, i=i, x2rs=x2rs: e.dma_start(
                out=x2r_sb[:, x2rs * DM:(x2rs + 1) * DM], in_=x2_sp[i * SUB:(i + 1) * SUB, :]),
                after=[ev_x2s[i][1], guard(f"x2r{x2rs}"), guard(f"x2_{x2rs}")], dma=True, stream="x2l")
            e_a0 = O("scalar", lambda e, i=i: e.activation(
                out=fs[3], in_=fs[2], func=AF.Copy, accum_out=st_acc[:, 4 * i + 2:4 * i + 3]),
                after=[e_ol, ev_stacc2_free, guard("fs3")])
            e_a1 = O("scalar", lambda e, i=i: e.activation(
                out=fs[3], in_=fs[2], func=AF.Square, accum_out=st_acc[:, 4 * i + 3:4 * i + 4]), after=[e_a0])
            setg("fs3", e_a1)
            tm = st_tmp[:, 3 * NSUB + i:3 * NSUB + i + 1]
            tv = st_tmp[:, 4 * NSUB + i:4 * NSUB + i + 1]
            e_m = O("vector", lambda e, i=i, tm=tm: e.tensor_single_scalar(
                out=tm, in_=st_acc[:, 4 * i + 2:4 * i + 3], scalar=-1.0 / DM, op=ALU.mult), after=[e_a1])
            tq = st_tmp[:, 7 * NSUB + i:7 * NSUB + i + 1]
            e_v1 = O("vector", lambda e, tm=tm, tq=tq: e.tensor_mul(out=tq, in0=tm, in1=tm), after=[e_m])
            e_v2 = O("vector", lambda e, i=i, tv=tv, tq=tq: e.scalar_tensor_tensor(
                out=tv, in0=st_acc[:, 4 * i + 3:4 * i + 4], scalar=1.0 / DM, in1=tq,
                op0=ALU.mult, op1=ALU.subtract), after=[e_v1])
            e_v3 = O("scalar", lambda e, tv=tv: e.activation(out=tv, in_=tv, func=AF.Ln, bias=cst_sb[:, 0:1]), after=[e_v2])
            e_v4 = O("scalar", lambda e, tv=tv: e.activation(out=tv, in_=tv, func=AF.Exp, scale=-0.5), after=[e_v3])
            nr3 = st_tmp[:, 7 * NSUB + i:7 * NSUB + i + 1]
            e_nr3 = O("vector", lambda e, tm=tm, tv=tv, nr3=nr3: e.tensor_mul(out=nr3, in0=tm, in1=tv),
                      after=[e_v4, guard("st7")])
            e_x1 = O("scalar", lambda e, tv=tv, nr3=nr3: e.activation(
                out=fs[2], in_=fs[2], func=AF.Identity, scale=tv, bias=nr3), after=[e_nr3])
            setg("st7", e_x1)
            e_x2f = O("vector", lambda e, x2rs=x2rs: e.tensor_add(
                out=fs[2], in0=fs[2], in1=x2r_sb[:, x2rs * DM:(x2rs + 1) * DM]), after=[e_x1, e_xr])
            setg(f"x2r{x2rs}", e_x2f)
            setg(f"x2_{x2rs}", e_x2f)
            last_store = O("scalar", lambda e, i=i, T0=T0: e.dma_start(
                out=y_out[T0 + i * SUB:T0 + (i + 1) * SUB, :], in_=fs[2]),
                after=[e_x2f, last_store], dma=True, stream="yout")
            setg("fs2", last_store)
            ev_stacc2_free = e_x2f
        setg("x2_sp", e_xr)
        setg("cc_f_out", e_ol)
        setg("cc_o_out", ev_x2s[NSUB - 1][0])

    gens = [macro_body(mq) for mq in range(NMAC)]

    def _step(g):
        try:
            next(g)
        except StopIteration:
            pass
    kpipe = os.environ.get("KPIPE", "2")
    if kpipe == "0":
        for g in gens:
            _step(g)
            _step(g)
            _step(g)
    elif kpipe == "2":
        _step(gens[0])
        for mi in range(NMAC):
            if mi + 1 < NMAC:
                _step(gens[mi + 1])
            _step(gens[mi])
            _step(gens[mi])
    else:
        _step(gens[0])
        _step(gens[1])
        _step(gens[0])
        for mi in range(NMAC):
            if mi + 2 < NMAC:
                _step(gens[mi + 2])
            _step(gens[mi])
            if mi + 1 < NMAC:
                _step(gens[mi + 1])

    sch.wait_only("scalar", [last_store])
    sch.wait_only("gpsimd", [last_store])

    import contextlib
    stack = contextlib.ExitStack()
    sems = {name: stack.enter_context(nc.semaphore(name)) for name in sch.cnt}
    with nc.Block() as block:
        sch.emit(block, sems)
    stack.close()
    n_ops = sum(len(v) for v in sch.ops.values())
    return nc, n_ops


_cache = {}


def kernel(**inputs):
    import os
    if "nc" not in _cache:
        _cache["nc"], _cache["nops"] = build_nc(
            debug=False, serial=os.environ.get("KSERIAL", "0") == "1",
            serial_phases=tuple(x for x in os.environ.get("KSPH", "p9b,p7").split(",") if x))
    nc = _cache["nc"]
    in_maps = [host_prep(inputs, core) for core in range(8)]
    res = run_bass_kernel_spmd(nc, in_maps, core_ids=list(range(8)))
    out = np.empty((4, L, DM), np.float32)
    for p in range(4):
        out[p, :L // 2] = res.results[2 * p]["y_out"][:L // 2]
        out[p, L // 2:] = res.results[2 * p + 1]["y_out"][L // 2:]
    return out


# version 07
# speedup vs baseline: 1.1419x; 1.1419x over previous
import numpy as np
import concourse.bass as bass
import concourse.mybir as mybir
from concourse.bass_utils import run_bass_kernel_spmd

F32 = mybir.dt.float32
F16 = mybir.dt.float16
BF16 = mybir.dt.bfloat16
I32 = mybir.dt.int32
AF = mybir.ActivationFunctionType
ALU = mybir.AluOpType

L = 8192
DM = 1024
DIN = 2048
DH = 1024
NC8 = 8
N16 = 16
HH = 1536
HK = 12
MACRO = 1024
NMAC = L // MACRO
SUB = 128
NSUB = MACRO // SUB
EPS = 1e-5
RG = [[0, 1], [2, 3], [4, 5], [6, 7]]


class Sched:
    ENG = ("sync", "scalar", "vector", "tensor", "gpsimd")

    def __init__(self, nc):
        self.nc = nc
        self.ops = {e: [] for e in self.ENG}
        self.cnt = {}
        for e in self.ENG:
            self.cnt["e_" + e] = 0
            self.cnt["d_" + e] = 0

    SERIAL = False
    MARK = False

    def op(self, eng, emit, after=(), dma=False, coll=False, stream=None):
        if dma or coll:
            sem = "d_" + (stream or eng)
        else:
            sem = "e_" + eng
        inc = 16 if dma else 1
        if sem not in self.cnt:
            self.cnt[sem] = 0
        self.cnt[sem] += inc
        after = tuple(a for a in after if a is not None)
        if self.SERIAL or self.MARK:
            after = after + ((self.last_ev,) if getattr(self, "last_ev", None) else ())
        self.ops[eng].append((emit, after, sem, inc))
        ev = (sem, self.cnt[sem])
        self.last_ev = ev
        return ev

    def wait_only(self, eng, after):
        self.ops[eng].append((None, tuple(a for a in after if a is not None), None, 0))

    def emit(self, block, sems):
        for eng_name in self.ENG:
            ops = self.ops[eng_name]
            if not ops:
                continue

            def mk(eng_name, ops):
                own_e = "e_" + eng_name
                tight = eng_name in ("vector", "scalar", "gpsimd")

                def body(eng):
                    hi = {}
                    own_cnt = 0
                    for emit, after, sem, inc in ops:
                        for wsem, wval in after:
                            if wsem == own_e:
                                if not tight or own_cnt - wval > 4:
                                    continue
                            if hi.get(wsem, 0) >= wval:
                                continue
                            eng.wait_ge(sems[wsem], wval)
                            hi[wsem] = wval
                        if emit is not None:
                            ins = emit(eng)
                            if ins is not None and sem is not None:
                                ins.then_inc(sems[sem], inc)
                        if sem == own_e:
                            own_cnt += inc
                return body

            getattr(block, eng_name)(mk(eng_name, ops))


def host_prep(inputs, core):
    pair, role = core // 2, core % 2
    d0 = role * DH
    h0 = role * HH
    ip = {}
    ip["x_seq"] = np.ascontiguousarray(np.asarray(inputs["L0_attn_x"])[pair], dtype=np.float32)
    W = np.asarray(inputs["L0_attn_in_proj_w"])
    ip["w_in"] = np.ascontiguousarray(
        np.concatenate([W[:, d0:d0 + DH], W[:, DIN + d0:DIN + d0 + DH]], axis=1))
    cw = np.asarray(inputs["L0_attn_conv_w"])[d0:d0 + DH]
    idx = np.arange(128)
    diag = np.zeros((4, NC8, 128, 128), np.float32)
    for j in range(4):
        for c in range(NC8):
            diag[j, c, idx, idx] = cw[c * 128:(c + 1) * 128, j]
    ip["conv_diag"] = diag
    ip["conv_b"] = np.ascontiguousarray(
        np.asarray(inputs["L0_attn_conv_b"])[d0:d0 + DH].reshape(NC8, 128).T)
    ip["xprojw"] = np.ascontiguousarray(np.asarray(inputs["L0_attn_x_proj_w"])[d0:d0 + DH])
    ip["dtw"] = np.ascontiguousarray(np.asarray(inputs["L0_attn_dt_proj_w"])[:, d0:d0 + DH])
    ip["dtb"] = np.ascontiguousarray(
        np.asarray(inputs["L0_attn_dt_proj_b"])[d0:d0 + DH].reshape(NC8, 128).T)
    ip["alog"] = np.ascontiguousarray(
        np.asarray(inputs["L0_attn_A_log"])[d0:d0 + DH].reshape(NC8, 128, N16)
        .transpose(1, 0, 2).reshape(128, NC8 * N16))
    Dv = np.asarray(inputs["L0_attn_D"])[d0:d0 + DH]
    ddiag = np.zeros((NC8, 128, 128), np.float32)
    for c in range(NC8):
        ddiag[c, idx, idx] = Dv[c * 128:(c + 1) * 128]
    ip["d_diag"] = ddiag
    ip["outw"] = np.ascontiguousarray(np.asarray(inputs["L0_attn_out_proj_w"])[d0:d0 + DH])
    ip["ln_g"] = np.ascontiguousarray(np.asarray(inputs["L0_attn_ln_g"]).reshape(NC8, 128).T)
    ip["ln_b"] = np.ascontiguousarray(np.asarray(inputs["L0_attn_ln_b"]).reshape(NC8, 128).T)
    ip["wgT"] = np.ascontiguousarray(np.asarray(inputs["L0_ffn_w_gate"])[h0:h0 + HH].T)
    ip["wuT"] = np.ascontiguousarray(np.asarray(inputs["L0_ffn_w_up"])[h0:h0 + HH].T)
    ip["wdT"] = np.ascontiguousarray(np.asarray(inputs["L0_ffn_w_down"])[:, h0:h0 + HH].T)
    ip["eye"] = np.eye(128, dtype=np.float32)
    return ip


def build_nc(debug=False, serial=False, serial_phases=()):
    nc = bass.Bass("TRN2", target_bir_lowering=False)
    A = lambda name, shape, dt: nc.dram_tensor(name, shape, dt, kind="ExternalInput").ap()
    x_seq = A("x_seq", [L, DM], F32)
    w_in = A("w_in", [DM, 2 * DH], F32)
    conv_diag = A("conv_diag", [4, NC8, 128, 128], F32)
    conv_b = A("conv_b", [128, NC8], F32)
    xprojw = A("xprojw", [DH, 96], F32)
    dtw_i = A("dtw", [64, DH], F32)
    dtb_i = A("dtb", [128, NC8], F32)
    alog = A("alog", [128, NC8 * N16], F32)
    d_diag = A("d_diag", [NC8, 128, 128], F32)
    outw = A("outw", [DH, DM], F32)
    ln_g = A("ln_g", [128, NC8], F32)
    ln_b = A("ln_b", [128, NC8], F32)
    wgT = A("wgT", [DM, HH], I32)
    wuT = A("wuT", [DM, HH], I32)
    wdT = A("wdT", [HH, DM], I32)
    eye_i = A("eye", [128, 128], F32)
    y_out = nc.dram_tensor("y_out", [L, DM], F32, kind="ExternalOutput").ap()

    Dr = lambda name, shape, dt: nc.dram_tensor(name, shape, dt).ap()
    wg_bf = Dr("wg_bf", [DM, 2 * DH], BF16)
    outw_bf = Dr("outw_bf", [DH, DM], BF16)
    ffng_bf = Dr("ffng_bf", [DM, HH], BF16)
    ffnu_bf = Dr("ffnu_bf", [DM, HH], BF16)
    ffnd_bf = Dr("ffnd_bf", [HH, DM], BF16)
    y2_sp = Dr("y2_sp", [NC8, 128, MACRO], BF16)
    sz_sp = Dr("sz_sp", [NC8, 128, MACRO], BF16)
    x2_sp = Dr("x2_sp", [MACRO, DM], F16)
    bwrow = Dr("bwrow", [2048], F32)
    cc_dbc_in = Dr("cc_dbc_in", [96, MACRO], F32)
    cc_dbc_out = Dr("cc_dbc_out", [96, MACRO], F32)
    bcrows = Dr("bcrows", [32, MACRO], F16)
    cc_o_in = Dr("cc_o_in", [MACRO, DM], F32)
    cc_o_out = Dr("cc_o_out", [MACRO, DM], F32)
    cc_f_in = Dr("cc_f_in", [MACRO, DM], BF16)
    cc_f_out = Dr("cc_f_out", [MACRO, DM], BF16)

    dbg = {}
    if debug:
        for nm, shape in [("dbg_xn", [128, DM]), ("dbg_u2", [128, MACRO]),
                          ("dbg_delta", [128, MACRO]), ("dbg_dbc", [96, MACRO]),
                          ("dbg_y2", [128, MACRO]), ("dbg_o", [MACRO, DM]),
                          ("dbg_x2", [128, DM]), ("dbg_upre", [128, MACRO]),
                          ("dbg_xnt", [128, MACRO]), ("dbg_bwt", [128, 16]), ("dbg_stats", [128, 40]), ("dbg_xln1", [128, DM]), ("dbg_nm", [128, 16]), ("dbg_can", [128, 40]), ("dbg_can2", [128, 40])]:
            dbg[nm] = nc.dram_tensor(nm, shape, F32, kind="ExternalOutput").ap()

    S = lambda name, shape, dt: nc.alloc_sbuf_tensor(name, shape, dt).ap()
    negm_all = S("negm_all", [128, NMAC * NSUB], F32)
    r1_all = S("r1_all", [128, NMAC * NSUB], F32)
    s_all = S("s_all", [128, NMAC * NSUB], F32)
    convd_sb = S("convd_sb", [128, 4 * NC8 * 128], BF16)
    ddiag_sb = S("ddiag_sb", [128, NC8 * 128], BF16)
    xprojw_sb = S("xprojw_sb", [128, NC8 * 96], BF16)
    dtw_sb = S("dtw_sb", [64, DH], BF16)
    a_sb = S("a_sb", [128, NC8 * N16], F32)
    eye_sb = S("eye_sb", [128, 128], BF16)
    bwt_sb = S("bwt_sb", [128, 16], F32)
    convb_sb = S("convb_sb", [128, NC8], F32)
    dtb_sb = S("dtb_sb", [128, NC8], F32)
    gcol_sb = S("gcol_sb", [128, NC8], F32)
    bcol_sb = S("bcol_sb", [128, NC8], BF16)
    wgoc_sb = S("wgoc_sb", [128, 2 * NC8 * 128], BF16)
    outw_sb = S("outw_sb", [128, NC8 * DM], BF16)
    ffg_sb = S("ffg_sb", [128, 2 * NC8 * 128], BF16)
    ffu_sb = S("ffu_sb", [128, 2 * NC8 * 128], BF16)
    xnt_sb = S("xnt_sb", [128, NC8 * MACRO], BF16)
    delta_sb = S("delta_sb", [128, NC8 * MACRO], F16)
    u2_sb = S("u2_sb", [128, NC8 * MACRO], BF16)
    upre_sb = S("upre_sb", [128, 2 * (MACRO + 4)], BF16)
    uhalo_sb = S("uhalo_sb", [128, 3 * NC8], BF16)
    wq_sb = S("wq_sb", [128, NC8 * MACRO], F16)
    ysb_sb = S("ysb_sb", [128, NC8 * MACRO], BF16)
    dA_sb = S("dA_sb", [128, 2 * MACRO], F32)
    dBu_sb = S("dBu_sb", [128, MACRO], F16)
    h8_sb = S("h8_sb", [128, 4 * MACRO], BF16)
    g_sb = S("g_sb", [128, 2 * MACRO], BF16)
    bbc_sb = S("bbc_sb", [128, 4 * MACRO], F16)
    cbc_sb = S("cbc_sb", [128, 4 * MACRO], F16)
    hst_sb = S("hst_sb", [128, NC8 * N16], F32)
    dt_sb = S("dt_sb", [64, MACRO], BF16)
    x2_sb = S("x2_sb", [128, 2 * DM], F16)
    stage_bf = S("stage_bf", [128, 2 * DM], BF16)
    stage_f32 = S("stage_f32", [128, 2 * DM], F32)
    fs = [S(f"fscr{i}", [128, DM], F32) for i in range(4)]
    y2r_sb = S("y2r_sb", [128, 2 * NC8 * 128], BF16)
    hact_sb = u2_sb[:, 0:HK * 512]
    wd_sb = wq_sb.bitcast(BF16)[:, 0:HK * 512]
    x2r_sb = x2_sb
    cst_sb = S("cst_sb", [128, 2], F32)
    st_acc = S("st_acc", [128, 4 * NSUB], F32)
    st_tmp = S("st_tmp", [128, 12 * NSUB], F32)

    P = lambda name, shape, dt: nc.alloc_psum_tensor(name, shape, dt).ap()
    psA = P("psA", [128, 1024], F32)
    psB = P("psB", [128, 1024], F32)
    psC = P("psC", [128, 1024], F32)
    psT = P("psT", [128, 1024], BF16)

    sch = Sched(nc)
    sch.SERIAL = serial
    O = sch.op

    def mark(ph):
        sch.MARK = ph in serial_phases or ph == "prep"
    war = {}

    def guard(name):
        return war.get(name)

    def setg(name, ev):
        war[name] = ev

    mark("prep")
    pf = u2_sb.bitcast(F32)[:, 0:3072]
    pb = u2_sb[:, 6144:8192]

    e1 = O("sync", lambda e: e.dma_start(out=fs[0][:, 0:128], in_=eye_i), dma=True)
    ev_eye = O("vector", lambda e: e.tensor_copy(out=eye_sb, in_=fs[0][:, 0:128]), after=[e1])
    for src, dst in ((conv_b, convb_sb), (dtb_i, dtb_sb), (ln_g, gcol_sb)):
        O("sync", lambda e, s=src, d=dst: e.dma_start(out=d, in_=s), dma=True)
    ev_cols = O("sync", lambda e: e.dma_start(out=fs[1][:, 0:NC8], in_=ln_b), dma=True)
    ev_bcol = O("vector", lambda e: e.tensor_copy(out=bcol_sb, in_=fs[1][:, 0:NC8]), after=[ev_cols])
    e1 = O("sync", lambda e: e.dma_start(out=a_sb, in_=alog), dma=True)
    e2 = O("scalar", lambda e: e.activation(out=a_sb, in_=a_sb, func=AF.Exp), after=[e1])
    ev_a = O("vector", lambda e: e.tensor_single_scalar(out=a_sb, in_=a_sb, scalar=-1.0, op=ALU.mult), after=[e2])
    ev_ffc = []
    for src, dst in ((wgT, ffng_bf), (wuT, ffnu_bf), (wdT, ffnd_bf)):
        ev_ffc.append(O("gpsimd", lambda e, s=src, d=dst: e.dma_start(out=d, in_=s), dma=True))
    ev_convd = None
    for jh in range(2):
        e1 = O("sync", lambda e, jh=jh: e.dma_start(
            out=pf[:, 0:2 * NC8 * 128].rearrange("p (j c n) -> p j c n", j=2, c=NC8),
            in_=conv_diag[2 * jh:2 * jh + 2].rearrange("j c p n -> p j c n")),
            after=[ev_convd], dma=True)
        ev_convd = O("vector", lambda e, jh=jh: e.tensor_copy(
            out=convd_sb[:, jh * 2 * NC8 * 128:(jh + 1) * 2 * NC8 * 128],
            in_=pf[:, 0:2 * NC8 * 128]), after=[e1])
    e1 = O("sync", lambda e: e.dma_start(
        out=pf[:, 0:NC8 * 128].rearrange("p (c n) -> p c n", c=NC8),
        in_=d_diag.rearrange("c p n -> p c n")), after=[ev_convd], dma=True)
    ev_dd = O("vector", lambda e: e.tensor_copy(out=ddiag_sb, in_=pf[:, 0:NC8 * 128]), after=[e1])
    e1 = O("sync", lambda e: e.dma_start(
        out=pf[:, 0:NC8 * 96].rearrange("p (c n) -> p c n", c=NC8),
        in_=xprojw.rearrange("(c p) n -> p c n", p=128)), after=[ev_dd], dma=True)
    ev_xpw = O("vector", lambda e: e.tensor_copy(out=xprojw_sb, in_=pf[:, 0:NC8 * 96]), after=[e1])
    e1 = O("sync", lambda e: e.dma_start(out=pf[0:64, 0:DH], in_=dtw_i), after=[ev_xpw], dma=True)
    ev_dtw = O("vector", lambda e: e.tensor_copy(out=dtw_sb, in_=pf[0:64, 0:DH]), after=[e1])

    ev_bw_mm = None
    ev_wg_st = {}
    ev_eg_prev = None
    for k in range(NC8):
        el = O("sync", lambda e, k=k: e.dma_start(out=pf[:, 0:2 * DH], in_=w_in[k * 128:(k + 1) * 128, :]),
               after=[ev_dtw, ev_eg_prev, ev_bw_mm], dma=True)
        er = O("vector", lambda e: e.tensor_copy(out=pb, in_=pf[:, 0:2 * DH]), after=[el])
        eg = O("vector", lambda e, k=k: e.tensor_scalar(
            out=stage_bf[:, 0:2 * DH], in0=pf[:, 0:2 * DH],
            scalar1=gcol_sb[:, k:k + 1], scalar2=None, op0=ALU.mult),
            after=[el, ev_wg_st.get(k - 1)])
        ev_eg_prev = eg
        ev_wg_st[k] = O("sync", lambda e, k=k: e.dma_start(
            out=wg_bf[k * 128:(k + 1) * 128, :], in_=stage_bf[:, 0:2 * DH]), after=[eg], dma=True)
        for fc in range(4):
            pdst = psA if fc < 2 else psC
            ev_bw_mm = O("tensor", lambda e, k=k, fc=fc, pdst=pdst: e.matmul(
                pdst[0:1, (fc % 2) * 512:(fc % 2 + 1) * 512], bcol_sb[:, k:k + 1], pb[:, fc * 512:(fc + 1) * 512],
                start=(k == 0), stop=(k == NC8 - 1)), after=[er, ev_bcol])
    e_bev = O("scalar", lambda e: e.copy(out=stage_f32[0:1, 0:1024], in_=psA[0:1, 0:1024]), after=[ev_bw_mm])
    e_bev = O("scalar", lambda e: e.copy(out=stage_f32[0:1, 1024:2048], in_=psC[0:1, 0:1024]), after=[ev_bw_mm, e_bev])
    setg("psA", e_bev)
    setg("psC", e_bev)
    e_bst = O("sync", lambda e: e.dma_start(out=bwrow.rearrange("(a b) -> a b", a=2)[0:1], in_=stage_f32[0:1, 0:1024]), after=[e_bev], dma=True)
    e_bst = O("sync", lambda e: e.dma_start(out=bwrow.rearrange("(a b) -> a b", a=2)[1:2], in_=stage_f32[0:1, 1024:2048]), after=[e_bev, e_bst], dma=True)
    ev_bwt = e_bst
    for oc in range(16):
        ev_bwt = O("sync", lambda e, oc=oc: e.dma_start(
            out=bwt_sb[:, oc:oc + 1], in_=bwrow.rearrange("(oc j) -> oc j", oc=16)[oc:oc + 1, :].rearrange("a j -> (a j) a" if False else "a j -> j a")),
            after=[e_bst], dma=True)

    prev_st = ev_wg_st[NC8 - 1]
    for k in range(NC8):
        el = O("sync", lambda e, k=k: e.dma_start(out=pf[:, 0:DM], in_=outw[k * 128:(k + 1) * 128, :]),
               after=[prev_st, ev_bw_mm], dma=True)
        ec = O("vector", lambda e: e.tensor_copy(out=stage_bf[:, 0:DM], in_=pf[:, 0:DM]), after=[el])
        prev_st = O("sync", lambda e, k=k: e.dma_start(
            out=outw_bf[k * 128:(k + 1) * 128, :], in_=stage_bf[:, 0:DM]), after=[ec], dma=True)
    ev_ce = O("vector", lambda e: e.memset(cst_sb[:, 0:1], EPS))
    ev_c1 = O("vector", lambda e: e.memset(cst_sb[:, 1:2], 1.0))
    setg("fs0", ev_eye)
    setg("fs1", ev_bcol)
    setg("stagebf0", prev_st)
    setg("stagebf1", prev_st)
    setg("stagef0", ev_bwt)
    setg("stagef1", ev_bwt)
    ev_z0 = O("vector", lambda e: e.memset(hst_sb, 0.0))
    ev_z1 = O("vector", lambda e: e.memset(uhalo_sb, 0.0))
    prep_done = prev_st
    setg("u2", prev_st)

    ev_scan_done = None
    ev_sz = {}
    ev_y2wr = {}
    ev_y2rd = None
    ev_hsave = {}
    last_store = None
    ev_xnt_free = None
    ev_stacc_free = None
    ev_stacc2_free = None

    for m in range(NMAC):
        T0 = m * MACRO
        gbase = m * NSUB

        mark("p1")
        ev_acc = []
        for i in range(NSUB):
            xs = i % 2
            e_l = O("sync", lambda e, i=i, xs=xs, T0=T0: e.dma_start(
                out=fs[xs], in_=x_seq[T0 + i * SUB:T0 + (i + 1) * SUB, :]),
                after=[guard(f"fs{xs}"), prep_done if m == 0 else None], dma=True, stream="xload")
            e0 = O("scalar", lambda e, i=i, xs=xs: e.activation(
                out=fs[2], in_=fs[xs], func=AF.Copy, accum_out=st_acc[:, 4 * i:4 * i + 1]),
                after=[e_l, ev_stacc_free, guard("fs2")])
            e1 = O("scalar", lambda e, i=i, xs=xs: e.activation(
                out=fs[2], in_=fs[xs], func=AF.Square, accum_out=st_acc[:, 4 * i + 1:4 * i + 2]),
                after=[e0])
            ev_acc.append(e1)
            setg(f"fs{xs}", e1)
            setg("fs2", e1)
        sa = st_acc.rearrange("p (i k) -> p k i", k=4)
        nm_c = negm_all[:, gbase:gbase + NSUB]
        r1_c = r1_all[:, gbase:gbase + NSUB]
        s_c = s_all[:, gbase:gbase + NSUB]
        t0 = st_tmp[:, 0:NSUB]
        t1 = st_tmp[:, NSUB:2 * NSUB]
        t2 = st_tmp[:, 2 * NSUB:3 * NSUB]
        e1 = O("vector", lambda e: e.tensor_single_scalar(out=nm_c, in_=sa[:, 0], scalar=-1.0 / DM, op=ALU.mult), after=[ev_acc[-1]])
        tmq = st_tmp[:, 7 * NSUB:8 * NSUB]
        e2 = O("vector", lambda e: e.tensor_mul(out=tmq, in0=nm_c, in1=nm_c), after=[e1, guard("st7")])
        setg("st7", e3 if False else None)
        e3 = O("vector", lambda e: e.scalar_tensor_tensor(out=t0, in0=sa[:, 1], scalar=1.0 / DM, in1=tmq, op0=ALU.mult, op1=ALU.subtract), after=[e2])
        e4 = O("scalar", lambda e: e.activation(out=t1, in_=t0, func=AF.Ln, bias=cst_sb[:, 0:1]), after=[e3])
        e5 = O("scalar", lambda e: e.activation(out=t1, in_=t1, func=AF.Exp, scale=-0.5), after=[e4])
        e6 = O("vector", lambda e: e.tensor_mul(out=t2, in0=t1, in1=t1), after=[e5])
        e7 = O("vector", lambda e: e.tensor_mul(out=t2, in0=t2, in1=t0), after=[e6])
        e8 = O("scalar", lambda e: e.activation(out=t2, in_=t2, func=AF.Ln, bias=cst_sb[:, 0:1]), after=[e7])
        e9 = O("scalar", lambda e: e.activation(out=t2, in_=t2, func=AF.Exp, scale=-0.5), after=[e8])
        e10 = O("vector", lambda e: e.tensor_copy(out=r1_c, in_=t1), after=[e5])
        ev_stats = O("vector", lambda e: e.tensor_mul(out=s_c, in0=t1, in1=t2), after=[e9, e10])
        ev_stacc_free = ev_stats
        if debug and m == 0:
            O("gpsimd", lambda e: e.dma_start(out=dbg["dbg_can2"][:, 0:NSUB], in_=negm_all[:, 0:NSUB]),
              after=[ev_stats], dma=True)

        xnt3 = xnt_sb.rearrange("p (k t) -> p k t", k=NC8)
        psT3 = psT.rearrange("p (k t) -> p k t", k=NC8)
        ev_tr = []
        for i in range(NSUB):
            xs = i % 2
            sl = i % 2
            e_xl2 = O("sync", lambda e, i=i, xs=xs, T0=T0: e.dma_start(
                out=fs[xs], in_=x_seq[T0 + i * SUB:T0 + (i + 1) * SUB, :]),
                after=[guard(f"fs{xs}")], dma=True, stream="xload")
            e_xn = O("vector", lambda e, i=i, xs=xs, sl=sl: e.tensor_scalar(
                out=stage_bf[:, sl * DM:(sl + 1) * DM], in0=fs[xs],
                scalar1=nm_c[:, i:i + 1], scalar2=s_c[:, i:i + 1], op0=ALU.add, op1=ALU.mult),
                after=[e_xl2, ev_stats, guard(f"stagebf{sl}")])
            setg(f"fs{xs}", e_xn)
            tp = None
            for k in range(NC8):
                tp = O("tensor", lambda e, k=k, sl=sl: e.transpose(
                    psT3[:, k], stage_bf[:, sl * DM + k * 128:sl * DM + (k + 1) * 128], eye_sb),
                    after=[e_xn, ev_eye, guard("psT")])
            e_ev = O("scalar", lambda e, i=i: e.copy(out=xnt3[:, :, i * SUB:(i + 1) * SUB], in_=psT3),
                     after=[tp, ev_xnt_free if i == 0 else None])
            setg("psT", e_ev)
            setg(f"stagebf{sl}", e_ev)
            ev_tr.append(e_ev)
        ev_xnt = ev_tr[-1]
        if debug and m == 0:
            O("gpsimd", lambda e: e.dma_start(out=dbg["dbg_can2"][:, NSUB:2 * NSUB], in_=negm_all[:, 0:NSUB]),
              after=[ev_xnt], dma=True)
            e_st8 = O("vector", lambda e: e.tensor_copy(out=fs[2][:, 0:3 * NSUB], in_=st_tmp[:, 0:3 * NSUB]), after=[ev_stats, guard("fs2")])
            e_st9 = O("vector", lambda e: e.tensor_copy(out=fs[2][:, 3 * NSUB:3 * NSUB + NSUB], in_=nm_c), after=[ev_stats, e_st8])
            e_st10 = O("vector", lambda e: e.tensor_copy(out=fs[2][:, 4 * NSUB:4 * NSUB + NSUB], in_=s_c), after=[ev_stats, e_st9])
            e_stD = O("gpsimd", lambda e: e.dma_start(out=dbg["dbg_stats"], in_=fs[2][:, 0:5 * NSUB]), after=[e_st10], dma=True)
            setg("fs2", e_stD)
            e_xd = O("vector", lambda e: e.tensor_copy(out=fs[3], in_=stage_bf[:, 0:DM]), after=[ev_tr[0], guard("fs3")])
            e_xdD = O("gpsimd", lambda e: e.dma_start(out=dbg["dbg_xn"], in_=fs[3]), after=[e_xd], dma=True)
            setg("fs3", e_xdD)

        mark("p2")
        wg3 = wgoc_sb.rearrange("p (s k n) -> p s k n", s=2, k=NC8)
        ev_u = {}
        ev_z = {}
        ev_u2 = {}
        for oc in range(16):
            sl = oc % 2
            ps = psA if sl == 0 else psC
            pname = "psA" if sl == 0 else "psC"
            e_w = O("sync", lambda e, oc=oc, sl=sl: e.dma_start(
                out=wg3[:, sl], in_=wg_bf[:, oc * 128:(oc + 1) * 128].rearrange("(k p) n -> p k n", p=128)),
                after=[guard(f"wgoc{sl}"), prep_done if m == 0 else None], dma=True, stream="wg")
            mm = None
            for k in range(NC8):
                for hf in range(2):
                    mm = O("tensor", lambda e, k=k, hf=hf, ps=ps, sl=sl: e.matmul(
                        ps[:, hf * 512:(hf + 1) * 512], wg3[:, sl, k],
                        xnt3[:, k, hf * 512:(hf + 1) * 512],
                        start=(k == 0), stop=(k == NC8 - 1)),
                        after=[e_w, ev_xnt, guard(pname)])
            setg(f"wgoc{sl}", mm)
            if oc < NC8:
                c = oc
                us = c % 2
                upv = upre_sb.rearrange("p (s t) -> p s t", s=2)[:, us]
                e_h = O("vector", lambda e, c=c, upv=upv: e.tensor_copy(
                    out=upv[:, 0:3], in_=uhalo_sb[:, c * 3:c * 3 + 3]),
                    after=[guard(f"upre{us}"), guard(f"uhalo{c}"), ev_z1 if m == 0 else None])
                e_u = O("scalar", lambda e, c=c, ps=ps, upv=upv: e.activation(
                    out=upv[:, 3:3 + MACRO], in_=ps, func=AF.Identity, bias=bwt_sb[:, c:c + 1]),
                    after=[mm, e_h, ev_bwt])
                e_h2 = O("vector", lambda e, c=c, upv=upv: e.tensor_copy(
                    out=uhalo_sb[:, c * 3:c * 3 + 3], in_=upv[:, MACRO:MACRO + 3]), after=[e_u])
                ev_u[c] = (e_u, e_h2)
                setg(f"uhalo{c}", e_h2)
                setg(pname, e_u)
                cmm = None
                for j in range(4):
                    for hf in range(2):
                        cmm = O("tensor", lambda e, j=j, c=c, hf=hf, ps=ps, upv=upv: e.matmul(
                            ps[:, hf * 512:(hf + 1) * 512],
                            convd_sb[:, (j * NC8 + c) * 128:(j * NC8 + c + 1) * 128],
                            upv[:, j + hf * 512:j + hf * 512 + 512],
                            start=(j == 0), stop=(j == 3)),
                            after=[e_u, ev_convd, guard(pname)])
                e_cs = O("scalar", lambda e, c=c, ps=ps: e.activation(
                    out=u2_sb[:, c * MACRO:(c + 1) * MACRO], in_=ps, func=AF.Silu,
                    bias=convb_sb[:, c:c + 1]), after=[cmm, guard("u2")])
                ev_u2[c] = e_cs
                setg(pname, e_cs)
                setg(f"upre{us}", cmm)
                if debug and m == 0 and c == 0:
                    O("gpsimd", lambda e: e.dma_start(out=dbg["dbg_can2"][:, 2 * NSUB:3 * NSUB], in_=negm_all[:, 0:NSUB]),
                      after=[ev_u[0][1]], dma=True)
                    e_upd = O("vector", lambda e, upv=upv: e.tensor_copy(out=fs[3], in_=upv[:, 3:3 + MACRO]),
                              after=[e_u, guard("fs3")])
                    e_upD = O("gpsimd", lambda e: e.dma_start(out=dbg["dbg_upre"], in_=fs[3][:, 0:MACRO]), after=[e_upd], dma=True)
                    setg("fs3", e_upD)
            else:
                c = oc - NC8
                zsl = oc % 2
                e_zv = O("scalar", lambda e, c=c, ps=ps, zsl=zsl: e.activation(
                    out=stage_bf[:, zsl * DM:zsl * DM + MACRO], in_=ps, func=AF.Silu,
                    bias=bwt_sb[:, NC8 + c:NC8 + c + 1]),
                    after=[mm, guard(f"stagebf{zsl}"), ev_bwt])
                e_zs = O("scalar", lambda e, c=c, zsl=zsl: e.dma_start(
                    out=sz_sp[c], in_=stage_bf[:, zsl * DM:zsl * DM + MACRO]),
                    after=[e_zv, ev_sz.get((m - 1, c))], dma=True, stream="spill")
                ev_z[c] = e_zs
                setg(f"stagebf{zsl}", e_zs)
                setg(pname, e_zv)

        if debug and m == 0:
            O("gpsimd", lambda e: e.dma_start(out=dbg["dbg_can"][:, 0:NSUB], in_=negm_all[:, 0:NSUB]),
              after=[ev_u2[NC8 - 1]], dma=True)
        if debug and m == 0:
            e_c = O("vector", lambda e: e.tensor_copy(out=fs[2], in_=u2_sb[:, 0:MACRO]), after=[ev_u2[0], guard("fs2")])
            e_cD = O("gpsimd", lambda e: e.dma_start(out=dbg["dbg_u2"], in_=fs[2][:, 0:MACRO]), after=[e_c], dma=True)
            setg("fs2", e_cD)
        mark("p4")
        mm = None
        for k in range(NC8):
            for hf in range(2):
                mm = O("tensor", lambda e, k=k, hf=hf: e.matmul(
                    psB[0:96, hf * 512:hf * 512 + 512],
                    xprojw_sb.rearrange("p (c n) -> p c n", c=NC8)[:, k],
                    u2_sb[:, k * MACRO + hf * 512:k * MACRO + (hf + 1) * 512],
                    start=(k == 0), stop=(k == NC8 - 1)),
                    after=[ev_u2[k], ev_xpw, guard("psB")])
        e_ev = O("scalar", lambda e: e.copy(out=stage_f32[0:96, 0:MACRO], in_=psB[0:96, 0:MACRO]),
                 after=[mm, guard("stagef0")])
        setg("psB", e_ev)
        e_st = O("gpsimd", lambda e: e.dma_start(out=cc_dbc_in, in_=stage_f32[0:96, 0:MACRO]),
                 after=[e_ev, guard("cc_dbc_in"), ev_ffc[-1] if m == 0 else None], dma=True, stream="gst96")
        setg("stagef0", e_st)
        e_cc = O("gpsimd", lambda e: e.collective_compute(
            "AllReduce", ALU.add, replica_groups=RG,
            ins=[cc_dbc_in.opt()], outs=[cc_dbc_out.opt()]),
            after=[e_st, guard("cc_dbc_out")], coll=True, stream="coll")
        setg("cc_dbc_in", e_cc)
        e_dt = O("gpsimd", lambda e: e.dma_start(out=dt_sb, in_=cc_dbc_out[0:64, :]),
                 after=[e_cc, guard("dt_sb")], dma=True, stream="dtld")
        e_bcr = O("gpsimd", lambda e: e.dma_start(out=bcrows, in_=cc_dbc_out[64:96, :]),
                  after=[e_cc, guard("bcrows")], dma=True, stream="bcr")
        setg("cc_dbc_out", e_bcr)
        if debug and m == 0:
            O("gpsimd", lambda e: e.dma_start(out=dbg["dbg_dbc"], in_=cc_dbc_out), after=[e_cc], dma=True)

        if debug and m == 0:
            O("gpsimd", lambda e: e.dma_start(out=dbg["dbg_can"][:, NSUB:2 * NSUB], in_=negm_all[:, 0:NSUB]),
              after=[e_bcr], dma=True)
        mark("p5")
        ev_dl = {}
        for c in range(NC8):
            sl = c % 2
            ps = psA if sl == 0 else psC
            pname = "psA" if sl == 0 else "psC"
            mm = None
            for hf in range(2):
                mm = O("tensor", lambda e, c=c, hf=hf, ps=ps: e.matmul(
                    ps[:, hf * 512:(hf + 1) * 512], dtw_sb[:, c * 128:(c + 1) * 128],
                    dt_sb[:, hf * 512:(hf + 1) * 512], start=True, stop=True),
                    after=[e_dt, ev_dtw, guard(pname)])
            ea = O("scalar", lambda e, c=c, ps=ps: e.activation(
                out=fs[2], in_=ps, func=AF.Abs, bias=dtb_sb[:, c:c + 1]), after=[mm, guard("fs2")])
            eb = O("scalar", lambda e: e.activation(out=fs[2], in_=fs[2], func=AF.Exp, scale=-1.0), after=[ea])
            ec = O("scalar", lambda e: e.activation(out=fs[2], in_=fs[2], func=AF.Ln, bias=cst_sb[:, 1:2]), after=[eb])
            ed = O("scalar", lambda e, c=c, ps=ps: e.activation(
                out=fs[3], in_=ps, func=AF.Relu, bias=dtb_sb[:, c:c + 1]), after=[ec, guard("fs3")])
            ev = O("vector", lambda e, c=c: e.tensor_add(
                out=delta_sb[:, c * MACRO:(c + 1) * MACRO], in0=fs[2], in1=fs[3]),
                after=[ed, guard("delta")])
            ev_dl[c] = ev
            setg("fs2", ev)
            setg("fs3", ev)
            setg(pname, ed)
        if debug and m == 0:
            e_c = O("vector", lambda e: e.tensor_copy(out=fs[2], in_=delta_sb[:, 0:MACRO]), after=[ev_dl[0], guard("fs2")])
            e_cD2 = O("gpsimd", lambda e: e.dma_start(out=dbg["dbg_delta"], in_=fs[2][:, 0:MACRO]), after=[e_c], dma=True)
            setg("fs2", e_cD2)

        mark("scan")
        ev_w = {}
        for c in range(NC8):
            ev_w[c] = O("vector", lambda e, c=c: e.tensor_mul(
                out=wq_sb[:, c * MACRO:(c + 1) * MACRO],
                in0=delta_sb[:, c * MACRO:(c + 1) * MACRO],
                in1=u2_sb[:, c * MACRO:(c + 1) * MACRO]),
                after=[ev_dl[c], ev_u2[c], guard("wq")])
        prev_tts = None
        jidx = 0
        ev_yq = {}
        ev_qevac = {}
        for nq in range(4):
            ev_bcq = {}
            for j2 in range(4):
                nn = nq * 4 + j2
                eb_ = O("sync", lambda e, nn=nn, j2=j2: e.dma_start(
                    out=bbc_sb[:, j2 * MACRO:(j2 + 1) * MACRO],
                    in_=bcrows[nn:nn + 1, :].broadcast_to([128, MACRO])),
                    after=[e_bcr, guard(f"bbc{j2}")], dma=True, stream="bc")
                ec_ = O("sync", lambda e, nn=nn, j2=j2: e.dma_start(
                    out=cbc_sb[:, j2 * MACRO:(j2 + 1) * MACRO],
                    in_=bcrows[16 + nn:17 + nn, :].broadcast_to([128, MACRO])),
                    after=[e_bcr, guard(f"cbc{j2}")], dma=True, stream="bc")
                ev_bcq[j2] = (eb_, ec_)
            for c in range(NC8):
                ya = None
                for jn in range(4):
                    n = nq * 4 + jn
                    da = jidx % 2
                    jidx += 1
                    e_exp = O("scalar", lambda e, c=c, n=n, da=da: e.activation(
                        out=dA_sb[:, da * MACRO:(da + 1) * MACRO],
                        in_=delta_sb[:, c * MACRO:(c + 1) * MACRO], func=AF.Exp,
                        scale=a_sb[:, c * N16 + n:c * N16 + n + 1]),
                        after=[ev_dl[c], ev_a, guard(f"dA{da}")])
                    e_dbu = O("vector", lambda e, c=c, jn=jn: e.tensor_mul(
                        out=dBu_sb, in0=wq_sb[:, c * MACRO:(c + 1) * MACRO],
                        in1=bbc_sb[:, jn * MACRO:(jn + 1) * MACRO]),
                        after=[ev_w[c], ev_bcq[jn][0], guard("dBu")])
                    e_tts = O("vector", lambda e, c=c, n=n, jn=jn, da=da: e.tensor_tensor_scan(
                        out=h8_sb[:, jn * MACRO:(jn + 1) * MACRO],
                        data0=dA_sb[:, da * MACRO:(da + 1) * MACRO], data1=dBu_sb,
                        initial=hst_sb[:, c * N16 + n:c * N16 + n + 1], op0=ALU.mult, op1=ALU.add),
                        after=[e_exp, e_dbu, guard(f"h8_{jn}"),
                               ev_z0 if m == 0 else None,
                               ev_hsave.get((c, nq))])
                    setg(f"dA{da}", e_tts)
                    setg("dBu", e_tts)
                    prev_tts = e_tts
                    gsl = jn % 2
                    e_ym = O("vector", lambda e, jn=jn, gsl=gsl: e.tensor_tensor(
                        out=g_sb[:, gsl * MACRO:(gsl + 1) * MACRO],
                        in0=h8_sb[:, jn * MACRO:(jn + 1) * MACRO],
                        in1=cbc_sb[:, jn * MACRO:(jn + 1) * MACRO], op=ALU.mult),
                        after=[e_tts, ev_bcq[jn][1], guard(f"g{gsl}")])
                    setg(f"h8_{jn}", e_ym)
                    for sj in range(2):
                        ya = O("tensor", lambda e, sj=sj, gsl=gsl, jn=jn: e.matmul(
                            psB[:, sj * 512:(sj + 1) * 512], eye_sb,
                            g_sb[:, gsl * MACRO + sj * 512:gsl * MACRO + (sj + 1) * 512],
                            start=(jn == 0), stop=(jn == 3)),
                            after=[e_ym] + ([guard("psB")] if jn == 0 else []))
                    setg(f"g{gsl}", ya)
                    if c == NC8 - 1:
                        setg(f"bbc{jn}", e_dbu)
                        setg(f"cbc{jn}", e_ym)
                eh = O("vector", lambda e, c=c, nq=nq: e.tensor_copy(
                    out=hst_sb[:, c * N16 + nq * 4:c * N16 + nq * 4 + 4],
                    in_=h8_sb.rearrange("p (s t) -> p s t", s=4)[:, :, MACRO - 1]),
                    after=[prev_tts])
                ev_hsave[(c, nq)] = eh
                yv = ysb_sb[:, c * MACRO:(c + 1) * MACRO]
                if nq == 0:
                    eq = O("vector", lambda e, yv=yv: e.tensor_copy(out=yv, in_=psB[:, 0:MACRO]),
                           after=[ya, guard(f"ysb{c}")])
                else:
                    eq = O("vector", lambda e, yv=yv: e.tensor_add(out=yv, in0=yv, in1=psB[:, 0:MACRO]),
                           after=[ya, ev_qevac.get((nq - 1, c))])
                ev_qevac[(nq, c)] = eq
                setg("psB", eq)
        ev_yfin_c = {}
        for c in range(NC8):
            ya = None
            for sj in range(2):
                ya = O("tensor", lambda e, c=c, sj=sj: e.matmul(
                    psB[:, sj * 512:(sj + 1) * 512], ddiag_sb[:, c * 128:(c + 1) * 128],
                    u2_sb[:, c * MACRO + sj * 512:c * MACRO + (sj + 1) * 512],
                    start=True, stop=True),
                    after=[ev_u2[c], ev_dd, guard("psB")])
            e_ud = O("vector", lambda e, c=c: e.tensor_add(
                out=ysb_sb[:, c * MACRO:(c + 1) * MACRO],
                in0=ysb_sb[:, c * MACRO:(c + 1) * MACRO], in1=psB[:, 0:MACRO]),
                after=[ya, ev_qevac[(3, c)]])
            setg("psB", e_ud)
            e_szl = O("sync", lambda e, c=c: e.dma_start(
                out=stage_bf[:, 0:MACRO], in_=sz_sp[c]),
                after=[ev_z[c], guard("stagebf0")], dma=True, stream="szl")
            e_fin = O("vector", lambda e, c=c: e.tensor_tensor(
                out=stage_bf[:, DM:DM + MACRO], in0=ysb_sb[:, c * MACRO:(c + 1) * MACRO],
                in1=stage_bf[:, 0:MACRO], op=ALU.mult),
                after=[e_ud, e_szl, guard("stagebf1")])
            setg(f"ysb{c}", e_fin)
            setg("stagebf0", e_fin)
            e_y2s = O("scalar", lambda e, c=c: e.dma_start(out=y2_sp[c], in_=stage_bf[:, DM:DM + MACRO]),
                      after=[e_fin, ev_y2rd], dma=True, stream="spill")
            ev_y2wr[c] = e_y2s
            ev_sz[(m, c)] = e_szl
            setg("stagebf1", e_y2s)
            ev_yfin_c[c] = e_y2s
            if debug and m == 0 and c == 0:
                e_cp = O("vector", lambda e: e.tensor_copy(out=fs[2], in_=stage_bf[:, DM:DM + MACRO]),
                         after=[e_fin, guard("fs2")])
                e_cpD = O("gpsimd", lambda e: e.dma_start(out=dbg["dbg_y2"], in_=fs[2][:, 0:MACRO]), after=[e_cp], dma=True)
                setg("fs2", e_cpD)
        ev_scan_done = ev_yfin_c[NC8 - 1]
        setg("wq", ev_scan_done)

        if debug and m == 0:
            O("gpsimd", lambda e: e.dma_start(out=dbg["dbg_can"][:, 2 * NSUB:3 * NSUB], in_=negm_all[:, 0:NSUB]),
              after=[ev_scan_done], dma=True)
        mark("p7")
        ev_ow = {}
        for c in range(NC8):
            ev_ow[c] = O("sync", lambda e, c=c: e.dma_start(
                out=outw_sb[:, c * DM:(c + 1) * DM], in_=outw_bf[c * 128:(c + 1) * 128, :]),
                after=[guard("outw"), prep_done if m == 0 else None], dma=True, stream="ow")
        ev_opst = {}
        mm = None
        for ts8 in range(NSUB):
            sl = ts8 % 2
            ps = psA if sl == 0 else psC
            pname = "psA" if sl == 0 else "psC"
            ysl = ts8 % 2
            e_yl = O("sync", lambda e, ts8=ts8, ysl=ysl: e.dma_start(
                out=y2r_sb.rearrange("p (s c n) -> p s c n", s=2, c=NC8)[:, ysl],
                in_=y2_sp[:, :, ts8 * SUB:(ts8 + 1) * SUB].rearrange("c p n -> p c n")),
                after=[ev_scan_done, guard(f"y2r{ysl}")], dma=True, stream="y2l")
            for c in range(NC8):
                for hf in range(2):
                    mm = O("tensor", lambda e, c=c, hf=hf, ps=ps, ysl=ysl: e.matmul(
                        ps[:, hf * 512:(hf + 1) * 512],
                        y2r_sb.rearrange("p (s c n) -> p s c n", s=2, c=NC8)[:, ysl, c],
                        outw_sb[:, c * DM + hf * 512:c * DM + (hf + 1) * 512],
                        start=(c == 0), stop=(c == NC8 - 1)),
                        after=[e_yl, ev_ow[c], guard(pname)])
            setg(f"y2r{ysl}", mm)
            sfl = ts8 % 2
            e_ev = O("scalar", lambda e, ps=ps, sfl=sfl: e.copy(
                out=stage_f32[:, sfl * DM:(sfl + 1) * DM], in_=ps), after=[mm, guard(f"stagef{sfl}")])
            setg(pname, e_ev)
            e_os = O("gpsimd", lambda e, ts8=ts8, sfl=sfl: e.dma_start(
                out=cc_o_in[ts8 * SUB:(ts8 + 1) * SUB, :], in_=stage_f32[:, sfl * DM:(sfl + 1) * DM]),
                after=[e_ev, guard("cc_o_in")], dma=True, stream="gst")
            ev_opst[ts8] = e_os
            setg(f"stagef{sfl}", e_os)
        setg("outw", mm)
        ev_y2rd = e_yl
        e_cco = O("gpsimd", lambda e: e.collective_compute(
            "AllReduce", ALU.add, replica_groups=RG,
            ins=[cc_o_in.opt()], outs=[cc_o_out.opt()]),
            after=[ev_opst[NSUB - 1], guard("cc_o_out")], coll=True, stream="coll")
        setg("cc_o_in", e_cco)
        if debug and m == 0:
            O("gpsimd", lambda e: e.dma_start(out=dbg["dbg_o"], in_=cc_o_out), after=[e_cco], dma=True)

        if debug and m == 0:
            O("gpsimd", lambda e: e.dma_start(out=dbg["dbg_can"][:, 3 * NSUB:4 * NSUB], in_=negm_all[:, 0:NSUB]),
              after=[e_cco], dma=True)
        mark("p8")
        ev_x2s = {}
        for i in range(NSUB):
            gi = gbase + i
            xs = i % 2
            e_xl = O("sync", lambda e, i=i, xs=xs, T0=T0: e.dma_start(
                out=fs[xs], in_=x_seq[T0 + i * SUB:T0 + (i + 1) * SUB, :]),
                after=[guard(f"fs{xs}")], dma=True, stream="xload")
            e_ol = O("sync", lambda e, i=i: e.dma_start(
                out=fs[2], in_=cc_o_out[i * SUB:(i + 1) * SUB, :]),
                after=[e_cco, guard("fs2")], dma=True, stream="ol")
            p_a0 = O("scalar", lambda e, i=i, xs=xs: e.activation(
                out=fs[3], in_=fs[xs], func=AF.Copy, accum_out=st_acc[:, 4 * i:4 * i + 1]),
                after=[e_xl, guard("fs3"), guard("st_acc")])
            p_a1 = O("scalar", lambda e, i=i, xs=xs: e.activation(
                out=fs[3], in_=fs[xs], func=AF.Square, accum_out=st_acc[:, 4 * i + 1:4 * i + 2]), after=[p_a0])
            pm = st_tmp[:, 8 * NSUB + i:8 * NSUB + i + 1]
            pv = st_tmp[:, 9 * NSUB + i:9 * NSUB + i + 1]
            pq = st_tmp[:, 10 * NSUB + i:10 * NSUB + i + 1]
            p_m = O("vector", lambda e, i=i, pm=pm: e.tensor_single_scalar(
                out=pm, in_=st_acc[:, 4 * i:4 * i + 1], scalar=-1.0 / DM, op=ALU.mult), after=[p_a1])
            p_v1 = O("vector", lambda e, pm=pm, pq=pq: e.tensor_mul(out=pq, in0=pm, in1=pm), after=[p_m])
            p_v2 = O("vector", lambda e, i=i, pv=pv, pq=pq: e.scalar_tensor_tensor(
                out=pv, in0=st_acc[:, 4 * i + 1:4 * i + 2], scalar=1.0 / DM, in1=pq,
                op0=ALU.mult, op1=ALU.subtract), after=[p_v1])
            p_v3 = O("scalar", lambda e, pv=pv: e.activation(out=pv, in_=pv, func=AF.Ln, bias=cst_sb[:, 0:1]), after=[p_v2])
            p_v4 = O("scalar", lambda e, pv=pv: e.activation(out=pv, in_=pv, func=AF.Exp, scale=-0.5), after=[p_v3])
            p_nr = O("vector", lambda e, pm=pm, pv=pv, pq=pq: e.tensor_mul(out=pq, in0=pm, in1=pv), after=[p_v4])
            e_o1 = O("scalar", lambda e, xs=xs, pv=pv, pq=pq: e.activation(
                out=fs[3], in_=fs[xs], func=AF.Identity, scale=pv, bias=pq), after=[p_nr, guard("fs3")])
            setg("st_acc", p_a1)
            if debug and m == 0 and i == 0:
                e_xlnD = O("gpsimd", lambda e: e.dma_start(out=dbg["dbg_xln1"], in_=fs[3]), after=[e_o1], dma=True)
                e_nmD = O("gpsimd", lambda e: e.dma_start(out=dbg["dbg_nm"][:, 0:NSUB], in_=negm_all[:, 0:NSUB]), after=[e_o1], dma=True)
                e_r1D = O("gpsimd", lambda e: e.dma_start(out=dbg["dbg_nm"][:, NSUB:2 * NSUB], in_=r1_all[:, 0:NSUB]), after=[e_o1], dma=True)
            e_o2 = O("vector", lambda e: e.tensor_add(out=fs[2], in0=fs[2], in1=fs[3]),
                     after=[e_ol, e_o1] + ([e_xlnD] if debug and m == 0 and i == 0 else []))
            setg("fs3", e_o2)
            e_a0 = O("scalar", lambda e, i=i: e.activation(
                out=fs[3], in_=fs[2], func=AF.Copy, accum_out=st_acc[:, 4 * i + 2:4 * i + 3]),
                after=[e_o2, ev_stacc2_free, guard("fs3")])
            e_a1 = O("scalar", lambda e, i=i: e.activation(
                out=fs[3], in_=fs[2], func=AF.Square, accum_out=st_acc[:, 4 * i + 3:4 * i + 4]), after=[e_a0])
            setg("fs3", e_a1)
            tm = st_tmp[:, 3 * NSUB + i:3 * NSUB + i + 1]
            tv = st_tmp[:, 4 * NSUB + i:4 * NSUB + i + 1]
            e_m = O("vector", lambda e, i=i, tm=tm: e.tensor_single_scalar(
                out=tm, in_=st_acc[:, 4 * i + 2:4 * i + 3], scalar=-1.0 / DM, op=ALU.mult), after=[e_a1])
            tq = st_tmp[:, 7 * NSUB + i:7 * NSUB + i + 1]
            e_v1 = O("vector", lambda e, tm=tm, tq=tq: e.tensor_mul(out=tq, in0=tm, in1=tm), after=[e_m])
            e_v2 = O("vector", lambda e, i=i, tv=tv, tq=tq: e.scalar_tensor_tensor(
                out=tv, in0=st_acc[:, 4 * i + 3:4 * i + 4], scalar=1.0 / DM, in1=tq,
                op0=ALU.mult, op1=ALU.subtract), after=[e_v1])
            e_v3 = O("scalar", lambda e, tv=tv: e.activation(out=tv, in_=tv, func=AF.Ln, bias=cst_sb[:, 0:1]), after=[e_v2])
            e_v4 = O("scalar", lambda e, tv=tv: e.activation(out=tv, in_=tv, func=AF.Exp, scale=-0.5), after=[e_v3])
            x2s = i % 2
            nr2 = st_tmp[:, 7 * NSUB + i:7 * NSUB + i + 1]
            e_nr2 = O("vector", lambda e, tm=tm, tv=tv, nr2=nr2: e.tensor_mul(out=nr2, in0=tm, in1=tv),
                      after=[e_v4, guard("st7")])
            e_x1 = O("scalar", lambda e, tv=tv, nr2=nr2: e.activation(
                out=fs[2], in_=fs[2], func=AF.Identity, scale=tv, bias=nr2), after=[e_nr2])
            setg("st7", e_x1)
            e_x2 = O("vector", lambda e, xs=xs, x2s=x2s: e.tensor_add(
                out=x2_sb[:, x2s * DM:(x2s + 1) * DM], in0=fs[2], in1=fs[xs]),
                after=[e_x1, guard(f"x2_{x2s}")])
            setg("fs2", e_x2)
            setg(f"fs{xs}", e_x2)
            e_x2sp = O("scalar", lambda e, i=i, x2s=x2s: e.dma_start(
                out=x2_sp[i * SUB:(i + 1) * SUB, :], in_=x2_sb[:, x2s * DM:(x2s + 1) * DM]),
                after=[e_x2, guard("x2_sp")], dma=True, stream="spill")
            ev_x2s[i] = (e_x2, e_x2sp, x2s)
            if debug and m == 0 and i == 0:
                e_cp = O("vector", lambda e: e.tensor_copy(out=fs[3], in_=x2_sb[:, 0:DM]), after=[e_x2, guard("fs3")])
                e_cpD2 = O("gpsimd", lambda e: e.dma_start(out=dbg["dbg_x2"], in_=fs[3]), after=[e_cp], dma=True)
                setg("fs3", e_cpD2)
            x2v = x2_sb[:, x2s * DM:(x2s + 1) * DM]
            f_a0 = O("scalar", lambda e, i=i, x2v=x2v: e.activation(
                out=fs[3], in_=x2v, func=AF.Copy, accum_out=st_acc[:, 4 * i:4 * i + 1]),
                after=[e_x2, ev_stacc_free, guard("fs3")])
            f_a1 = O("scalar", lambda e, i=i, x2v=x2v: e.activation(
                out=fs[3], in_=x2v, func=AF.Square, accum_out=st_acc[:, 4 * i + 1:4 * i + 2]), after=[f_a0])
            setg("fs3", f_a1)
            tm3 = st_tmp[:, 5 * NSUB + i:5 * NSUB + i + 1]
            tv3 = st_tmp[:, 6 * NSUB + i:6 * NSUB + i + 1]
            f_m = O("vector", lambda e, i=i, tm3=tm3: e.tensor_single_scalar(
                out=tm3, in_=st_acc[:, 4 * i:4 * i + 1], scalar=-1.0 / DM, op=ALU.mult), after=[f_a1])
            tq3 = st_tmp[:, 7 * NSUB + i:7 * NSUB + i + 1]
            f_v1 = O("vector", lambda e, tm3=tm3, tq3=tq3: e.tensor_mul(out=tq3, in0=tm3, in1=tm3), after=[f_m])
            f_v2 = O("vector", lambda e, i=i, tv3=tv3, tq3=tq3: e.scalar_tensor_tensor(
                out=tv3, in0=st_acc[:, 4 * i + 1:4 * i + 2], scalar=1.0 / DM, in1=tq3,
                op0=ALU.mult, op1=ALU.subtract), after=[f_v1])
            f_v3 = O("scalar", lambda e, tv3=tv3: e.activation(out=tv3, in_=tv3, func=AF.Ln, bias=cst_sb[:, 0:1]), after=[f_v2])
            f_v4 = O("scalar", lambda e, tv3=tv3: e.activation(out=tv3, in_=tv3, func=AF.Exp, scale=-0.5), after=[f_v3])
            sl = i % 2
            f_xn = O("vector", lambda e, tm3=tm3, tv3=tv3, sl=sl, x2v=x2v: e.tensor_scalar(
                out=stage_bf[:, sl * DM:(sl + 1) * DM], in0=x2v, scalar1=tm3, scalar2=tv3,
                op0=ALU.add, op1=ALU.mult), after=[f_v4, guard(f"stagebf{sl}")])
            setg(f"x2_{x2s}", f_xn)
            tp = None
            for k in range(NC8):
                tp = O("tensor", lambda e, k=k, sl=sl: e.transpose(
                    psT3[:, k], stage_bf[:, sl * DM + k * 128:sl * DM + (k + 1) * 128], eye_sb),
                    after=[f_xn, guard("psT")])
            f_ev = O("scalar", lambda e, i=i: e.copy(out=xnt3[:, :, i * SUB:(i + 1) * SUB], in_=psT3),
                     after=[tp])
            setg("psT", f_ev)
            setg(f"stagebf{sl}", f_ev)
            ev_x2s[i] = (e_x2, e_x2sp, x2s)
            if i == NSUB - 1:
                ev_xnt3_done = f_ev
        setg("x2_sp", ev_x2s[NSUB - 1][1])
        ev_stacc_free = ev_xnt3_done

        ffg3 = ffg_sb.rearrange("p (s k n) -> p s k n", s=2, k=NC8)
        mark("p9")
        ffu3 = ffu_sb.rearrange("p (s k n) -> p s k n", s=2, k=NC8)
        ev_dn = {}
        mm_last_xnt = None
        for half in range(2):
            mark("p9a")
            hof = half * 512
            ev_hact = {}
            for hc in range(HK):
                sl = hc % 2
                e_wg = O("sync", lambda e, hc=hc, sl=sl: e.dma_start(
                    out=ffg3[:, sl], in_=ffng_bf[:, hc * 128:(hc + 1) * 128].rearrange("(k p) n -> p k n", p=128)),
                    after=[guard(f"ffg{sl}"), ev_ffc[0] if m == 0 else None], dma=True, stream="ffgu")
                e_wu = O("sync", lambda e, hc=hc, sl=sl: e.dma_start(
                    out=ffu3[:, sl], in_=ffnu_bf[:, hc * 128:(hc + 1) * 128].rearrange("(k p) n -> p k n", p=128)),
                    after=[guard(f"ffu{sl}"), ev_ffc[1] if m == 0 else None], dma=True, stream="ffgu")
                mmg = mmu = None
                for k in range(NC8):
                    mmg = O("tensor", lambda e, k=k, sl=sl, hof=hof: e.matmul(
                        psA[:, 0:512], ffg3[:, sl, k],
                        xnt3[:, k, hof:hof + 512], start=(k == 0), stop=(k == NC8 - 1)),
                        after=[e_wg, ev_xnt3_done, guard("psA")])
                for k in range(NC8):
                    mmu = O("tensor", lambda e, k=k, sl=sl, hof=hof: e.matmul(
                        psC[:, 0:512], ffu3[:, sl, k],
                        xnt3[:, k, hof:hof + 512], start=(k == 0), stop=(k == NC8 - 1)),
                        after=[e_wu, ev_xnt3_done, guard("psC")])
                setg(f"ffg{sl}", mmg)
                setg(f"ffu{sl}", mmu)
                mm_last_xnt = mmu
                e_sg = O("scalar", lambda e: e.activation(
                    out=stage_bf[:, 0:512], in_=psA[:, 0:512], func=AF.Silu, scale=1.0 / 127.0),
                    after=[mmg, guard("stagebf0")])
                setg("psA", e_sg)
                e_su = O("scalar", lambda e: e.activation(
                    out=stage_bf[:, DM:DM + 512], in_=psC[:, 0:512], func=AF.Copy, scale=1.0 / 127.0),
                    after=[mmu, guard("stagebf1")])
                setg("psC", e_su)
                e_h = O("vector", lambda e, hc=hc: e.tensor_mul(
                    out=hact_sb[:, hc * 512:(hc + 1) * 512],
                    in0=stage_bf[:, 0:512], in1=stage_bf[:, DM:DM + 512]),
                    after=[e_sg, e_su, guard("hact")])
                ev_hact[hc] = e_h
                setg("stagebf0", e_h)
                setg("stagebf1", e_h)
            mark("p9b")
            for dmh in range(2):
                ev_wd = {}
                for hk in range(HK):
                    ev_wd[hk] = O("sync", lambda e, hk=hk, dmh=dmh: e.dma_start(
                        out=wd_sb[:, hk * 512:(hk + 1) * 512],
                        in_=ffnd_bf[hk * 128:(hk + 1) * 128, dmh * 512:(dmh + 1) * 512]),
                        after=[guard("wq"), ev_ffc[2] if m == 0 else None,
                               guard(f"wd{hk}")], dma=True, stream="wd")
                for t4 in range(4):
                    ts8 = half * 4 + t4
                    sl = t4 % 2
                    ps = psA if sl == 0 else psC
                    pname = "psA" if sl == 0 else "psC"
                    mm = None
                    for hk in range(HK):
                        mm = O("tensor", lambda e, hk=hk, t4=t4, ps=ps: e.matmul(
                            ps[:, 0:512],
                            hact_sb[:, hk * 512 + t4 * SUB:hk * 512 + (t4 + 1) * SUB],
                            wd_sb[:, hk * 512:(hk + 1) * 512],
                            start=(hk == 0), stop=(hk == HK - 1)),
                            after=[ev_hact[hk], ev_wd[hk], guard(pname)])
                        if t4 == 3:
                            setg(f"wd{hk}", mm)
                    sfl = t4 % 2
                    stbf = stage_f32.bitcast(BF16)
                    e_ev = O("scalar", lambda e, ps=ps, sfl=sfl, stbf=stbf: e.activation(
                        out=stbf[:, sfl * 2 * DM:sfl * 2 * DM + 512], in_=ps[:, 0:512],
                        func=AF.Copy, scale=1.0 / 127.0),
                        after=[mm, guard(f"stagef{sfl}")])
                    setg(pname, e_ev)
                    e_fsd = O("gpsimd", lambda e, ts8=ts8, dmh=dmh, sfl=sfl, stbf=stbf: e.dma_start(
                        out=cc_f_in[ts8 * SUB:(ts8 + 1) * SUB, dmh * 512:(dmh + 1) * 512],
                        in_=stbf[:, sfl * 2 * DM:sfl * 2 * DM + 512]),
                        after=[e_ev, guard("cc_f_in")], dma=True, stream="gst")
                    ev_dn[(half, dmh, t4)] = e_fsd
                    setg(f"stagef{sfl}", e_fsd)
            setg("hact", ev_dn[(half, 1, 3)])
        ev_xnt_free = mm_last_xnt
        setg("wq", ev_dn[(1, 1, 3)])
        setg("u2", ev_dn[(1, 1, 3)])
        e_ccf = O("gpsimd", lambda e: e.collective_compute(
            "AllReduce", ALU.add, replica_groups=RG,
            ins=[cc_f_in.opt()], outs=[cc_f_out.opt()]),
            after=[ev_dn[(1, 1, 3)], guard("cc_f_out")], coll=True, stream="coll")
        setg("cc_f_in", e_ccf)

        mark("p10")
        for i in range(NSUB):
            e_ol = O("gpsimd", lambda e, i=i: e.dma_start(
                out=fs[2], in_=cc_f_out[i * SUB:(i + 1) * SUB, :]),
                after=[e_ccf, guard("fs2")], dma=True, stream="o2l")
            x2rs = i % 2
            e_xr = O("sync", lambda e, i=i, x2rs=x2rs: e.dma_start(
                out=x2r_sb[:, x2rs * DM:(x2rs + 1) * DM], in_=x2_sp[i * SUB:(i + 1) * SUB, :]),
                after=[ev_x2s[i][1], guard(f"x2r{x2rs}"), guard(f"x2_{x2rs}")], dma=True, stream="x2l")
            e_a0 = O("scalar", lambda e, i=i: e.activation(
                out=fs[3], in_=fs[2], func=AF.Copy, accum_out=st_acc[:, 4 * i + 2:4 * i + 3]),
                after=[e_ol, ev_stacc2_free, guard("fs3")])
            e_a1 = O("scalar", lambda e, i=i: e.activation(
                out=fs[3], in_=fs[2], func=AF.Square, accum_out=st_acc[:, 4 * i + 3:4 * i + 4]), after=[e_a0])
            setg("fs3", e_a1)
            tm = st_tmp[:, 3 * NSUB + i:3 * NSUB + i + 1]
            tv = st_tmp[:, 4 * NSUB + i:4 * NSUB + i + 1]
            e_m = O("vector", lambda e, i=i, tm=tm: e.tensor_single_scalar(
                out=tm, in_=st_acc[:, 4 * i + 2:4 * i + 3], scalar=-1.0 / DM, op=ALU.mult), after=[e_a1])
            tq = st_tmp[:, 7 * NSUB + i:7 * NSUB + i + 1]
            e_v1 = O("vector", lambda e, tm=tm, tq=tq: e.tensor_mul(out=tq, in0=tm, in1=tm), after=[e_m])
            e_v2 = O("vector", lambda e, i=i, tv=tv, tq=tq: e.scalar_tensor_tensor(
                out=tv, in0=st_acc[:, 4 * i + 3:4 * i + 4], scalar=1.0 / DM, in1=tq,
                op0=ALU.mult, op1=ALU.subtract), after=[e_v1])
            e_v3 = O("scalar", lambda e, tv=tv: e.activation(out=tv, in_=tv, func=AF.Ln, bias=cst_sb[:, 0:1]), after=[e_v2])
            e_v4 = O("scalar", lambda e, tv=tv: e.activation(out=tv, in_=tv, func=AF.Exp, scale=-0.5), after=[e_v3])
            nr3 = st_tmp[:, 7 * NSUB + i:7 * NSUB + i + 1]
            e_nr3 = O("vector", lambda e, tm=tm, tv=tv, nr3=nr3: e.tensor_mul(out=nr3, in0=tm, in1=tv),
                      after=[e_v4, guard("st7")])
            e_x1 = O("scalar", lambda e, tv=tv, nr3=nr3: e.activation(
                out=fs[2], in_=fs[2], func=AF.Identity, scale=tv, bias=nr3), after=[e_nr3])
            setg("st7", e_x1)
            e_x2f = O("vector", lambda e, x2rs=x2rs: e.tensor_add(
                out=fs[2], in0=fs[2], in1=x2r_sb[:, x2rs * DM:(x2rs + 1) * DM]), after=[e_x1, e_xr])
            setg(f"x2r{x2rs}", e_x2f)
            setg(f"x2_{x2rs}", e_x2f)
            last_store = O("scalar", lambda e, i=i, T0=T0: e.dma_start(
                out=y_out[T0 + i * SUB:T0 + (i + 1) * SUB, :], in_=fs[2]),
                after=[e_x2f, last_store], dma=True, stream="yout")
            setg("fs2", last_store)
            ev_stacc2_free = e_x2f
        setg("x2_sp", e_xr)
        setg("cc_f_out", e_ol)
        setg("cc_o_out", ev_x2s[NSUB - 1][0])

    sch.wait_only("scalar", [last_store])
    sch.wait_only("gpsimd", [last_store])

    import contextlib
    stack = contextlib.ExitStack()
    sems = {name: stack.enter_context(nc.semaphore(name)) for name in sch.cnt}
    with nc.Block() as block:
        sch.emit(block, sems)
    stack.close()
    n_ops = sum(len(v) for v in sch.ops.values())
    return nc, n_ops


_cache = {}


def kernel(**inputs):
    import os
    if "nc" not in _cache:
        _cache["nc"], _cache["nops"] = build_nc(
            debug=False, serial=os.environ.get("KSERIAL", "0") == "1",
            serial_phases=tuple(x for x in os.environ.get("KSPH", "p9b").split(",") if x))
    nc = _cache["nc"]
    in_maps = [host_prep(inputs, core) for core in range(8)]
    res = run_bass_kernel_spmd(nc, in_maps, core_ids=list(range(8)))
    out = np.empty((4, L, DM), np.float32)
    for p in range(4):
        out[p, :L // 2] = res.results[2 * p]["y_out"][:L // 2]
        out[p, L // 2:] = res.results[2 * p + 1]["y_out"][L // 2:]
    return out


# version 08
# speedup vs baseline: 1.1642x; 1.0195x over previous
import numpy as np
import concourse.bass as bass
import concourse.mybir as mybir
from concourse.bass_utils import run_bass_kernel_spmd

F32 = mybir.dt.float32
F16 = mybir.dt.float16
BF16 = mybir.dt.bfloat16
I32 = mybir.dt.int32
AF = mybir.ActivationFunctionType
ALU = mybir.AluOpType

L = 8192
DM = 1024
DIN = 2048
DH = 1024
NC8 = 8
N16 = 16
HH = 1536
HK = 12
MACRO = 1024
NMAC = L // MACRO
SUB = 128
NSUB = MACRO // SUB
EPS = 1e-5
RG = [[0, 1], [2, 3], [4, 5], [6, 7]]


class Sched:
    ENG = ("sync", "scalar", "vector", "tensor", "gpsimd")

    def __init__(self, nc):
        self.nc = nc
        self.ops = {e: [] for e in self.ENG}
        self.cnt = {}
        for e in self.ENG:
            self.cnt["e_" + e] = 0
            self.cnt["d_" + e] = 0

    SERIAL = False
    MARK = False

    def op(self, eng, emit, after=(), dma=False, coll=False, stream=None):
        if dma or coll:
            sem = "d_" + (stream or eng)
        else:
            sem = "e_" + eng
        inc = 16 if dma else 1
        if sem not in self.cnt:
            self.cnt[sem] = 0
        self.cnt[sem] += inc
        after = tuple(a for a in after if a is not None)
        if self.SERIAL or self.MARK:
            after = after + ((self.last_ev,) if getattr(self, "last_ev", None) else ())
        self.ops[eng].append((emit, after, sem, inc))
        ev = (sem, self.cnt[sem])
        self.last_ev = ev
        return ev

    def wait_only(self, eng, after):
        self.ops[eng].append((None, tuple(a for a in after if a is not None), None, 0))

    def emit(self, block, sems):
        for eng_name in self.ENG:
            ops = self.ops[eng_name]
            if not ops:
                continue

            def mk(eng_name, ops):
                own_e = "e_" + eng_name
                tight = eng_name in ("vector", "scalar", "gpsimd")

                def body(eng):
                    hi = {}
                    own_cnt = 0
                    for emit, after, sem, inc in ops:
                        for wsem, wval in after:
                            if wsem == own_e:
                                if not tight or own_cnt - wval > 4:
                                    continue
                            if hi.get(wsem, 0) >= wval:
                                continue
                            eng.wait_ge(sems[wsem], wval)
                            hi[wsem] = wval
                        if emit is not None:
                            ins = emit(eng)
                            if ins is not None and sem is not None:
                                ins.then_inc(sems[sem], inc)
                        if sem == own_e:
                            own_cnt += inc
                return body

            getattr(block, eng_name)(mk(eng_name, ops))


def host_prep(inputs, core):
    pair, role = core // 2, core % 2
    d0 = role * DH
    h0 = role * HH
    ip = {}
    ip["x_seq"] = np.ascontiguousarray(np.asarray(inputs["L0_attn_x"])[pair], dtype=np.float32)
    W = np.asarray(inputs["L0_attn_in_proj_w"])
    ip["w_in"] = np.ascontiguousarray(
        np.concatenate([W[:, d0:d0 + DH], W[:, DIN + d0:DIN + d0 + DH]], axis=1))
    cw = np.asarray(inputs["L0_attn_conv_w"])[d0:d0 + DH]
    idx = np.arange(128)
    diag = np.zeros((4, NC8, 128, 128), np.float32)
    for j in range(4):
        for c in range(NC8):
            diag[j, c, idx, idx] = cw[c * 128:(c + 1) * 128, j]
    ip["conv_diag"] = diag
    ip["conv_b"] = np.ascontiguousarray(
        np.asarray(inputs["L0_attn_conv_b"])[d0:d0 + DH].reshape(NC8, 128).T)
    ip["xprojw"] = np.ascontiguousarray(np.asarray(inputs["L0_attn_x_proj_w"])[d0:d0 + DH])
    ip["dtw"] = np.ascontiguousarray(np.asarray(inputs["L0_attn_dt_proj_w"])[:, d0:d0 + DH])
    ip["dtb"] = np.ascontiguousarray(
        np.asarray(inputs["L0_attn_dt_proj_b"])[d0:d0 + DH].reshape(NC8, 128).T)
    ip["alog"] = np.ascontiguousarray(
        np.asarray(inputs["L0_attn_A_log"])[d0:d0 + DH].reshape(NC8, 128, N16)
        .transpose(1, 0, 2).reshape(128, NC8 * N16))
    Dv = np.asarray(inputs["L0_attn_D"])[d0:d0 + DH]
    ddiag = np.zeros((NC8, 128, 128), np.float32)
    for c in range(NC8):
        ddiag[c, idx, idx] = Dv[c * 128:(c + 1) * 128]
    ip["d_diag"] = ddiag
    ip["outw"] = np.ascontiguousarray(np.asarray(inputs["L0_attn_out_proj_w"])[d0:d0 + DH])
    ip["ln_g"] = np.ascontiguousarray(np.asarray(inputs["L0_attn_ln_g"]).reshape(NC8, 128).T)
    ip["ln_b"] = np.ascontiguousarray(np.asarray(inputs["L0_attn_ln_b"]).reshape(NC8, 128).T)
    ip["wgT"] = np.ascontiguousarray(np.asarray(inputs["L0_ffn_w_gate"])[h0:h0 + HH].T)
    ip["wuT"] = np.ascontiguousarray(np.asarray(inputs["L0_ffn_w_up"])[h0:h0 + HH].T)
    ip["wdT"] = np.ascontiguousarray(np.asarray(inputs["L0_ffn_w_down"])[:, h0:h0 + HH].T)
    ip["eye"] = np.eye(128, dtype=np.float32)
    return ip


def build_nc(debug=False, serial=False, serial_phases=()):
    nc = bass.Bass("TRN2", target_bir_lowering=False)
    A = lambda name, shape, dt: nc.dram_tensor(name, shape, dt, kind="ExternalInput").ap()
    x_seq = A("x_seq", [L, DM], F32)
    w_in = A("w_in", [DM, 2 * DH], F32)
    conv_diag = A("conv_diag", [4, NC8, 128, 128], F32)
    conv_b = A("conv_b", [128, NC8], F32)
    xprojw = A("xprojw", [DH, 96], F32)
    dtw_i = A("dtw", [64, DH], F32)
    dtb_i = A("dtb", [128, NC8], F32)
    alog = A("alog", [128, NC8 * N16], F32)
    d_diag = A("d_diag", [NC8, 128, 128], F32)
    outw = A("outw", [DH, DM], F32)
    ln_g = A("ln_g", [128, NC8], F32)
    ln_b = A("ln_b", [128, NC8], F32)
    wgT = A("wgT", [DM, HH], I32)
    wuT = A("wuT", [DM, HH], I32)
    wdT = A("wdT", [HH, DM], I32)
    eye_i = A("eye", [128, 128], F32)
    y_out = nc.dram_tensor("y_out", [L, DM], F32, kind="ExternalOutput").ap()

    Dr = lambda name, shape, dt: nc.dram_tensor(name, shape, dt).ap()
    wg_bf = Dr("wg_bf", [DM, 2 * DH], BF16)
    outw_bf = Dr("outw_bf", [DH, DM], BF16)
    ffng_bf = Dr("ffng_bf", [DM, HH], BF16)
    ffnu_bf = Dr("ffnu_bf", [DM, HH], BF16)
    ffnd_bf = Dr("ffnd_bf", [HH, DM], BF16)
    y2_sp = Dr("y2_sp", [NC8, 128, MACRO], BF16)
    sz_sp = Dr("sz_sp", [NC8, 128, MACRO], BF16)
    x2_sp = Dr("x2_sp", [MACRO, DM], F16)
    bwrow = Dr("bwrow", [2048], F32)
    cc_dbc_in = Dr("cc_dbc_in", [96, MACRO], F32)
    cc_dbc_out = Dr("cc_dbc_out", [96, MACRO], F32)
    bcrows = Dr("bcrows", [32, MACRO], F16)
    cc_o_in = Dr("cc_o_in", [MACRO, DM], BF16)
    cc_o_out = Dr("cc_o_out", [MACRO, DM], BF16)
    cc_f_in = Dr("cc_f_in", [MACRO, DM], BF16)
    cc_f_out = Dr("cc_f_out", [MACRO, DM], BF16)

    dbg = {}
    if debug:
        for nm, shape in [("dbg_xn", [128, DM]), ("dbg_u2", [128, MACRO]),
                          ("dbg_delta", [128, MACRO]), ("dbg_dbc", [96, MACRO]),
                          ("dbg_y2", [128, MACRO]), ("dbg_o", [MACRO, DM]),
                          ("dbg_x2", [128, DM]), ("dbg_upre", [128, MACRO]),
                          ("dbg_xnt", [128, MACRO]), ("dbg_bwt", [128, 16]), ("dbg_stats", [128, 40]), ("dbg_xln1", [128, DM]), ("dbg_nm", [128, 16]), ("dbg_can", [128, 40]), ("dbg_can2", [128, 40])]:
            dbg[nm] = nc.dram_tensor(nm, shape, F32, kind="ExternalOutput").ap()

    S = lambda name, shape, dt: nc.alloc_sbuf_tensor(name, shape, dt).ap()
    negm_all = S("negm_all", [128, NMAC * NSUB], F32)
    r1_all = S("r1_all", [128, NMAC * NSUB], F32)
    s_all = S("s_all", [128, NMAC * NSUB], F32)
    convd_sb = S("convd_sb", [128, 4 * NC8 * 128], BF16)
    ddiag_sb = S("ddiag_sb", [128, NC8 * 128], BF16)
    xprojw_sb = S("xprojw_sb", [128, NC8 * 96], BF16)
    dtw_sb = S("dtw_sb", [64, DH], BF16)
    a_sb = S("a_sb", [128, NC8 * N16], F32)
    eye_sb = S("eye_sb", [128, 128], BF16)
    bwt_sb = S("bwt_sb", [128, 16], F32)
    convb_sb = S("convb_sb", [128, NC8], F32)
    dtb_sb = S("dtb_sb", [128, NC8], F32)
    gcol_sb = S("gcol_sb", [128, NC8], F32)
    bcol_sb = S("bcol_sb", [128, NC8], BF16)
    wgoc_sb = S("wgoc_sb", [128, 2 * NC8 * 128], BF16)
    outw_sb = S("outw_sb", [128, NC8 * DM], BF16)
    ffg_sb = S("ffg_sb", [128, 2 * NC8 * 128], BF16)
    ffu_sb = S("ffu_sb", [128, 2 * NC8 * 128], BF16)
    xnt_sb = S("xnt_sb", [128, NC8 * MACRO], BF16)
    delta_sb = S("delta_sb", [128, NC8 * MACRO], F16)
    u2_sb = S("u2_sb", [128, NC8 * MACRO], BF16)
    upre_sb = S("upre_sb", [128, 2 * (MACRO + 4)], BF16)
    uhalo_sb = S("uhalo_sb", [128, 3 * NC8], BF16)
    wq_sb = S("wq_sb", [128, NC8 * MACRO], F16)
    ysb_sb = S("ysb_sb", [128, NC8 * MACRO], BF16)
    dA_sb = S("dA_sb", [128, 2 * MACRO], F32)
    dBu_sb = S("dBu_sb", [128, MACRO], F16)
    h8_sb = S("h8_sb", [128, 4 * MACRO], BF16)
    g_sb = S("g_sb", [128, 2 * MACRO], BF16)
    bbc_sb = S("bbc_sb", [128, 4 * MACRO], F16)
    cbc_sb = S("cbc_sb", [128, 4 * MACRO], F16)
    hst_sb = S("hst_sb", [128, NC8 * N16], F32)
    dt_sb = S("dt_sb", [64, MACRO], BF16)
    x2_sb = S("x2_sb", [128, 2 * DM], F16)
    stage_bf = S("stage_bf", [128, 2 * DM], BF16)
    stage_f32 = S("stage_f32", [128, 2 * DM], F32)
    fs = [S(f"fscr{i}", [128, DM], F32) for i in range(4)]
    y2r_sb = S("y2r_sb", [128, 2 * NC8 * 128], BF16)
    hact_sb = u2_sb[:, 0:HK * 512]
    wd_sb = wq_sb.bitcast(BF16)[:, 0:HK * 512]
    x2r_sb = x2_sb
    cst_sb = S("cst_sb", [128, 2], F32)
    st_acc = S("st_acc", [128, 4 * NSUB], F32)
    st_tmp = S("st_tmp", [128, 12 * NSUB], F32)

    P = lambda name, shape, dt: nc.alloc_psum_tensor(name, shape, dt).ap()
    psA = P("psA", [128, 1024], F32)
    psB = P("psB", [128, 1024], F32)
    psC = P("psC", [128, 1024], F32)
    psT = P("psT", [128, 1024], BF16)

    sch = Sched(nc)
    sch.SERIAL = serial
    O = sch.op

    def mark(ph):
        sch.MARK = ph in serial_phases or ph == "prep"
    war = {}

    def guard(name):
        return war.get(name)

    def setg(name, ev):
        war[name] = ev

    mark("prep")
    pf = u2_sb.bitcast(F32)[:, 0:3072]
    pb = u2_sb[:, 6144:8192]

    e1 = O("sync", lambda e: e.dma_start(out=fs[0][:, 0:128], in_=eye_i), dma=True)
    ev_eye = O("vector", lambda e: e.tensor_copy(out=eye_sb, in_=fs[0][:, 0:128]), after=[e1])
    for src, dst in ((conv_b, convb_sb), (dtb_i, dtb_sb), (ln_g, gcol_sb)):
        O("sync", lambda e, s=src, d=dst: e.dma_start(out=d, in_=s), dma=True)
    ev_cols = O("sync", lambda e: e.dma_start(out=fs[1][:, 0:NC8], in_=ln_b), dma=True)
    ev_bcol = O("vector", lambda e: e.tensor_copy(out=bcol_sb, in_=fs[1][:, 0:NC8]), after=[ev_cols])
    e1 = O("sync", lambda e: e.dma_start(out=a_sb, in_=alog), dma=True)
    e2 = O("scalar", lambda e: e.activation(out=a_sb, in_=a_sb, func=AF.Exp), after=[e1])
    ev_a = O("vector", lambda e: e.tensor_single_scalar(out=a_sb, in_=a_sb, scalar=-1.0, op=ALU.mult), after=[e2])
    ev_ffc = []
    for src, dst in ((wgT, ffng_bf), (wuT, ffnu_bf), (wdT, ffnd_bf)):
        ev_ffc.append(O("gpsimd", lambda e, s=src, d=dst: e.dma_start(out=d, in_=s), dma=True))
    ev_convd = None
    for jh in range(2):
        e1 = O("sync", lambda e, jh=jh: e.dma_start(
            out=pf[:, 0:2 * NC8 * 128].rearrange("p (j c n) -> p j c n", j=2, c=NC8),
            in_=conv_diag[2 * jh:2 * jh + 2].rearrange("j c p n -> p j c n")),
            after=[ev_convd], dma=True)
        ev_convd = O("vector", lambda e, jh=jh: e.tensor_copy(
            out=convd_sb[:, jh * 2 * NC8 * 128:(jh + 1) * 2 * NC8 * 128],
            in_=pf[:, 0:2 * NC8 * 128]), after=[e1])
    e1 = O("sync", lambda e: e.dma_start(
        out=pf[:, 0:NC8 * 128].rearrange("p (c n) -> p c n", c=NC8),
        in_=d_diag.rearrange("c p n -> p c n")), after=[ev_convd], dma=True)
    ev_dd = O("vector", lambda e: e.tensor_copy(out=ddiag_sb, in_=pf[:, 0:NC8 * 128]), after=[e1])
    e1 = O("sync", lambda e: e.dma_start(
        out=pf[:, 0:NC8 * 96].rearrange("p (c n) -> p c n", c=NC8),
        in_=xprojw.rearrange("(c p) n -> p c n", p=128)), after=[ev_dd], dma=True)
    ev_xpw = O("vector", lambda e: e.tensor_copy(out=xprojw_sb, in_=pf[:, 0:NC8 * 96]), after=[e1])
    e1 = O("sync", lambda e: e.dma_start(out=pf[0:64, 0:DH], in_=dtw_i), after=[ev_xpw], dma=True)
    ev_dtw = O("vector", lambda e: e.tensor_copy(out=dtw_sb, in_=pf[0:64, 0:DH]), after=[e1])

    ev_bw_mm = None
    ev_wg_st = {}
    ev_eg_prev = None
    for k in range(NC8):
        el = O("sync", lambda e, k=k: e.dma_start(out=pf[:, 0:2 * DH], in_=w_in[k * 128:(k + 1) * 128, :]),
               after=[ev_dtw, ev_eg_prev, ev_bw_mm], dma=True)
        er = O("vector", lambda e: e.tensor_copy(out=pb, in_=pf[:, 0:2 * DH]), after=[el])
        eg = O("vector", lambda e, k=k: e.tensor_scalar(
            out=stage_bf[:, 0:2 * DH], in0=pf[:, 0:2 * DH],
            scalar1=gcol_sb[:, k:k + 1], scalar2=None, op0=ALU.mult),
            after=[el, ev_wg_st.get(k - 1)])
        ev_eg_prev = eg
        ev_wg_st[k] = O("sync", lambda e, k=k: e.dma_start(
            out=wg_bf[k * 128:(k + 1) * 128, :], in_=stage_bf[:, 0:2 * DH]), after=[eg], dma=True)
        for fc in range(4):
            pdst = psA if fc < 2 else psC
            ev_bw_mm = O("tensor", lambda e, k=k, fc=fc, pdst=pdst: e.matmul(
                pdst[0:1, (fc % 2) * 512:(fc % 2 + 1) * 512], bcol_sb[:, k:k + 1], pb[:, fc * 512:(fc + 1) * 512],
                start=(k == 0), stop=(k == NC8 - 1)), after=[er, ev_bcol])
    e_bev = O("scalar", lambda e: e.copy(out=stage_f32[0:1, 0:1024], in_=psA[0:1, 0:1024]), after=[ev_bw_mm])
    e_bev = O("scalar", lambda e: e.copy(out=stage_f32[0:1, 1024:2048], in_=psC[0:1, 0:1024]), after=[ev_bw_mm, e_bev])
    setg("psA", e_bev)
    setg("psC", e_bev)
    e_bst = O("sync", lambda e: e.dma_start(out=bwrow.rearrange("(a b) -> a b", a=2)[0:1], in_=stage_f32[0:1, 0:1024]), after=[e_bev], dma=True)
    e_bst = O("sync", lambda e: e.dma_start(out=bwrow.rearrange("(a b) -> a b", a=2)[1:2], in_=stage_f32[0:1, 1024:2048]), after=[e_bev, e_bst], dma=True)
    ev_bwt = e_bst
    for oc in range(16):
        ev_bwt = O("sync", lambda e, oc=oc: e.dma_start(
            out=bwt_sb[:, oc:oc + 1], in_=bwrow.rearrange("(oc j) -> oc j", oc=16)[oc:oc + 1, :].rearrange("a j -> (a j) a" if False else "a j -> j a")),
            after=[e_bst], dma=True)

    prev_st = ev_wg_st[NC8 - 1]
    for k in range(NC8):
        el = O("sync", lambda e, k=k: e.dma_start(out=pf[:, 0:DM], in_=outw[k * 128:(k + 1) * 128, :]),
               after=[prev_st, ev_bw_mm], dma=True)
        ec = O("vector", lambda e: e.tensor_copy(out=stage_bf[:, 0:DM], in_=pf[:, 0:DM]), after=[el])
        prev_st = O("sync", lambda e, k=k: e.dma_start(
            out=outw_bf[k * 128:(k + 1) * 128, :], in_=stage_bf[:, 0:DM]), after=[ec], dma=True)
    ev_ce = O("vector", lambda e: e.memset(cst_sb[:, 0:1], EPS))
    ev_c1 = O("vector", lambda e: e.memset(cst_sb[:, 1:2], 1.0))
    setg("fs0", ev_eye)
    setg("fs1", ev_bcol)
    setg("stagebf0", prev_st)
    setg("stagebf1", prev_st)
    setg("stagef0", ev_bwt)
    setg("stagef1", ev_bwt)
    ev_z0 = O("vector", lambda e: e.memset(hst_sb, 0.0))
    ev_z1 = O("vector", lambda e: e.memset(uhalo_sb, 0.0))
    prep_done = prev_st
    setg("u2", prev_st)

    ev_scan_done = None
    ev_sz = {}
    ev_y2wr = {}
    ev_y2rd = None
    ev_hsave = {}
    last_store = None
    ev_xnt_free = None
    ev_stacc_free = None
    ev_stacc2_free = None

    for m in range(NMAC):
        T0 = m * MACRO
        gbase = m * NSUB

        mark("p1")
        ev_acc = []
        for i in range(NSUB):
            xs = i % 2
            e_l = O("sync", lambda e, i=i, xs=xs, T0=T0: e.dma_start(
                out=fs[xs], in_=x_seq[T0 + i * SUB:T0 + (i + 1) * SUB, :]),
                after=[guard(f"fs{xs}"), prep_done if m == 0 else None], dma=True, stream="xload")
            e0 = O("scalar", lambda e, i=i, xs=xs: e.activation(
                out=fs[2], in_=fs[xs], func=AF.Copy, accum_out=st_acc[:, 4 * i:4 * i + 1]),
                after=[e_l, ev_stacc_free, guard("fs2")])
            e1 = O("scalar", lambda e, i=i, xs=xs: e.activation(
                out=fs[2], in_=fs[xs], func=AF.Square, accum_out=st_acc[:, 4 * i + 1:4 * i + 2]),
                after=[e0])
            ev_acc.append(e1)
            setg(f"fs{xs}", e1)
            setg("fs2", e1)
        sa = st_acc.rearrange("p (i k) -> p k i", k=4)
        nm_c = negm_all[:, gbase:gbase + NSUB]
        r1_c = r1_all[:, gbase:gbase + NSUB]
        s_c = s_all[:, gbase:gbase + NSUB]
        t0 = st_tmp[:, 0:NSUB]
        t1 = st_tmp[:, NSUB:2 * NSUB]
        t2 = st_tmp[:, 2 * NSUB:3 * NSUB]
        e1 = O("vector", lambda e: e.tensor_single_scalar(out=nm_c, in_=sa[:, 0], scalar=-1.0 / DM, op=ALU.mult), after=[ev_acc[-1]])
        tmq = st_tmp[:, 7 * NSUB:8 * NSUB]
        e2 = O("vector", lambda e: e.tensor_mul(out=tmq, in0=nm_c, in1=nm_c), after=[e1, guard("st7")])
        setg("st7", e3 if False else None)
        e3 = O("vector", lambda e: e.scalar_tensor_tensor(out=t0, in0=sa[:, 1], scalar=1.0 / DM, in1=tmq, op0=ALU.mult, op1=ALU.subtract), after=[e2])
        e4 = O("scalar", lambda e: e.activation(out=t1, in_=t0, func=AF.Ln, bias=cst_sb[:, 0:1]), after=[e3])
        e5 = O("scalar", lambda e: e.activation(out=t1, in_=t1, func=AF.Exp, scale=-0.5), after=[e4])
        e6 = O("vector", lambda e: e.tensor_mul(out=t2, in0=t1, in1=t1), after=[e5])
        e7 = O("vector", lambda e: e.tensor_mul(out=t2, in0=t2, in1=t0), after=[e6])
        e8 = O("scalar", lambda e: e.activation(out=t2, in_=t2, func=AF.Ln, bias=cst_sb[:, 0:1]), after=[e7])
        e9 = O("scalar", lambda e: e.activation(out=t2, in_=t2, func=AF.Exp, scale=-0.5), after=[e8])
        e10 = O("vector", lambda e: e.tensor_copy(out=r1_c, in_=t1), after=[e5])
        ev_stats = O("vector", lambda e: e.tensor_mul(out=s_c, in0=t1, in1=t2), after=[e9, e10])
        ev_stacc_free = ev_stats
        if debug and m == 0:
            O("gpsimd", lambda e: e.dma_start(out=dbg["dbg_can2"][:, 0:NSUB], in_=negm_all[:, 0:NSUB]),
              after=[ev_stats], dma=True)

        xnt3 = xnt_sb.rearrange("p (k t) -> p k t", k=NC8)
        psT3 = psT.rearrange("p (k t) -> p k t", k=NC8)
        ev_tr = []
        for i in range(NSUB):
            xs = i % 2
            sl = i % 2
            e_xl2 = O("sync", lambda e, i=i, xs=xs, T0=T0: e.dma_start(
                out=fs[xs], in_=x_seq[T0 + i * SUB:T0 + (i + 1) * SUB, :]),
                after=[guard(f"fs{xs}")], dma=True, stream="xload")
            e_xn = O("vector", lambda e, i=i, xs=xs, sl=sl: e.tensor_scalar(
                out=stage_bf[:, sl * DM:(sl + 1) * DM], in0=fs[xs],
                scalar1=nm_c[:, i:i + 1], scalar2=s_c[:, i:i + 1], op0=ALU.add, op1=ALU.mult),
                after=[e_xl2, ev_stats, guard(f"stagebf{sl}")])
            setg(f"fs{xs}", e_xn)
            tp = None
            for k in range(NC8):
                tp = O("tensor", lambda e, k=k, sl=sl: e.transpose(
                    psT3[:, k], stage_bf[:, sl * DM + k * 128:sl * DM + (k + 1) * 128], eye_sb),
                    after=[e_xn, ev_eye, guard("psT")])
            e_ev = O("scalar", lambda e, i=i: e.copy(out=xnt3[:, :, i * SUB:(i + 1) * SUB], in_=psT3),
                     after=[tp, ev_xnt_free if i == 0 else None])
            setg("psT", e_ev)
            setg(f"stagebf{sl}", e_ev)
            ev_tr.append(e_ev)
        ev_xnt = ev_tr[-1]
        if debug and m == 0:
            O("gpsimd", lambda e: e.dma_start(out=dbg["dbg_can2"][:, NSUB:2 * NSUB], in_=negm_all[:, 0:NSUB]),
              after=[ev_xnt], dma=True)
            e_st8 = O("vector", lambda e: e.tensor_copy(out=fs[2][:, 0:3 * NSUB], in_=st_tmp[:, 0:3 * NSUB]), after=[ev_stats, guard("fs2")])
            e_st9 = O("vector", lambda e: e.tensor_copy(out=fs[2][:, 3 * NSUB:3 * NSUB + NSUB], in_=nm_c), after=[ev_stats, e_st8])
            e_st10 = O("vector", lambda e: e.tensor_copy(out=fs[2][:, 4 * NSUB:4 * NSUB + NSUB], in_=s_c), after=[ev_stats, e_st9])
            e_stD = O("gpsimd", lambda e: e.dma_start(out=dbg["dbg_stats"], in_=fs[2][:, 0:5 * NSUB]), after=[e_st10], dma=True)
            setg("fs2", e_stD)
            e_xd = O("vector", lambda e: e.tensor_copy(out=fs[3], in_=stage_bf[:, 0:DM]), after=[ev_tr[0], guard("fs3")])
            e_xdD = O("gpsimd", lambda e: e.dma_start(out=dbg["dbg_xn"], in_=fs[3]), after=[e_xd], dma=True)
            setg("fs3", e_xdD)

        mark("p2")
        wg3 = wgoc_sb.rearrange("p (s k n) -> p s k n", s=2, k=NC8)
        ev_u = {}
        ev_z = {}
        ev_u2 = {}
        for oc in range(16):
            sl = oc % 2
            ps = psA if sl == 0 else psC
            pname = "psA" if sl == 0 else "psC"
            e_w = O("sync", lambda e, oc=oc, sl=sl: e.dma_start(
                out=wg3[:, sl], in_=wg_bf[:, oc * 128:(oc + 1) * 128].rearrange("(k p) n -> p k n", p=128)),
                after=[guard(f"wgoc{sl}"), prep_done if m == 0 else None], dma=True, stream="wg")
            mm = None
            for k in range(NC8):
                for hf in range(2):
                    mm = O("tensor", lambda e, k=k, hf=hf, ps=ps, sl=sl: e.matmul(
                        ps[:, hf * 512:(hf + 1) * 512], wg3[:, sl, k],
                        xnt3[:, k, hf * 512:(hf + 1) * 512],
                        start=(k == 0), stop=(k == NC8 - 1)),
                        after=[e_w, ev_xnt, guard(pname)])
            setg(f"wgoc{sl}", mm)
            if oc < NC8:
                c = oc
                us = c % 2
                upv = upre_sb.rearrange("p (s t) -> p s t", s=2)[:, us]
                e_h = O("vector", lambda e, c=c, upv=upv: e.tensor_copy(
                    out=upv[:, 0:3], in_=uhalo_sb[:, c * 3:c * 3 + 3]),
                    after=[guard(f"upre{us}"), guard(f"uhalo{c}"), ev_z1 if m == 0 else None])
                e_u = O("scalar", lambda e, c=c, ps=ps, upv=upv: e.activation(
                    out=upv[:, 3:3 + MACRO], in_=ps, func=AF.Identity, bias=bwt_sb[:, c:c + 1]),
                    after=[mm, e_h, ev_bwt])
                e_h2 = O("vector", lambda e, c=c, upv=upv: e.tensor_copy(
                    out=uhalo_sb[:, c * 3:c * 3 + 3], in_=upv[:, MACRO:MACRO + 3]), after=[e_u])
                ev_u[c] = (e_u, e_h2)
                setg(f"uhalo{c}", e_h2)
                setg(pname, e_u)
                cmm = None
                for j in range(4):
                    for hf in range(2):
                        cmm = O("tensor", lambda e, j=j, c=c, hf=hf, ps=ps, upv=upv: e.matmul(
                            ps[:, hf * 512:(hf + 1) * 512],
                            convd_sb[:, (j * NC8 + c) * 128:(j * NC8 + c + 1) * 128],
                            upv[:, j + hf * 512:j + hf * 512 + 512],
                            start=(j == 0), stop=(j == 3)),
                            after=[e_u, ev_convd, guard(pname)])
                e_cs = O("scalar", lambda e, c=c, ps=ps: e.activation(
                    out=u2_sb[:, c * MACRO:(c + 1) * MACRO], in_=ps, func=AF.Silu,
                    bias=convb_sb[:, c:c + 1]), after=[cmm, guard("u2")])
                ev_u2[c] = e_cs
                setg(pname, e_cs)
                setg(f"upre{us}", cmm)
                if debug and m == 0 and c == 0:
                    O("gpsimd", lambda e: e.dma_start(out=dbg["dbg_can2"][:, 2 * NSUB:3 * NSUB], in_=negm_all[:, 0:NSUB]),
                      after=[ev_u[0][1]], dma=True)
                    e_upd = O("vector", lambda e, upv=upv: e.tensor_copy(out=fs[3], in_=upv[:, 3:3 + MACRO]),
                              after=[e_u, guard("fs3")])
                    e_upD = O("gpsimd", lambda e: e.dma_start(out=dbg["dbg_upre"], in_=fs[3][:, 0:MACRO]), after=[e_upd], dma=True)
                    setg("fs3", e_upD)
            else:
                c = oc - NC8
                zsl = oc % 2
                e_zv = O("scalar", lambda e, c=c, ps=ps, zsl=zsl: e.activation(
                    out=stage_bf[:, zsl * DM:zsl * DM + MACRO], in_=ps, func=AF.Silu,
                    bias=bwt_sb[:, NC8 + c:NC8 + c + 1]),
                    after=[mm, guard(f"stagebf{zsl}"), ev_bwt])
                e_zs = O("scalar", lambda e, c=c, zsl=zsl: e.dma_start(
                    out=sz_sp[c], in_=stage_bf[:, zsl * DM:zsl * DM + MACRO]),
                    after=[e_zv, ev_sz.get((m - 1, c))], dma=True, stream="spill")
                ev_z[c] = e_zs
                setg(f"stagebf{zsl}", e_zs)
                setg(pname, e_zv)

        if debug and m == 0:
            O("gpsimd", lambda e: e.dma_start(out=dbg["dbg_can"][:, 0:NSUB], in_=negm_all[:, 0:NSUB]),
              after=[ev_u2[NC8 - 1]], dma=True)
        if debug and m == 0:
            e_c = O("vector", lambda e: e.tensor_copy(out=fs[2], in_=u2_sb[:, 0:MACRO]), after=[ev_u2[0], guard("fs2")])
            e_cD = O("gpsimd", lambda e: e.dma_start(out=dbg["dbg_u2"], in_=fs[2][:, 0:MACRO]), after=[e_c], dma=True)
            setg("fs2", e_cD)
        mark("p4")
        mm = None
        for k in range(NC8):
            for hf in range(2):
                mm = O("tensor", lambda e, k=k, hf=hf: e.matmul(
                    psB[0:96, hf * 512:hf * 512 + 512],
                    xprojw_sb.rearrange("p (c n) -> p c n", c=NC8)[:, k],
                    u2_sb[:, k * MACRO + hf * 512:k * MACRO + (hf + 1) * 512],
                    start=(k == 0), stop=(k == NC8 - 1)),
                    after=[ev_u2[k], ev_xpw, guard("psB")])
        e_ev = O("scalar", lambda e: e.copy(out=stage_f32[0:96, 0:MACRO], in_=psB[0:96, 0:MACRO]),
                 after=[mm, guard("stagef0")])
        setg("psB", e_ev)
        e_st = O("gpsimd", lambda e: e.dma_start(out=cc_dbc_in, in_=stage_f32[0:96, 0:MACRO]),
                 after=[e_ev, guard("cc_dbc_in"), ev_ffc[-1] if m == 0 else None], dma=True, stream="gst96")
        setg("stagef0", e_st)
        e_cc = O("gpsimd", lambda e: e.collective_compute(
            "AllReduce", ALU.add, replica_groups=RG,
            ins=[cc_dbc_in.opt()], outs=[cc_dbc_out.opt()]),
            after=[e_st, guard("cc_dbc_out")], coll=True, stream="coll")
        setg("cc_dbc_in", e_cc)
        e_dt = O("gpsimd", lambda e: e.dma_start(out=dt_sb, in_=cc_dbc_out[0:64, :]),
                 after=[e_cc, guard("dt_sb")], dma=True, stream="dtld")
        e_bcr = O("gpsimd", lambda e: e.dma_start(out=bcrows, in_=cc_dbc_out[64:96, :]),
                  after=[e_cc, guard("bcrows")], dma=True, stream="bcr")
        setg("cc_dbc_out", e_bcr)
        if debug and m == 0:
            O("gpsimd", lambda e: e.dma_start(out=dbg["dbg_dbc"], in_=cc_dbc_out), after=[e_cc], dma=True)

        if debug and m == 0:
            O("gpsimd", lambda e: e.dma_start(out=dbg["dbg_can"][:, NSUB:2 * NSUB], in_=negm_all[:, 0:NSUB]),
              after=[e_bcr], dma=True)
        mark("p5")
        ev_dl = {}
        for c in range(NC8):
            sl = c % 2
            ps = psA if sl == 0 else psC
            pname = "psA" if sl == 0 else "psC"
            mm = None
            for hf in range(2):
                mm = O("tensor", lambda e, c=c, hf=hf, ps=ps: e.matmul(
                    ps[:, hf * 512:(hf + 1) * 512], dtw_sb[:, c * 128:(c + 1) * 128],
                    dt_sb[:, hf * 512:(hf + 1) * 512], start=True, stop=True),
                    after=[e_dt, ev_dtw, guard(pname)])
            ea = O("scalar", lambda e, c=c, ps=ps: e.activation(
                out=fs[2], in_=ps, func=AF.Abs, bias=dtb_sb[:, c:c + 1]), after=[mm, guard("fs2")])
            eb = O("scalar", lambda e: e.activation(out=fs[2], in_=fs[2], func=AF.Exp, scale=-1.0), after=[ea])
            ec = O("scalar", lambda e: e.activation(out=fs[2], in_=fs[2], func=AF.Ln, bias=cst_sb[:, 1:2]), after=[eb])
            ed = O("scalar", lambda e, c=c, ps=ps: e.activation(
                out=fs[3], in_=ps, func=AF.Relu, bias=dtb_sb[:, c:c + 1]), after=[ec, guard("fs3")])
            ev = O("vector", lambda e, c=c: e.tensor_add(
                out=delta_sb[:, c * MACRO:(c + 1) * MACRO], in0=fs[2], in1=fs[3]),
                after=[ed, guard("delta")])
            ev_dl[c] = ev
            setg("fs2", ev)
            setg("fs3", ev)
            setg(pname, ed)
        if debug and m == 0:
            e_c = O("vector", lambda e: e.tensor_copy(out=fs[2], in_=delta_sb[:, 0:MACRO]), after=[ev_dl[0], guard("fs2")])
            e_cD2 = O("gpsimd", lambda e: e.dma_start(out=dbg["dbg_delta"], in_=fs[2][:, 0:MACRO]), after=[e_c], dma=True)
            setg("fs2", e_cD2)

        mark("scan")
        ev_w = {}
        for c in range(NC8):
            ev_w[c] = O("vector", lambda e, c=c: e.tensor_mul(
                out=wq_sb[:, c * MACRO:(c + 1) * MACRO],
                in0=delta_sb[:, c * MACRO:(c + 1) * MACRO],
                in1=u2_sb[:, c * MACRO:(c + 1) * MACRO]),
                after=[ev_dl[c], ev_u2[c], guard("wq")])
        prev_tts = None
        jidx = 0
        ev_yq = {}
        ev_qevac = {}
        for nq in range(4):
            ev_bcq = {}
            for j2 in range(4):
                nn = nq * 4 + j2
                eb_ = O("sync", lambda e, nn=nn, j2=j2: e.dma_start(
                    out=bbc_sb[:, j2 * MACRO:(j2 + 1) * MACRO],
                    in_=bcrows[nn:nn + 1, :].broadcast_to([128, MACRO])),
                    after=[e_bcr, guard(f"bbc{j2}")], dma=True, stream="bc")
                ec_ = O("sync", lambda e, nn=nn, j2=j2: e.dma_start(
                    out=cbc_sb[:, j2 * MACRO:(j2 + 1) * MACRO],
                    in_=bcrows[16 + nn:17 + nn, :].broadcast_to([128, MACRO])),
                    after=[e_bcr, guard(f"cbc{j2}")], dma=True, stream="bc")
                ev_bcq[j2] = (eb_, ec_)
            for c in range(NC8):
                ya = None
                for jn in range(4):
                    n = nq * 4 + jn
                    da = jidx % 2
                    jidx += 1
                    e_exp = O("scalar", lambda e, c=c, n=n, da=da: e.activation(
                        out=dA_sb[:, da * MACRO:(da + 1) * MACRO],
                        in_=delta_sb[:, c * MACRO:(c + 1) * MACRO], func=AF.Exp,
                        scale=a_sb[:, c * N16 + n:c * N16 + n + 1]),
                        after=[ev_dl[c], ev_a, guard(f"dA{da}")])
                    e_dbu = O("vector", lambda e, c=c, jn=jn: e.tensor_mul(
                        out=dBu_sb, in0=wq_sb[:, c * MACRO:(c + 1) * MACRO],
                        in1=bbc_sb[:, jn * MACRO:(jn + 1) * MACRO]),
                        after=[ev_w[c], ev_bcq[jn][0], guard("dBu")])
                    e_tts = O("vector", lambda e, c=c, n=n, jn=jn, da=da: e.tensor_tensor_scan(
                        out=h8_sb[:, jn * MACRO:(jn + 1) * MACRO],
                        data0=dA_sb[:, da * MACRO:(da + 1) * MACRO], data1=dBu_sb,
                        initial=hst_sb[:, c * N16 + n:c * N16 + n + 1], op0=ALU.mult, op1=ALU.add),
                        after=[e_exp, e_dbu, guard(f"h8_{jn}"),
                               ev_z0 if m == 0 else None,
                               ev_hsave.get((c, nq))])
                    setg(f"dA{da}", e_tts)
                    setg("dBu", e_tts)
                    prev_tts = e_tts
                    gsl = jn % 2
                    e_ym = O("vector", lambda e, jn=jn, gsl=gsl: e.tensor_tensor(
                        out=g_sb[:, gsl * MACRO:(gsl + 1) * MACRO],
                        in0=h8_sb[:, jn * MACRO:(jn + 1) * MACRO],
                        in1=cbc_sb[:, jn * MACRO:(jn + 1) * MACRO], op=ALU.mult),
                        after=[e_tts, ev_bcq[jn][1], guard(f"g{gsl}")])
                    setg(f"h8_{jn}", e_ym)
                    for sj in range(2):
                        ya = O("tensor", lambda e, sj=sj, gsl=gsl, jn=jn: e.matmul(
                            psB[:, sj * 512:(sj + 1) * 512], eye_sb,
                            g_sb[:, gsl * MACRO + sj * 512:gsl * MACRO + (sj + 1) * 512],
                            start=(jn == 0), stop=(jn == 3)),
                            after=[e_ym] + ([guard("psB")] if jn == 0 else []))
                    setg(f"g{gsl}", ya)
                    if c == NC8 - 1:
                        setg(f"bbc{jn}", e_dbu)
                        setg(f"cbc{jn}", e_ym)
                eh = O("vector", lambda e, c=c, nq=nq: e.tensor_copy(
                    out=hst_sb[:, c * N16 + nq * 4:c * N16 + nq * 4 + 4],
                    in_=h8_sb.rearrange("p (s t) -> p s t", s=4)[:, :, MACRO - 1]),
                    after=[prev_tts])
                ev_hsave[(c, nq)] = eh
                yv = ysb_sb[:, c * MACRO:(c + 1) * MACRO]
                if nq == 0:
                    eq = O("vector", lambda e, yv=yv: e.tensor_copy(out=yv, in_=psB[:, 0:MACRO]),
                           after=[ya, guard(f"ysb{c}")])
                else:
                    eq = O("vector", lambda e, yv=yv: e.tensor_add(out=yv, in0=yv, in1=psB[:, 0:MACRO]),
                           after=[ya, ev_qevac.get((nq - 1, c))])
                ev_qevac[(nq, c)] = eq
                setg("psB", eq)
        ev_yfin_c = {}
        for c in range(NC8):
            ya = None
            for sj in range(2):
                ya = O("tensor", lambda e, c=c, sj=sj: e.matmul(
                    psB[:, sj * 512:(sj + 1) * 512], ddiag_sb[:, c * 128:(c + 1) * 128],
                    u2_sb[:, c * MACRO + sj * 512:c * MACRO + (sj + 1) * 512],
                    start=True, stop=True),
                    after=[ev_u2[c], ev_dd, guard("psB")])
            e_ud = O("vector", lambda e, c=c: e.tensor_add(
                out=ysb_sb[:, c * MACRO:(c + 1) * MACRO],
                in0=ysb_sb[:, c * MACRO:(c + 1) * MACRO], in1=psB[:, 0:MACRO]),
                after=[ya, ev_qevac[(3, c)]])
            setg("psB", e_ud)
            e_szl = O("sync", lambda e, c=c: e.dma_start(
                out=stage_bf[:, 0:MACRO], in_=sz_sp[c]),
                after=[ev_z[c], guard("stagebf0")], dma=True, stream="szl")
            e_fin = O("vector", lambda e, c=c: e.tensor_tensor(
                out=stage_bf[:, DM:DM + MACRO], in0=ysb_sb[:, c * MACRO:(c + 1) * MACRO],
                in1=stage_bf[:, 0:MACRO], op=ALU.mult),
                after=[e_ud, e_szl, guard("stagebf1")])
            setg(f"ysb{c}", e_fin)
            setg("stagebf0", e_fin)
            e_y2s = O("scalar", lambda e, c=c: e.dma_start(out=y2_sp[c], in_=stage_bf[:, DM:DM + MACRO]),
                      after=[e_fin, ev_y2rd], dma=True, stream="spill")
            ev_y2wr[c] = e_y2s
            ev_sz[(m, c)] = e_szl
            setg("stagebf1", e_y2s)
            ev_yfin_c[c] = e_y2s
            if debug and m == 0 and c == 0:
                e_cp = O("vector", lambda e: e.tensor_copy(out=fs[2], in_=stage_bf[:, DM:DM + MACRO]),
                         after=[e_fin, guard("fs2")])
                e_cpD = O("gpsimd", lambda e: e.dma_start(out=dbg["dbg_y2"], in_=fs[2][:, 0:MACRO]), after=[e_cp], dma=True)
                setg("fs2", e_cpD)
        ev_scan_done = ev_yfin_c[NC8 - 1]
        setg("wq", ev_scan_done)

        if debug and m == 0:
            O("gpsimd", lambda e: e.dma_start(out=dbg["dbg_can"][:, 2 * NSUB:3 * NSUB], in_=negm_all[:, 0:NSUB]),
              after=[ev_scan_done], dma=True)
        mark("p7")
        ev_ow = {}
        for c in range(NC8):
            ev_ow[c] = O("sync", lambda e, c=c: e.dma_start(
                out=outw_sb[:, c * DM:(c + 1) * DM], in_=outw_bf[c * 128:(c + 1) * 128, :]),
                after=[guard("outw"), prep_done if m == 0 else None], dma=True, stream="ow")
        ev_opst = {}
        mm = None
        for ts8 in range(NSUB):
            sl = ts8 % 2
            ps = psA if sl == 0 else psC
            pname = "psA" if sl == 0 else "psC"
            ysl = ts8 % 2
            e_yl = O("sync", lambda e, ts8=ts8, ysl=ysl: e.dma_start(
                out=y2r_sb.rearrange("p (s c n) -> p s c n", s=2, c=NC8)[:, ysl],
                in_=y2_sp[:, :, ts8 * SUB:(ts8 + 1) * SUB].rearrange("c p n -> p c n")),
                after=[ev_scan_done, guard(f"y2r{ysl}")], dma=True, stream="y2l")
            for c in range(NC8):
                for hf in range(2):
                    mm = O("tensor", lambda e, c=c, hf=hf, ps=ps, ysl=ysl: e.matmul(
                        ps[:, hf * 512:(hf + 1) * 512],
                        y2r_sb.rearrange("p (s c n) -> p s c n", s=2, c=NC8)[:, ysl, c],
                        outw_sb[:, c * DM + hf * 512:c * DM + (hf + 1) * 512],
                        start=(c == 0), stop=(c == NC8 - 1)),
                        after=[e_yl, ev_ow[c], guard(pname)])
            setg(f"y2r{ysl}", mm)
            sfl = ts8 % 2
            stbo = stage_f32.bitcast(BF16)
            e_ev = O("scalar", lambda e, ps=ps, sfl=sfl, stbo=stbo: e.copy(
                out=stbo[:, sfl * 2 * DM:sfl * 2 * DM + DM], in_=ps), after=[mm, guard(f"stagef{sfl}")])
            setg(pname, e_ev)
            e_os = O("gpsimd", lambda e, ts8=ts8, sfl=sfl, stbo=stbo: e.dma_start(
                out=cc_o_in[ts8 * SUB:(ts8 + 1) * SUB, :], in_=stbo[:, sfl * 2 * DM:sfl * 2 * DM + DM]),
                after=[e_ev, guard("cc_o_in")], dma=True, stream="gst")
            ev_opst[ts8] = e_os
            setg(f"stagef{sfl}", e_os)
        setg("outw", mm)
        ev_y2rd = e_yl
        e_cco = O("gpsimd", lambda e: e.collective_compute(
            "AllReduce", ALU.add, replica_groups=RG,
            ins=[cc_o_in.opt()], outs=[cc_o_out.opt()]),
            after=[ev_opst[NSUB - 1], guard("cc_o_out")], coll=True, stream="coll")
        setg("cc_o_in", e_cco)
        if debug and m == 0:
            O("gpsimd", lambda e: e.dma_start(out=dbg["dbg_o"], in_=cc_o_out), after=[e_cco], dma=True)

        if debug and m == 0:
            O("gpsimd", lambda e: e.dma_start(out=dbg["dbg_can"][:, 3 * NSUB:4 * NSUB], in_=negm_all[:, 0:NSUB]),
              after=[e_cco], dma=True)
        mark("p8")
        ev_x2s = {}
        for i in range(NSUB):
            gi = gbase + i
            xs = i % 2
            e_xl = O("sync", lambda e, i=i, xs=xs, T0=T0: e.dma_start(
                out=fs[xs], in_=x_seq[T0 + i * SUB:T0 + (i + 1) * SUB, :]),
                after=[guard(f"fs{xs}")], dma=True, stream="xload")
            e_ol = O("gpsimd", lambda e, i=i: e.dma_start(
                out=fs[2], in_=cc_o_out[i * SUB:(i + 1) * SUB, :]),
                after=[e_cco, guard("fs2")], dma=True, stream="obl")
            p_a0 = O("scalar", lambda e, i=i, xs=xs: e.activation(
                out=fs[3], in_=fs[xs], func=AF.Copy, accum_out=st_acc[:, 4 * i:4 * i + 1]),
                after=[e_xl, guard("fs3"), guard("st_acc")])
            p_a1 = O("scalar", lambda e, i=i, xs=xs: e.activation(
                out=fs[3], in_=fs[xs], func=AF.Square, accum_out=st_acc[:, 4 * i + 1:4 * i + 2]), after=[p_a0])
            pm = st_tmp[:, 8 * NSUB + i:8 * NSUB + i + 1]
            pv = st_tmp[:, 9 * NSUB + i:9 * NSUB + i + 1]
            pq = st_tmp[:, 10 * NSUB + i:10 * NSUB + i + 1]
            p_m = O("vector", lambda e, i=i, pm=pm: e.tensor_single_scalar(
                out=pm, in_=st_acc[:, 4 * i:4 * i + 1], scalar=-1.0 / DM, op=ALU.mult), after=[p_a1])
            p_v1 = O("vector", lambda e, pm=pm, pq=pq: e.tensor_mul(out=pq, in0=pm, in1=pm), after=[p_m])
            p_v2 = O("vector", lambda e, i=i, pv=pv, pq=pq: e.scalar_tensor_tensor(
                out=pv, in0=st_acc[:, 4 * i + 1:4 * i + 2], scalar=1.0 / DM, in1=pq,
                op0=ALU.mult, op1=ALU.subtract), after=[p_v1])
            p_v3 = O("scalar", lambda e, pv=pv: e.activation(out=pv, in_=pv, func=AF.Ln, bias=cst_sb[:, 0:1]), after=[p_v2])
            p_v4 = O("scalar", lambda e, pv=pv: e.activation(out=pv, in_=pv, func=AF.Exp, scale=-0.5), after=[p_v3])
            p_nr = O("vector", lambda e, pm=pm, pv=pv, pq=pq: e.tensor_mul(out=pq, in0=pm, in1=pv), after=[p_v4])
            e_o1 = O("scalar", lambda e, xs=xs, pv=pv, pq=pq: e.activation(
                out=fs[3], in_=fs[xs], func=AF.Identity, scale=pv, bias=pq), after=[p_nr, guard("fs3")])
            setg("st_acc", p_a1)
            if debug and m == 0 and i == 0:
                e_xlnD = O("gpsimd", lambda e: e.dma_start(out=dbg["dbg_xln1"], in_=fs[3]), after=[e_o1], dma=True)
                e_nmD = O("gpsimd", lambda e: e.dma_start(out=dbg["dbg_nm"][:, 0:NSUB], in_=negm_all[:, 0:NSUB]), after=[e_o1], dma=True)
                e_r1D = O("gpsimd", lambda e: e.dma_start(out=dbg["dbg_nm"][:, NSUB:2 * NSUB], in_=r1_all[:, 0:NSUB]), after=[e_o1], dma=True)
            e_o2 = O("vector", lambda e: e.tensor_add(out=fs[2], in0=fs[2], in1=fs[3]),
                     after=[e_ol, e_o1] + ([e_xlnD] if debug and m == 0 and i == 0 else []))
            setg("fs3", e_o2)
            e_a0 = O("scalar", lambda e, i=i: e.activation(
                out=fs[3], in_=fs[2], func=AF.Copy, accum_out=st_acc[:, 4 * i + 2:4 * i + 3]),
                after=[e_o2, ev_stacc2_free, guard("fs3")])
            e_a1 = O("scalar", lambda e, i=i: e.activation(
                out=fs[3], in_=fs[2], func=AF.Square, accum_out=st_acc[:, 4 * i + 3:4 * i + 4]), after=[e_a0])
            setg("fs3", e_a1)
            tm = st_tmp[:, 3 * NSUB + i:3 * NSUB + i + 1]
            tv = st_tmp[:, 4 * NSUB + i:4 * NSUB + i + 1]
            e_m = O("vector", lambda e, i=i, tm=tm: e.tensor_single_scalar(
                out=tm, in_=st_acc[:, 4 * i + 2:4 * i + 3], scalar=-1.0 / DM, op=ALU.mult), after=[e_a1])
            tq = st_tmp[:, 7 * NSUB + i:7 * NSUB + i + 1]
            e_v1 = O("vector", lambda e, tm=tm, tq=tq: e.tensor_mul(out=tq, in0=tm, in1=tm), after=[e_m])
            e_v2 = O("vector", lambda e, i=i, tv=tv, tq=tq: e.scalar_tensor_tensor(
                out=tv, in0=st_acc[:, 4 * i + 3:4 * i + 4], scalar=1.0 / DM, in1=tq,
                op0=ALU.mult, op1=ALU.subtract), after=[e_v1])
            e_v3 = O("scalar", lambda e, tv=tv: e.activation(out=tv, in_=tv, func=AF.Ln, bias=cst_sb[:, 0:1]), after=[e_v2])
            e_v4 = O("scalar", lambda e, tv=tv: e.activation(out=tv, in_=tv, func=AF.Exp, scale=-0.5), after=[e_v3])
            x2s = i % 2
            nr2 = st_tmp[:, 7 * NSUB + i:7 * NSUB + i + 1]
            e_nr2 = O("vector", lambda e, tm=tm, tv=tv, nr2=nr2: e.tensor_mul(out=nr2, in0=tm, in1=tv),
                      after=[e_v4, guard("st7")])
            e_x1 = O("scalar", lambda e, tv=tv, nr2=nr2: e.activation(
                out=fs[2], in_=fs[2], func=AF.Identity, scale=tv, bias=nr2), after=[e_nr2])
            setg("st7", e_x1)
            e_x2 = O("vector", lambda e, xs=xs, x2s=x2s: e.tensor_add(
                out=x2_sb[:, x2s * DM:(x2s + 1) * DM], in0=fs[2], in1=fs[xs]),
                after=[e_x1, guard(f"x2_{x2s}")])
            setg("fs2", e_x2)
            setg(f"fs{xs}", e_x2)
            e_x2sp = O("scalar", lambda e, i=i, x2s=x2s: e.dma_start(
                out=x2_sp[i * SUB:(i + 1) * SUB, :], in_=x2_sb[:, x2s * DM:(x2s + 1) * DM]),
                after=[e_x2, guard("x2_sp")], dma=True, stream="spill")
            ev_x2s[i] = (e_x2, e_x2sp, x2s)
            if debug and m == 0 and i == 0:
                e_cp = O("vector", lambda e: e.tensor_copy(out=fs[3], in_=x2_sb[:, 0:DM]), after=[e_x2, guard("fs3")])
                e_cpD2 = O("gpsimd", lambda e: e.dma_start(out=dbg["dbg_x2"], in_=fs[3]), after=[e_cp], dma=True)
                setg("fs3", e_cpD2)
            x2v = x2_sb[:, x2s * DM:(x2s + 1) * DM]
            f_a0 = O("scalar", lambda e, i=i, x2v=x2v: e.activation(
                out=fs[3], in_=x2v, func=AF.Copy, accum_out=st_acc[:, 4 * i:4 * i + 1]),
                after=[e_x2, ev_stacc_free, guard("fs3")])
            f_a1 = O("scalar", lambda e, i=i, x2v=x2v: e.activation(
                out=fs[3], in_=x2v, func=AF.Square, accum_out=st_acc[:, 4 * i + 1:4 * i + 2]), after=[f_a0])
            setg("fs3", f_a1)
            tm3 = st_tmp[:, 5 * NSUB + i:5 * NSUB + i + 1]
            tv3 = st_tmp[:, 6 * NSUB + i:6 * NSUB + i + 1]
            f_m = O("vector", lambda e, i=i, tm3=tm3: e.tensor_single_scalar(
                out=tm3, in_=st_acc[:, 4 * i:4 * i + 1], scalar=-1.0 / DM, op=ALU.mult), after=[f_a1])
            tq3 = st_tmp[:, 7 * NSUB + i:7 * NSUB + i + 1]
            f_v1 = O("vector", lambda e, tm3=tm3, tq3=tq3: e.tensor_mul(out=tq3, in0=tm3, in1=tm3), after=[f_m])
            f_v2 = O("vector", lambda e, i=i, tv3=tv3, tq3=tq3: e.scalar_tensor_tensor(
                out=tv3, in0=st_acc[:, 4 * i + 1:4 * i + 2], scalar=1.0 / DM, in1=tq3,
                op0=ALU.mult, op1=ALU.subtract), after=[f_v1])
            f_v3 = O("scalar", lambda e, tv3=tv3: e.activation(out=tv3, in_=tv3, func=AF.Ln, bias=cst_sb[:, 0:1]), after=[f_v2])
            f_v4 = O("scalar", lambda e, tv3=tv3: e.activation(out=tv3, in_=tv3, func=AF.Exp, scale=-0.5), after=[f_v3])
            sl = i % 2
            f_xn = O("vector", lambda e, tm3=tm3, tv3=tv3, sl=sl, x2v=x2v: e.tensor_scalar(
                out=stage_bf[:, sl * DM:(sl + 1) * DM], in0=x2v, scalar1=tm3, scalar2=tv3,
                op0=ALU.add, op1=ALU.mult), after=[f_v4, guard(f"stagebf{sl}")])
            setg(f"x2_{x2s}", f_xn)
            tp = None
            for k in range(NC8):
                tp = O("tensor", lambda e, k=k, sl=sl: e.transpose(
                    psT3[:, k], stage_bf[:, sl * DM + k * 128:sl * DM + (k + 1) * 128], eye_sb),
                    after=[f_xn, guard("psT")])
            f_ev = O("scalar", lambda e, i=i: e.copy(out=xnt3[:, :, i * SUB:(i + 1) * SUB], in_=psT3),
                     after=[tp])
            setg("psT", f_ev)
            setg(f"stagebf{sl}", f_ev)
            ev_x2s[i] = (e_x2, e_x2sp, x2s)
            if i == NSUB - 1:
                ev_xnt3_done = f_ev
        setg("x2_sp", ev_x2s[NSUB - 1][1])
        ev_stacc_free = ev_xnt3_done

        ffg3 = ffg_sb.rearrange("p (s k n) -> p s k n", s=2, k=NC8)
        mark("p9")
        ffu3 = ffu_sb.rearrange("p (s k n) -> p s k n", s=2, k=NC8)
        ev_dn = {}
        mm_last_xnt = None
        for half in range(2):
            mark("p9a")
            hof = half * 512
            ev_hact = {}
            for hc in range(HK):
                sl = hc % 2
                e_wg = O("sync", lambda e, hc=hc, sl=sl: e.dma_start(
                    out=ffg3[:, sl], in_=ffng_bf[:, hc * 128:(hc + 1) * 128].rearrange("(k p) n -> p k n", p=128)),
                    after=[guard(f"ffg{sl}"), ev_ffc[0] if m == 0 else None], dma=True, stream="ffgu")
                e_wu = O("sync", lambda e, hc=hc, sl=sl: e.dma_start(
                    out=ffu3[:, sl], in_=ffnu_bf[:, hc * 128:(hc + 1) * 128].rearrange("(k p) n -> p k n", p=128)),
                    after=[guard(f"ffu{sl}"), ev_ffc[1] if m == 0 else None], dma=True, stream="ffgu")
                mmg = mmu = None
                for k in range(NC8):
                    mmg = O("tensor", lambda e, k=k, sl=sl, hof=hof: e.matmul(
                        psA[:, 0:512], ffg3[:, sl, k],
                        xnt3[:, k, hof:hof + 512], start=(k == 0), stop=(k == NC8 - 1)),
                        after=[e_wg, ev_xnt3_done, guard("psA")])
                for k in range(NC8):
                    mmu = O("tensor", lambda e, k=k, sl=sl, hof=hof: e.matmul(
                        psC[:, 0:512], ffu3[:, sl, k],
                        xnt3[:, k, hof:hof + 512], start=(k == 0), stop=(k == NC8 - 1)),
                        after=[e_wu, ev_xnt3_done, guard("psC")])
                setg(f"ffg{sl}", mmg)
                setg(f"ffu{sl}", mmu)
                mm_last_xnt = mmu
                e_sg = O("scalar", lambda e: e.activation(
                    out=stage_bf[:, 0:512], in_=psA[:, 0:512], func=AF.Silu, scale=1.0 / 127.0),
                    after=[mmg, guard("stagebf0")])
                setg("psA", e_sg)
                e_su = O("scalar", lambda e: e.activation(
                    out=stage_bf[:, DM:DM + 512], in_=psC[:, 0:512], func=AF.Copy, scale=1.0 / 127.0),
                    after=[mmu, guard("stagebf1")])
                setg("psC", e_su)
                e_h = O("vector", lambda e, hc=hc: e.tensor_mul(
                    out=hact_sb[:, hc * 512:(hc + 1) * 512],
                    in0=stage_bf[:, 0:512], in1=stage_bf[:, DM:DM + 512]),
                    after=[e_sg, e_su, guard("hact")])
                ev_hact[hc] = e_h
                setg("stagebf0", e_h)
                setg("stagebf1", e_h)
            mark("p9b")
            for dmh in range(2):
                ev_wd = {}
                for hk in range(HK):
                    ev_wd[hk] = O("sync", lambda e, hk=hk, dmh=dmh: e.dma_start(
                        out=wd_sb[:, hk * 512:(hk + 1) * 512],
                        in_=ffnd_bf[hk * 128:(hk + 1) * 128, dmh * 512:(dmh + 1) * 512]),
                        after=[guard("wq"), ev_ffc[2] if m == 0 else None,
                               guard(f"wd{hk}")], dma=True, stream="wd")
                for t4 in range(4):
                    ts8 = half * 4 + t4
                    sl = t4 % 2
                    ps = psA if sl == 0 else psC
                    pname = "psA" if sl == 0 else "psC"
                    mm = None
                    for hk in range(HK):
                        mm = O("tensor", lambda e, hk=hk, t4=t4, ps=ps: e.matmul(
                            ps[:, 0:512],
                            hact_sb[:, hk * 512 + t4 * SUB:hk * 512 + (t4 + 1) * SUB],
                            wd_sb[:, hk * 512:(hk + 1) * 512],
                            start=(hk == 0), stop=(hk == HK - 1)),
                            after=[ev_hact[hk], ev_wd[hk], guard(pname)])
                        if t4 == 3:
                            setg(f"wd{hk}", mm)
                    sfl = t4 % 2
                    stbf = stage_f32.bitcast(BF16)
                    e_ev = O("scalar", lambda e, ps=ps, sfl=sfl, stbf=stbf: e.activation(
                        out=stbf[:, sfl * 2 * DM:sfl * 2 * DM + 512], in_=ps[:, 0:512],
                        func=AF.Copy, scale=1.0 / 127.0),
                        after=[mm, guard(f"stagef{sfl}")])
                    setg(pname, e_ev)
                    e_fsd = O("gpsimd", lambda e, ts8=ts8, dmh=dmh, sfl=sfl, stbf=stbf: e.dma_start(
                        out=cc_f_in[ts8 * SUB:(ts8 + 1) * SUB, dmh * 512:(dmh + 1) * 512],
                        in_=stbf[:, sfl * 2 * DM:sfl * 2 * DM + 512]),
                        after=[e_ev, guard("cc_f_in")], dma=True, stream="gst")
                    ev_dn[(half, dmh, t4)] = e_fsd
                    setg(f"stagef{sfl}", e_fsd)
            setg("hact", ev_dn[(half, 1, 3)])
        ev_xnt_free = mm_last_xnt
        setg("wq", ev_dn[(1, 1, 3)])
        setg("u2", ev_dn[(1, 1, 3)])
        e_ccf = O("gpsimd", lambda e: e.collective_compute(
            "AllReduce", ALU.add, replica_groups=RG,
            ins=[cc_f_in.opt()], outs=[cc_f_out.opt()]),
            after=[ev_dn[(1, 1, 3)], guard("cc_f_out")], coll=True, stream="coll")
        setg("cc_f_in", e_ccf)

        mark("p10")
        for i in range(NSUB):
            e_ol = O("gpsimd", lambda e, i=i: e.dma_start(
                out=fs[2], in_=cc_f_out[i * SUB:(i + 1) * SUB, :]),
                after=[e_ccf, guard("fs2")], dma=True, stream="o2l")
            x2rs = i % 2
            e_xr = O("sync", lambda e, i=i, x2rs=x2rs: e.dma_start(
                out=x2r_sb[:, x2rs * DM:(x2rs + 1) * DM], in_=x2_sp[i * SUB:(i + 1) * SUB, :]),
                after=[ev_x2s[i][1], guard(f"x2r{x2rs}"), guard(f"x2_{x2rs}")], dma=True, stream="x2l")
            e_a0 = O("scalar", lambda e, i=i: e.activation(
                out=fs[3], in_=fs[2], func=AF.Copy, accum_out=st_acc[:, 4 * i + 2:4 * i + 3]),
                after=[e_ol, ev_stacc2_free, guard("fs3")])
            e_a1 = O("scalar", lambda e, i=i: e.activation(
                out=fs[3], in_=fs[2], func=AF.Square, accum_out=st_acc[:, 4 * i + 3:4 * i + 4]), after=[e_a0])
            setg("fs3", e_a1)
            tm = st_tmp[:, 3 * NSUB + i:3 * NSUB + i + 1]
            tv = st_tmp[:, 4 * NSUB + i:4 * NSUB + i + 1]
            e_m = O("vector", lambda e, i=i, tm=tm: e.tensor_single_scalar(
                out=tm, in_=st_acc[:, 4 * i + 2:4 * i + 3], scalar=-1.0 / DM, op=ALU.mult), after=[e_a1])
            tq = st_tmp[:, 7 * NSUB + i:7 * NSUB + i + 1]
            e_v1 = O("vector", lambda e, tm=tm, tq=tq: e.tensor_mul(out=tq, in0=tm, in1=tm), after=[e_m])
            e_v2 = O("vector", lambda e, i=i, tv=tv, tq=tq: e.scalar_tensor_tensor(
                out=tv, in0=st_acc[:, 4 * i + 3:4 * i + 4], scalar=1.0 / DM, in1=tq,
                op0=ALU.mult, op1=ALU.subtract), after=[e_v1])
            e_v3 = O("scalar", lambda e, tv=tv: e.activation(out=tv, in_=tv, func=AF.Ln, bias=cst_sb[:, 0:1]), after=[e_v2])
            e_v4 = O("scalar", lambda e, tv=tv: e.activation(out=tv, in_=tv, func=AF.Exp, scale=-0.5), after=[e_v3])
            nr3 = st_tmp[:, 7 * NSUB + i:7 * NSUB + i + 1]
            e_nr3 = O("vector", lambda e, tm=tm, tv=tv, nr3=nr3: e.tensor_mul(out=nr3, in0=tm, in1=tv),
                      after=[e_v4, guard("st7")])
            e_x1 = O("scalar", lambda e, tv=tv, nr3=nr3: e.activation(
                out=fs[2], in_=fs[2], func=AF.Identity, scale=tv, bias=nr3), after=[e_nr3])
            setg("st7", e_x1)
            e_x2f = O("vector", lambda e, x2rs=x2rs: e.tensor_add(
                out=fs[2], in0=fs[2], in1=x2r_sb[:, x2rs * DM:(x2rs + 1) * DM]), after=[e_x1, e_xr])
            setg(f"x2r{x2rs}", e_x2f)
            setg(f"x2_{x2rs}", e_x2f)
            last_store = O("scalar", lambda e, i=i, T0=T0: e.dma_start(
                out=y_out[T0 + i * SUB:T0 + (i + 1) * SUB, :], in_=fs[2]),
                after=[e_x2f, last_store], dma=True, stream="yout")
            setg("fs2", last_store)
            ev_stacc2_free = e_x2f
        setg("x2_sp", e_xr)
        setg("cc_f_out", e_ol)
        setg("cc_o_out", ev_x2s[NSUB - 1][0])

    sch.wait_only("scalar", [last_store])
    sch.wait_only("gpsimd", [last_store])

    import contextlib
    stack = contextlib.ExitStack()
    sems = {name: stack.enter_context(nc.semaphore(name)) for name in sch.cnt}
    with nc.Block() as block:
        sch.emit(block, sems)
    stack.close()
    n_ops = sum(len(v) for v in sch.ops.values())
    return nc, n_ops


_cache = {}


def kernel(**inputs):
    import os
    if "nc" not in _cache:
        _cache["nc"], _cache["nops"] = build_nc(
            debug=False, serial=os.environ.get("KSERIAL", "0") == "1",
            serial_phases=tuple(x for x in os.environ.get("KSPH", "p9b").split(",") if x))
    nc = _cache["nc"]
    in_maps = [host_prep(inputs, core) for core in range(8)]
    res = run_bass_kernel_spmd(nc, in_maps, core_ids=list(range(8)))
    out = np.empty((4, L, DM), np.float32)
    for p in range(4):
        out[p, :L // 2] = res.results[2 * p]["y_out"][:L // 2]
        out[p, L // 2:] = res.results[2 * p + 1]["y_out"][L // 2:]
    return out
